# version 18
# baseline (speedup 1.0000x reference)
import numpy as np
import ml_dtypes
import concourse.bass as bass
import concourse.mybir as mybir
from concourse.bass_utils import run_bass_kernel_spmd

F32 = mybir.dt.float32
BF16 = mybir.dt.bfloat16
AF = mybir.ActivationFunctionType
ALU = mybir.AluOpType
AX = mybir.AxisListType

T = 4096
D = 1024
H = 8
C = 64
TILE = 512
NT = T // TILE
CHT = TILE // C
INNER = 4096
MAGIC = 12582912.0


class Sem:
    def __init__(self, h):
        self.h = h
        self.n = 0


class Prog:
    ENGINES = ("sync", "tensor", "vector", "scalar", "gpsimd")

    def __init__(self):
        self.ops = {e: [] for e in self.ENGINES}
        self.self_sem = {}

    def emit(self, eng, fn, waits=(), inc=None, inc_k=1):
        waits = list(waits)
        ss = self.self_sem.get(eng)
        if ss is not None and ss.n > 0 and inc is ss:
            waits.append((ss, ss.n))
        wl = [(s.h, v) for (s, v) in waits]
        tgt = None
        if inc is not None:
            inc.n += inc_k
            tgt = (inc.h, inc_k)
        self.ops[eng].append([fn, wl, tgt])

    def inc_last(self, eng, sem, k=1):
        rec = self.ops[eng][-1]
        assert isinstance(rec, list) and rec[2] is None
        sem.n += k
        rec[2] = (sem.h, k)
        return sem.n

    def wait(self, eng, sem, val=None):
        v = sem.n if val is None else val
        h = sem.h
        self.ops[eng].append(lambda e: e.wait_ge(h, v))

    def replay(self, eng, e):
        for rec in self.ops[eng]:
            if isinstance(rec, list):
                fn, wl, tgt = rec
                for h, v in wl:
                    e.wait_ge(h, v)
                r = fn(e)
                if tgt is not None:
                    r.then_inc(tgt[0], tgt[1])
            else:
                rec(e)


def build_nc():
    nc = bass.Bass("TRN2", target_bir_lowering=False)

    def din(name, shape, dt=F32):
        return nc.dram_tensor(name, shape, dt, kind="ExternalInput").ap()

    xT = din("xT", [128, 8 * 4096])
    wwT = din("wwT", [128, 8 * D])
    wrT = din("wrT", [128, 8 * D])
    wbT = din("wbT", [D, H])
    dec = din("dec", [1, H])
    c_row = din("c_row", [1, D])
    wmod = din("wmod", [D, 6 * D])
    wgT = din("wgT", [D, INNER])
    wuT = din("wuT", [D, INNER])
    wdT = din("wdT", [INNER, D])
    iota_rep = din("iota_rep", [1, 512])
    ones_row_f = din("ones_row_f", [1, 128])
    ones_col_b = din("ones_col_b", [128, 1], BF16)
    ident128 = din("ident128", [128, 128], BF16)
    identcol = din("identcol", [128, 64], BF16)
    ipacked = din("ipacked", [128, 512], BF16)
    trilS = din("trilS", [128, 512], BF16)
    triuS = din("triuS", [128, 512], BF16)
    triuI = din("triuI", [128, 512], BF16)
    e16rep = din("e16rep", [16, 16], BF16)
    e8rep = din("e8rep", [8, 8], BF16)

    outT = nc.dram_tensor("outT", [128, 8 * 4096], F32, kind="ExternalOutput").ap()
    sc_g = nc.dram_tensor("sc_g", [D, INNER], BF16, kind="Internal").ap()
    sc_u = nc.dram_tensor("sc_u", [D, INNER], BF16, kind="Internal").ap()
    sc_d = nc.dram_tensor("sc_d", [D, INNER], BF16, kind="Internal").ap()

    import contextlib
    stack = contextlib.ExitStack()

    def sb(name, shape, dt=BF16):
        return stack.enter_context(nc.sbuf_tensor(name, shape, dt))

    def ps(name, shape, dt=F32):
        return stack.enter_context(nc.psum_tensor(name, shape, dt))

    xt = [sb("xt0", [128, 8 * TILE], F32), sb("xt1", [128, 8 * TILE], F32)]
    h_fm = sb("h_fm", [128, 8 * TILE])
    kdw_tm = sb("kdw_tm", [128, 4 * D])
    vtm = sb("vtm", [128, 4 * D])
    o_fm = sb("o_fm", [128, H * TILE])
    m_fm = sb("m_fm", [128, 32 * TILE])
    kpair_t = sb("kpair", [128, H * CHT * 128])
    kpair = kpair_t[:]
    kp32 = kpair_t[:].bitcast(F32)
    xq = sb("xq", [128, 8 * TILE])
    xq32 = xq[:].bitcast(F32)
    Xs = sb("Xs", [128, 512])
    Ys = sb("Ys", [128, 512])
    Ps = sb("Ps", [128, 512])
    at_all = sb("at_all", [128, 4 * 512])
    att_all = sb("att_all", [128, 4 * 512])
    w0tm = sb("w0tm", [128, H * 128])
    vnewtm = sb("vnewtm", [128, H * 128])
    kde_s = sb("kde_s", [128, 2 * H * C])
    kbdn_s = sb("kbdn_s", [128, 2 * H * C])
    Sb = sb("Sb", [128, 2 * H * 128])
    S32 = sb("S32", [128, H * 128], F32)
    wsh = sb("wsh", [128, 8 * D])
    wbT_sb = sb("wbT_sb", [128, 8 * H])
    wbuf = [sb("wbuf0", [128, 8 * 512]), sb("wbuf1", [128, 8 * 512])]
    gu = sb("gu", [128, 8 * 512])
    maskA = sb("maskA", [128, 512])
    maskAT = sb("maskAT", [128, 512])
    maskLTI = sb("maskLTI", [128, 512])
    de_pat = sb("de_pat", [128, H * C])
    ipk = sb("ipk", [128, 512])
    idc = sb("idc", [128, 64])
    id128 = sb("id128", [128, 128])
    idf32 = sb("idf32", [128, 128], F32)
    ocol = sb("ocol", [128, 1])
    orow_f = sb("orow_f", [1, 128], F32)
    e16 = sb("e16", [16, 16])
    e8 = sb("e8", [8, 8])
    modcols = sb("modcols", [128, 64], F32)
    gacol = sb("gacol", [128, 8], F32)
    gbcol = sb("gbcol", [128, 8], F32)
    capcol = sb("capcol", [128, 8], F32)
    cbpcol = sb("cbpcol", [128, 8], F32)
    sccol_b = sb("sccol_b", [128, 8])
    rows = xq[:].bitcast(F32)[0:1, 0:1024]
    ctdcol = sb("ctdcol", [128, H], F32)
    dwcol = sb("dwcol", [128, H], F32)
    betacol = sb("betacol", [128, 32], F32)
    f1col = sb("f1col", [128, 32], F32)
    f2col = sb("f2col", [128, 32], F32)
    kdwfcol = sb("kdwfcol", [128, 32], F32)
    fac2 = sb("fac2", [128, 64], F32)
    Mfac = sb("Mfac", [16, 512])
    Mq = sb("Mq", [8, 128])
    tq = sb("tq", [128, 512])
    tq2 = sb("tq2", [128, 512])
    tq3 = sb("tq3", [128, 512])
    sbc = sb("sbc", [128, 512])
    sibc = sb("sibc", [128, 512])
    amaxc = sb("amaxc", [128, 8], F32)
    aucol = sb("aucol", [128, 1], F32)
    adcol = sb("adcol", [128, 1], F32)
    agcol = sb("agcol", [128, 1], F32)
    thrg = sb("thrg", [128, 1], F32)
    thru = sb("thru", [128, 1], F32)
    thrd = sb("thrd", [128, 1], F32)
    acc_abs = sb("acc_abs", [128, 4], F32)
    stat = sb("stat", [1, 1536], F32)
    epsc = sb("epsc", [1, 1], F32)
    onef = sb("onef", [128, 1], F32)
    magc = sb("magc", [128, 1], F32)
    nmagc = sb("nmagc", [128, 1], F32)
    negt = sb("negt", [128, 1], F32)
    cumrow = stat[0:1, 0:512]
    derow = stat[0:1, 512:1024]
    dwrow = stat[0:1, 1024:1536]

    mf32 = m_fm[:].bitcast(F32)

    pA = ps("pA", [128, 512])
    pB = ps("pB", [128, 1024])
    pC = ps("pC", [128, 512])
    pD = ps("pD", [128, 512])
    pE = ps("pE", [128, 512])
    pF = ps("pF", [128, 512])
    psT = ps("psT", [128, 1024], BF16)
    psT32 = psT[:].bitcast(F32)

    dx = Sem(stack.enter_context(nc.semaphore("dx")))
    dxt = Sem(stack.enter_context(nc.semaphore("dxt")))
    dwmP = [Sem(stack.enter_context(nc.semaphore("dwm0"))),
            Sem(stack.enter_context(nc.semaphore("dwm1")))]
    dstP = [Sem(stack.enter_context(nc.semaphore("dst0"))),
            Sem(stack.enter_context(nc.semaphore("dst1")))]
    pe = Sem(stack.enter_context(nc.semaphore("pe")))
    ve = Sem(stack.enter_context(nc.semaphore("ve")))
    ac = Sem(stack.enter_context(nc.semaphore("ac")))
    po = Sem(stack.enter_context(nc.semaphore("po")))
    so = Sem(stack.enter_context(nc.semaphore("so")))

    P = Prog()
    P.self_sem = {"vector": ve, "scalar": ac, "gpsimd": po}

    def mm(out, lhsT, rhs, start=True, stop=True, tp=None, waits=(), inc=None):
        P.emit("tensor", lambda t: t.matmul(out, lhsT, rhs, start=start, stop=stop,
                                            tile_position=tp), waits=waits, inc=inc)

    def penop(waits=()):
        assert not waits
        return P.inc_last("tensor", pe)

    def v_op(fn, waits=()):
        P.emit("vector", fn, waits=waits, inc=ve)
        return ve.n

    def a_op(fn, waits=()):
        P.emit("scalar", fn, waits=waits, inc=ac)
        return ac.n

    def g_op(fn, waits=()):
        P.emit("gpsimd", fn, waits=waits, inc=po)
        return po.n

    def s_dma(out, in_, waits=(), inc=None, selfwait=False):
        inc = inc or dx
        waits = list(waits)
        if selfwait and inc.n > 0:
            waits.append((inc, inc.n))
        P.emit("sync", lambda s: s.dma_start(out=out, in_=in_), waits=waits, inc=inc, inc_k=16)
        return (inc, inc.n)

    def g_dma(out, in_, waits=(), par=0, inc=None):
        inc = inc if inc is not None else dwmP[par]
        waits = list(waits)
        if inc.n > 0:
            waits.append((inc, inc.n))
        P.emit("gpsimd", lambda g: g.dma_start(out=out, in_=in_), waits=waits, inc=inc, inc_k=16)
        return (inc, inc.n)

    bank_free = {}
    _bkey = {}

    def bank_key(bank):
        return _bkey.get(id(bank), id(bank))

    def bank_wait(bank):
        k = bank_key(bank)
        return [bank_free[k]] if k in bank_free else []

    s_dma(ocol[:], ones_col_b[:])
    s_dma(orow_f[:], ones_row_f[:])
    s_dma(id128[:], ident128[:])
    s_dma(idc[:], identcol[:])
    s_dma(ipk[:], ipacked[:])
    s_dma(e16[:], e16rep[:])
    s_dma(e8[:], e8rep[:])
    s_dma(rows[0:1, 0:512], iota_rep[:])
    s_dma(rows[0:1, 512:520], dec[:])
    s_dma(maskA[:], trilS[:])
    s_dma(maskAT[:], triuS[:])
    s_dma(maskLTI[:], triuI[:])

    s_dma(xt[0][:], xT[:, 0:4096], inc=dxt, selfwait=True)
    dma0 = (dx, dx.n)
    dwA = g_dma(wbT_sb[:].rearrange("p (c h) -> p c h", c=8),
                wbT.rearrange("(c p) h -> p c h", p=128), par=0)

    v_op(lambda v: v.tensor_copy(idf32[:], id128[:]), waits=[dma0])
    v_op(lambda v: v.memset(epsc[:], 1e-5))
    v_op(lambda v: v.memset(onef[:], 1.0))
    v_op(lambda v: v.memset(magc[:], MAGIC))
    v_op(lambda v: v.memset(nmagc[:], -MAGIC))
    a1 = a_op(lambda s: s.activation(rows[0:1, 520:528], rows[0:1, 512:520], AF.Sigmoid),
              waits=[dma0])
    a1 = a_op(lambda s: s.activation(rows[0:1, 528:536], rows[0:1, 520:528], AF.Ln))
    v1 = v_op(lambda v: v.tensor_copy(
        cumrow.rearrange("r (h q) -> r h q", h=8),
        rows[0:1, 528:536].unsqueeze(2).broadcast_to([1, 8, 64])),
        waits=[(ac, a1)])
    v1 = v_op(lambda v: v.tensor_mul(cumrow, cumrow, rows[0:1, 0:512]))
    aexp = a_op(lambda s: s.activation(derow, cumrow, AF.Exp), waits=[(ve, v1)])
    v1 = v_op(lambda v: v.tensor_copy(
        rows[0:1, 0:512].rearrange("r (h q) -> r h q", h=8),
        cumrow[0:1, 63::64].unsqueeze(2).broadcast_to([1, 8, 64])))
    v1 = v_op(lambda v: v.tensor_sub(rows[0:1, 512:1024], rows[0:1, 0:512], cumrow))
    adw = a_op(lambda s: s.activation(dwrow, rows[0:1, 512:1024], AF.Exp), waits=[(ve, v1)])
    vneg = v_op(lambda v: v.tensor_scalar_mul(rows[0:1, 0:512], cumrow, -1.0))
    vneg = max(vneg, v1)

    def build_mask(first_rhs_is_ones, dsts):
        for half in range(2):
            for h in range(H):
                o = pE[half * 64:half * 64 + 64, h * 64:(h + 1) * 64]
                w0 = [(ve, vneg), (ac, adw)] if (half == 0 and h == 0) else ()
                if first_rhs_is_ones:
                    mm(o, cumrow[0:1, h * 64:(h + 1) * 64], orow_f[0:1, 0:64],
                       start=True, stop=False, tp=(0, half * 64), waits=w0)
                    mm(o, orow_f[0:1, 0:64], rows[0:1, h * 64:(h + 1) * 64],
                       start=False, stop=True, tp=(0, half * 64))
                else:
                    mm(o, rows[0:1, h * 64:(h + 1) * 64], orow_f[0:1, 0:64],
                       start=True, stop=False, tp=(0, half * 64), waits=w0)
                    mm(o, orow_f[0:1, 0:64], cumrow[0:1, h * 64:(h + 1) * 64],
                       start=False, stop=True, tp=(0, half * 64))
        p1 = penop()
        vm = v_op(lambda v: v.tensor_scalar_min(pE[:, 0:512], pE[:, 0:512], 0.0),
                  waits=[(pe, p1)])
        am = a_op(lambda s: s.activation(pF[:, 0:512], pE[:, 0:512], AF.Exp), waits=[(ve, vm)])
        vlast = 0
        for dstt in dsts:
            vlast = v_op(lambda v, dstt=dstt: v.tensor_mul(dstt[:], pF[:, 0:512], dstt[:]),
                         waits=[(ac, am)])
        return vlast

    vmA = build_mask(True, [maskA])
    P.wait("tensor", ve, vmA)
    vmT = build_mask(False, [maskAT, maskLTI])
    P.wait("tensor", ve, vmT)

    for h in range(H):
        mm(pD[:, h * 64:(h + 1) * 64], orow_f[:], derow[0:1, h * 64:(h + 1) * 64],
           waits=[(ac, aexp)] if h == 0 else ())
    p1 = penop()
    vdep = v_op(lambda v: v.tensor_copy(de_pat[:], pD[:, 0:512]), waits=[(pe, p1)])
    vctd = v_op(lambda v: v.tensor_copy(rows[0:1, 512:520], derow[0:1, 63::64]))
    mm(pD[:, 64:72], orow_f[:], rows[0:1, 512:520], waits=[(ve, vctd), (ve, vdep)])
    p1 = penop()
    vctd = v_op(lambda v: v.tensor_copy(ctdcol[:], pD[:, 64:72]), waits=[(pe, p1)])
    vr2 = 0
    for h in range(H):
        vr2 = v_op(lambda v, h=h: v.tensor_copy(
            rows[0:1, h * 128:(h + 1) * 128].rearrange("r (p q) -> r p q", p=2),
            dwrow[0:1, h * 64:(h + 1) * 64].unsqueeze(1).broadcast_to([1, 2, 64])),
            waits=[(ac, adw)] if h == 0 else ())
    for h in range(H):
        P.emit("tensor", lambda t, h=h: t.transpose(
            pD[:, 88 + h:89 + h], rows[0:1, h * 128:(h + 1) * 128], idf32[0:1, 0:1]),
            waits=[(ve, vr2), (ve, vctd)] if h == 0 else (), inc=pe)
    p1 = pe.n
    vdwc = v_op(lambda v: v.tensor_copy(dwcol[:], pD[:, 88:96]), waits=[(pe, p1)])

    dcr = s_dma(rows[0:1, 0:1024], c_row[:], waits=[(pe, pe.n)], selfwait=True)
    for k in range(8):
        P.emit("tensor", lambda t, k=k: t.transpose(
            pD[:, 72 + k:73 + k], rows[0:1, k * 128:(k + 1) * 128], idf32[0:1, 0:1]),
            waits=[dcr, (ve, vdwc)] if k == 0 else (), inc=pe)
    p1 = pe.n
    vcc = v_op(lambda v: v.tensor_copy(modcols[:, 40:48], pD[:, 72:80]), waits=[(pe, p1)])
    asc = a_op(lambda s: s.activation(sccol_b[:], modcols[:, 40:48], AF.Silu),
               waits=[(ve, vcc)])
    pk_prev = 0
    for k in range(8):
        piece = m_fm[:, (k % 2) * 6144:(k % 2) * 6144 + 6144]
        wts = [dwA] if k == 0 else [(pe, pk_prev)]
        dk = g_dma(piece, wmod[k * 128:(k + 1) * 128, :], waits=wts, par=k % 2)
        for mc in range(48):
            mm(pF[:, mc * 8 + k: mc * 8 + k + 1],
               piece[:, mc * 128:(mc + 1) * 128], sccol_b[:, k:k + 1],
               waits=[dk, (ac, asc)] if mc == 0 else ())
        pk_prev = P.inc_last("tensor", pe)
    p1 = pk_prev
    vmod = v_op(lambda v: v.tensor_reduce(
        modcols[:, 0:48], pF[:, 0:384].rearrange("p (m k) -> p m k", k=8), AX.X, ALU.add),
        waits=[(pe, p1)])
    v_op(lambda v: v.tensor_scalar_add(capcol[:], modcols[:, 8:16], 1.0))
    v_op(lambda v: v.tensor_scalar_add(cbpcol[:], modcols[:, 32:40], 1.0))
    for dstc, off in ((gacol, 16), (gbcol, 40)):
        v_op(lambda v, off=off: v.tensor_scalar(fac2[:, 0:8], modcols[:, off:off + 8],
                                                0.0, 2.0, ALU.is_ge, ALU.mult))
        vg = v_op(lambda v: v.tensor_scalar_sub(fac2[:, 0:8], fac2[:, 0:8], 1.0))
        ag = a_op(lambda s, off=off: s.activation(fac2[:, 8:16], modcols[:, off:off + 8], AF.Abs),
                  waits=[(ve, vg)])
        v_op(lambda v: v.tensor_scalar_max(fac2[:, 8:16], fac2[:, 8:16], 0.25),
             waits=[(ac, ag)])
        v_op(lambda v, dstc=dstc: v.tensor_mul(dstc[:], fac2[:, 0:8], fac2[:, 8:16]))
    mod_ready = ve.n
    P.wait("tensor", ve, mod_ready)


    def tern_matrix(src, scr, thr_col, acol, store):
        R, CC = src.shape
        npieces = R // 128
        vlast = ve.n
        entry_w = [(ve, ve.n), (po, po.n), (pe, pe.n)]
        for p in range(npieces):
            buf = mf32[:, (p % 2) * CC:(p % 2) * CC + CC]
            dk = g_dma(buf, src[p * 128:(p + 1) * 128, :],
                       waits=[(ve, vlast)] if p >= 2 else entry_w, par=p % 2)
            vlast = v_op(lambda v, buf=buf, p=p: v.tensor_reduce(
                acc_abs[:, 1 + p % 2:2 + p % 2], buf, AX.X, ALU.add,
                apply_absolute_value=True), waits=[dk])
            if p == 0:
                vlast = v_op(lambda v: v.tensor_copy(acc_abs[:, 0:1], acc_abs[:, 1:2]))
            else:
                vlast = v_op(lambda v, p=p: v.tensor_add(
                    acc_abs[:, 0:1], acc_abs[:, 0:1], acc_abs[:, 1 + p % 2:2 + p % 2]))
            yield
        mm(pF[0:1, 0:1], acc_abs[:, 0:1], onef[:],
           waits=[(ve, vlast)] + bank_wait(pF))
        p1 = penop()
        vthr = v_op(lambda v: v.tensor_scalar(stat[0:1, 0:1], pF[0:1, 0:1],
                                              1.0 / (R * CC), 1e-5, ALU.mult, ALU.max),
                    waits=[(pe, p1)])
        mm(pF[:, 4:5], orow_f[:], stat[0:1, 0:1], waits=[(ve, vthr)])
        p1 = penop()
        v_op(lambda v: v.tensor_copy(acol[:], pF[:, 4:5]), waits=[(pe, p1)])
        v_op(lambda v: v.tensor_scalar_mul(thr_col[:], pF[:, 4:5], 0.5))
        vthr = v_op(lambda v: v.tensor_scalar_mul(negt[:], thr_col[:], -1.0))
        bank_free[bank_key(pF)] = (ve, vthr)
        plast = po.n
        entry_pe = pe.n
        for p in range(npieces):
            buf = mf32[:, (p % 2) * CC:(p % 2) * CC + CC]
            wts = [(ve, vthr), (pe, entry_pe)] if p < 2 else [(ve, plast)]
            dk = g_dma(buf, src[p * 128:(p + 1) * 128, :], waits=wts, par=p % 2)
            v_op(lambda v, buf=buf: v.tensor_single_scalar(gu[:, 0:CC], buf,
                                                           thr_col[:], ALU.is_gt),
                 waits=[dk, (dstP[0], dstP[0].n), (dstP[1], dstP[1].n)])
            v_op(lambda v, buf=buf: v.tensor_single_scalar(xq[:, 0:CC], buf,
                                                           negt[:], ALU.is_lt))
            plast = v_op(lambda v, CC=CC: v.tensor_sub(gu[:, 0:CC], gu[:, 0:CC],
                                                       xq[:, 0:CC]))
            store(scr, p, [(ve, plast)], dstP[p % 2])
            yield
        return

    def store_gu(scr, p, waits, inc):
        out = scr.rearrange("(og q) (k o) -> q og k o", og=8, k=8)[:, :, p, :]
        in_ = gu[:, 0:INNER].rearrange("q (og o) -> q og o", og=8)
        g_dma(out, in_, waits=waits, inc=inc)

    def store_d(scr, p, waits, inc):
        kg, k = p // 8, p % 8
        out = scr.rearrange("(b q) (k o) -> q b k o", b=8, k=8)[:, kg * 2:kg * 2 + 2, k, :]
        in_ = gu[:, 0:D].rearrange("q (b o) -> q b o", b=2)
        g_dma(out, in_, waits=waits, inc=inc)

    def tern_gen():
        for a in tern_matrix(wgT, sc_g, thrg, agcol, store_gu):
            yield
        for a in tern_matrix(wuT, sc_u, thru, aucol, store_gu):
            yield
        for a in tern_matrix(wdT, sc_d, thrd, adcol, store_d):
            yield
    tern_ve = ve.n

    v_op(lambda v: v.memset(S32[:], 0.0))
    init_ve = v_op(lambda v: v.memset(Sb[:], 0.0))

    dwWW = g_dma(wsh[:], wwT[:], par=0)

    wbuf_ms = [0, 0]

    ffn_ms = {}

    def tile_gen(tile):
        xtile = xt[tile % 2]
        t0 = tile * TILE
        P.wait("vector", dxt, 16 * (tile + 1))
        P.wait("scalar", dxt, 16 * (tile + 1))
        P.wait("tensor", dxt, 16 * (tile + 1))
        if tile == 0:
            P.wait("vector", *dma0)
            P.wait("scalar", *dma0)
            P.wait("gpsimd", ve, init_ve)

        def ln_pass(capc, shoff, vpre, apre, xtile=None, exw=(), xdone=None, fold=None):
            sqdone = []
            for c in range(8):
                sl = slice(c * TILE, (c + 1) * TILE)
                if c % 2 == 0:
                    a_op(lambda s, sl=sl, xtile=xtile: s.activation(kpair[:, sl], xtile[:, sl],
                                                                    AF.Square),
                         waits=(([(ac, apre)] + list(exw)) if c == 0 else [])
                         + ([xdone[c]] if xdone else []))
                    sqdone.append((ac, ac.n))
                else:
                    v_op(lambda v, sl=sl, xtile=xtile: v.tensor_mul(kpair[:, sl], xtile[:, sl],
                                                                    xtile[:, sl]),
                         waits=([(ve, vpre)] + list(exw)) if c == 1 else ())
                    sqdone.append((ve, ve.n))
            for c in range(8):
                mm(pE[0:1, 0:512], onef[:], xtile[:, c * TILE:(c + 1) * TILE],
                   start=(c == 0), stop=(c == 7),
                   waits=(bank_wait(pE) if c == 0 else [])
                   + ([xdone[c]] if xdone else []))
            for c in range(8):
                mm(pF[0:1, 0:512], ocol[:], kpair[:, c * TILE:(c + 1) * TILE],
                   start=(c == 0), stop=(c == 7),
                   waits=([sqdone[c]] + (bank_wait(pF) if c == 0 else [])))
            p1 = penop()
            v_op(lambda v: v.tensor_scalar_mul(stat[0:1, 0:512], pE[0:1, 0:512], 1.0 / D),
                 waits=[(pe, p1)])
            v_op(lambda v: v.tensor_scalar_mul(stat[0:1, 512:1024], pF[0:1, 0:512], 1.0 / D))
            v_op(lambda v: v.tensor_mul(stat[0:1, 1024:1536], stat[0:1, 0:512], stat[0:1, 0:512]))
            vv = v_op(lambda v: v.tensor_sub(stat[0:1, 512:1024], stat[0:1, 512:1024],
                                             stat[0:1, 1024:1536]))
            aa = a_op(lambda s: s.activation(stat[0:1, 1024:1536], stat[0:1, 512:1024],
                                             AF.Sqrt, bias=epsc[0:1, 0:1]), waits=[(ve, vv)])
            v_op(lambda v: v.reciprocal(stat[0:1, 512:1024], stat[0:1, 1024:1536]),
                 waits=[(ac, aa)])
            vv = v_op(lambda v: v.tensor_mul(stat[0:1, 1024:1536], stat[0:1, 0:512],
                                             stat[0:1, 512:1024]))
            mm(pE[:, 0:512], orow_f[:], stat[0:1, 512:1024], waits=[(ve, vv)])
            mm(pF[:, 0:512], orow_f[:], stat[0:1, 1024:1536])
            p1 = penop()
            alast = 0
            aprev = 0
            rbc = pE[:, 0:512].unsqueeze(1).broadcast_to([128, 2, 512])
            cbc = pF[:, 0:512].unsqueeze(1).broadcast_to([128, 2, 512])
            for cp in range(4):
                sl = slice(cp * 1024, (cp + 1) * 1024)
                aw = ([(pe, p1)] if cp == 0 else []) + ([(ac, aprev)] if aprev else [])
                v_op(lambda v, sl=sl, xtile=xtile: v.tensor_mul(
                    kp32[:, 0:1024].rearrange("p (a q) -> p a q", a=2),
                    xtile[:, sl].rearrange("p (a q) -> p a q", a=2), rbc), waits=aw)
                vap = v_op(lambda v: v.tensor_sub(
                    kp32[:, 0:1024].rearrange("p (a q) -> p a q", a=2),
                    kp32[:, 0:1024].rearrange("p (a q) -> p a q", a=2), cbc))
                for c in (2 * cp, 2 * cp + 1):
                    sl1 = slice(c * TILE, (c + 1) * TILE)
                    ssl = slice((c % 2) * TILE, (c % 2) * TILE + TILE)
                    alast = a_op(lambda s, sl1=sl1, ssl=ssl, c=c: s.activation(
                        h_fm[:, sl1], kp32[:, ssl], AF.Identity,
                        bias=modcols[:, shoff + c:shoff + c + 1],
                        scale=capc[:, c:c + 1]), waits=[(ve, vap)])
                aprev = alast
                if fold is not None:
                    fold(cp, alast)
            bank_free[bank_key(pE)] = (ve, ve.n)
            bank_free[bank_key(pF)] = (ve, ve.n)
            P.wait("tensor", ve, ve.n)
            P.wait("tensor", ac, alast)
            return alast

        yield
        if tile == 0:
            P.wait("vector", ve, mod_ready)
            h_ready = ln_pass(capcol, 0, tern_ve, ac.n, xtile=xtile)
        else:
            h_ready = ln_pass(capcol, 0, ve.n, ac.n, xtile=xtile,
                              exw=[(pe, pe.n), (po, po.n)])
        yield

        for tc in range(4):
            for k in range(8):
                mm(pE[:, tc * 8:(tc + 1) * 8],
                   h_fm[:, k * TILE + tc * 128: k * TILE + tc * 128 + 128],
                   wbT_sb[:, k * 8:(k + 1) * 8], start=(k == 0), stop=(k == 7),
                   waits=([(ac, h_ready), dwA] + bank_wait(pE))
                   if (tc == 0 and k == 0) else ())
        p1 = penop()
        ab = a_op(lambda s: s.activation(betacol[:], pE[:, 0:32], AF.Sigmoid),
                  waits=[(pe, p1)])
        bank_free[bank_key(pE)] = (ac, ab)

        hsqdone = []
        for c in range(8):
            sl = slice(c * TILE, (c + 1) * TILE)
            if c % 2 == 0:
                a_op(lambda s, sl=sl: s.activation(kpair[:, sl], h_fm[:, sl], AF.Square))
                hsqdone.append((ac, ac.n))
            else:
                v_op(lambda v, sl=sl: v.tensor_mul(kpair[:, sl], h_fm[:, sl], h_fm[:, sl]),
                     waits=[(ac, h_ready)] if c == 1 else [])
                hsqdone.append((ve, ve.n))
        for h in range(H):
            for tc in range(4):
                mm(pF[:, tc * 8 + h: tc * 8 + h + 1],
                   kpair[:, h * TILE + tc * 128: h * TILE + tc * 128 + 128], ocol[:],
                   waits=([hsqdone[h]] + (bank_wait(pF) if h == 0 else [])) if tc == 0 else ())
        p1 = penop()
        aa = a_op(lambda s: s.activation(f1col[:], pF[:, 0:32], AF.Sqrt),
                  waits=[(pe, p1)])
        v_op(lambda v: v.tensor_scalar_max(f1col[:], f1col[:], 1e-12),
             waits=[(ac, aa)])
        f1v = fac2[:, 0:64].rearrange("p (t f h) -> p t f h", t=4, f=2)[:, :, 0, :]
        f2v = fac2[:, 0:64].rearrange("p (t f h) -> p t f h", t=4, f=2)[:, :, 1, :]
        v_op(lambda v: v.reciprocal(f1v, f1col[:].rearrange("p (t h) -> p t h", t=4)))
        v_op(lambda v: v.scalar_tensor_tensor(f2v, betacol[:].rearrange("p (t h) -> p t h", t=4),
                                              -1.0, f1v, ALU.mult, ALU.mult))
        vfac = v_op(lambda v: v.tensor_mul(
            kdwfcol[:].rearrange("p (t h) -> p t h", t=4), f1v,
            dwcol[:].unsqueeze(1).broadcast_to([128, 4, 8])))
        bank_free[bank_key(pF)] = (ve, vfac)
        yield

        for tc in range(4):
            for k in range(8):
                lh = h_fm[:, k * TILE + tc * 128: k * TILE + tc * 128 + 128]
                for half in range(2):
                    mm(pB[:, half * 512:(half + 1) * 512], lh,
                       wsh[:, k * D + half * 512: k * D + (half + 1) * 512],
                       start=(k == 0), stop=(k == 7),
                       waits=([dwWW] + bank_wait(pB))
                       if (tc == 0 and k == 0 and half == 0) else ())
            p1 = penop()
            for h in range(H):
                v_op(lambda v, tc=tc, h=h: v.tensor_single_scalar(
                    vtm[:, tc * D + h * 128: tc * D + (h + 1) * 128],
                    pB[:, h * 128:(h + 1) * 128],
                    betacol[:, tc * 8 + h: tc * 8 + h + 1], ALU.mult),
                    waits=[(ac, ab), (pe, p1)] if h == 0 else ())
            bank_free[bank_key(pB)] = (ve, ve.n)
            P.wait("tensor", ve, ve.n)
            if tc == 1:
                for fc in range(4):
                    inp = fac2[:, fc * 16:(fc + 1) * 16]
                    P.emit("tensor", lambda t, inp=inp, fc=fc: t.transpose(
                        pD[0:16, fc * 128:(fc + 1) * 128], inp, idf32[:]),
                        waits=([(ve, vfac)] + bank_wait(pD)) if fc == 0 else (), inc=pe)
                p1t = pe.n
                vM = v_op(lambda v: v.tensor_copy(Mfac[:], pD[0:16, 0:512]),
                          waits=[(pe, p1t)])
                bank_free[bank_key(pD)] = (ve, vM)
            yield
        vvtm = ve.n
        P.wait("tensor", ve, vM)
        yield
        pBa = pB[:, 0:512]
        pBb = pB[:, 512:1024]
        _bkey[id(pBa)] = bank_key(pB)
        _bkey[id(pBb)] = bank_key(pB)
        _bkey[id(psT32)] = bank_key(psT)
        kbanks = [(pE, pF), (pBa, pBb)]
        for h in range(H):
            bE, bF = kbanks[h % 2]
            mm(bE[:, 0:512], e16[:, h:h + 1].broadcast_to([16, 128]), Mfac[:],
               waits=bank_wait(bE))
            mm(bF[:, 0:512], e16[:, 8 + h:9 + h].broadcast_to([16, 128]), Mfac[:],
               waits=bank_wait(bF))
            p1 = penop()
            hsl = h_fm[:, h * TILE:(h + 1) * TILE].rearrange("p (c q) -> p c q", c=8)
            kcell = kpair[:, h * 8 * 128:(h + 1) * 8 * 128].rearrange("p (c q) -> p c q", c=8)
            v_op(lambda v, hsl=hsl, kcell=kcell, bE=bE: v.tensor_mul(
                kcell[:, :, 0:64], hsl, bE[:, 0:512].rearrange("p (c q) -> p c q", c=8)),
                waits=[(pe, p1)])
            vkb = v_op(lambda v, hsl=hsl, kcell=kcell, bF=bF: v.tensor_mul(
                kcell[:, :, 64:128], hsl, bF[:, 0:512].rearrange("p (c q) -> p c q", c=8)))
            bank_free[bank_key(bE)] = (ve, vkb)
            bank_free[bank_key(bF)] = (ve, vkb)
            if h % 2 == 1:
                yield
        P.wait("tensor", ve, vkb)
        vkp = ve.n

        for tc in range(4):
            for h in range(H):
                P.emit("tensor", lambda t, tc=tc, h=h: t.transpose(
                    psT[:, h * 128:(h + 1) * 128],
                    h_fm[:, h * TILE + tc * 128: h * TILE + tc * 128 + 128], id128[:]),
                    waits=([(ac, h_ready)] + bank_wait(psT)) if (tc == 0 and h == 0) else (),
                    inc=pe)
            p1 = pe.n
            for h in range(H):
                v_op(lambda v, tc=tc, h=h: v.tensor_single_scalar(
                    kdw_tm[:, tc * D + h * 128: tc * D + (h + 1) * 128],
                    psT[:, h * 128:(h + 1) * 128],
                    kdwfcol[:, tc * 8 + h: tc * 8 + h + 1], ALU.mult),
                    waits=[(pe, p1)] if h == 0 else ())
            bank_free[bank_key(psT)] = (ve, ve.n)
            P.wait("tensor", ve, ve.n)
            yield

        pBa = pB[:, 0:512]
        pBb = pB[:, 512:1024]
        _bkey[id(pBa)] = bank_key(pB)
        _bkey[id(pBb)] = bank_key(pB)
        _bkey[id(psT32)] = bank_key(psT)

        def a_raws(n, b0, b1, b2):
            par = n % 2
            phs = slice(par * 64, par * 64 + 64)
            for h in range(H):
                cell = kpair[:, (h * 8 + n) * 128:(h * 8 + n + 1) * 128]
                mm(b0[phs, h * 64:(h + 1) * 64], cell[:, 64:128], cell[:, 0:64],
                   tp=(0, par * 64),
                   waits=(([(ve, vkp)] if n == 0 else []) + bank_wait(b0)
                          + bank_wait(b1) + bank_wait(b2)) if h == 0 else ())
            for h in range(H):
                cell = kpair[:, (h * 8 + n) * 128:(h * 8 + n + 1) * 128]
                mm(b1[phs, h * 64:(h + 1) * 64], cell[:, 0:64], cell[:, 0:64],
                   tp=(0, par * 64))
            for h in range(H):
                cell = kpair[:, (h * 8 + n) * 128:(h * 8 + n + 1) * 128]
                mm(b2[phs, h * 64:(h + 1) * 64], cell[:, 0:64], cell[:, 64:128],
                   tp=(0, par * 64))
            return penop()

        def a_inits(n, b0, b1, b2, praw):
            par = n % 2
            phs = slice(par * 64, par * 64 + 64)
            v_op(lambda v, phs=phs, b0=b0: v.tensor_mul(Xs[phs, :], b0[phs, 0:512],
                                                        maskA[phs, :]), waits=[(pe, praw)])
            v_op(lambda v, phs=phs, n=n, b1=b1: v.tensor_mul(
                att_all[phs, (n // 2) * 512:(n // 2 + 1) * 512], b1[phs, 0:512],
                maskLTI[phs, :]))
            v_op(lambda v, phs=phs, b2=b2: v.tensor_mul(Ys[phs, :], b2[phs, 0:512],
                                                        maskAT[phs, :]))
            return v_op(lambda v, phs=phs: v.tensor_add(Ps[phs, :], Ys[phs, :], ipk[phs, :]))

        def a_xy(n, j, b0, b2, vrdy):
            par = n % 2
            phs = slice(par * 64, par * 64 + 64)
            for h in range(H):
                mm(b0[phs, h * 64:(h + 1) * 64], Ys[phs, h * 64:(h + 1) * 64],
                   Xs[phs, h * 64:(h + 1) * 64], tp=(par * 64, par * 64),
                   waits=[(ve, vrdy), (ac, ac.n)] if h == 0 else ())
            if j < 5:
                for h in range(H):
                    mm(b2[phs, h * 64:(h + 1) * 64], Xs[phs, h * 64:(h + 1) * 64],
                       Ys[phs, h * 64:(h + 1) * 64], tp=(par * 64, par * 64))
            return penop()

        def a_xyev(n, j, b0, b2, pxy):
            par = n % 2
            phs = slice(par * 64, par * 64 + 64)
            aX = a_op(lambda s, phs=phs, b0=b0: s.activation(Xs[phs, :], b0[phs, 0:512],
                                                             AF.Identity), waits=[(pe, pxy)])
            if j < 5:
                aX = a_op(lambda s, phs=phs, b2=b2: s.activation(Ys[phs, :], b2[phs, 0:512],
                                                                 AF.Identity))
            return aX

        def a_pmm(n, b1, aX):
            par = n % 2
            phs = slice(par * 64, par * 64 + 64)
            for h in range(H):
                mm(b1[phs, h * 64:(h + 1) * 64], Xs[phs, h * 64:(h + 1) * 64],
                   Ps[phs, h * 64:(h + 1) * 64], tp=(par * 64, par * 64),
                   waits=[(ac, aX)] if h == 0 else ())
            return penop()

        def a_pev(n, j, b1, ppm):
            par = n % 2
            phs = slice(par * 64, par * 64 + 64)
            if j < 5:
                return v_op(lambda v, phs=phs, b1=b1: v.tensor_add(Ps[phs, :], Ps[phs, :],
                                                                   b1[phs, 0:512]),
                            waits=[(pe, ppm)])
            return v_op(lambda v, phs=phs, n=n, b1=b1: v.tensor_add(
                at_all[phs, (n // 2) * 512:(n // 2 + 1) * 512], Ps[phs, :],
                b1[phs, 0:512]), waits=[(pe, ppm)])

        bt_even = (pD, pE, pF)
        bt_odd = (pBa, psT32, pBb)
        for pair in range(4):
            c0, c1 = 2 * pair, 2 * pair + 1
            praw0 = a_raws(c0, *bt_even)
            praw1 = a_raws(c1, *bt_odd)
            v0 = a_inits(c0, *bt_even, praw0)
            v1 = a_inits(c1, *bt_odd, praw1)
            for j in range(1, 6):
                pxy0 = a_xy(c0, j, bt_even[0], bt_even[2], v0)
                pxy1 = a_xy(c1, j, bt_odd[0], bt_odd[2], v1)
                aX0 = a_xyev(c0, j, bt_even[0], bt_even[2], pxy0)
                aX1 = a_xyev(c1, j, bt_odd[0], bt_odd[2], pxy1)
                ppm0 = a_pmm(c0, bt_even[1], aX0)
                ppm1 = a_pmm(c1, bt_odd[1], aX1)
                v0 = a_pev(c0, j, bt_even[1], ppm0)
                v1 = a_pev(c1, j, bt_odd[1], ppm1)
            for b in bt_even:
                bank_free[bank_key(b)] = (ve, v0)
            for b in bt_odd:
                bank_free[bank_key(b)] = (ve, v1)
            yield
        bank_free[bank_key(pB)] = (ve, v1)
        vchainA = ve.n

        dwR0 = g_dma(wbuf[0][:], wrT[:, 0:4096], waits=[(pe, wbuf_ms[0])], par=0)
        dwR1 = g_dma(wbuf[1][:], wrT[:, 4096:8192], waits=[(pe, wbuf_ms[1])], par=1)
        def mk_kde(n, wts):
            kf_c = kpair.rearrange("p (h c q) -> p h c q", h=8, c=8)[:, :, n, 0:64]
            kb_c = kpair.rearrange("p (h c q) -> p h c q", h=8, c=8)[:, :, n, 64:128]
            dep = de_pat[:].rearrange("p (h q) -> p h q", h=8)
            pb = n % 2
            g_op(lambda g, kf_c=kf_c, dep=dep, pb=pb: g.tensor_mul(
                kde_s[:, pb * 512:(pb + 1) * 512].rearrange("p (h q) -> p h q", h=8),
                kf_c, dep), waits=wts)
            return g_op(lambda g, kb_c=kb_c, dep=dep, pb=pb: g.tensor_mul(
                kbdn_s[:, pb * 512:(pb + 1) * 512].rearrange("p (h q) -> p h q", h=8),
                kb_c, dep))

        def b_o_mms(n):
            par = n % 2
            tc = n // 2
            phs = slice(par * 64, par * 64 + 64)
            sb_v = Sb[:, (par) * 1024:(par) * 1024 + 1024]
            for h in range(H):
                o1w = bank_wait(psT) if h == 0 else ()
                mm(psT32[:, h * 64:(h + 1) * 64], sb_v[:, h * 128:(h + 1) * 128],
                   kde_s[:, par * 512 + h * 64: par * 512 + (h + 1) * 64],
                   start=True, stop=False, waits=o1w)
                mm(psT32[:, h * 64:(h + 1) * 64], vnewtm[phs, h * 128:(h + 1) * 128],
                   att_all[phs, (n // 2) * 512 + h * 64:(n // 2) * 512 + (h + 1) * 64],
                   start=False, stop=True, tp=(par * 64, 0))
            return penop()

        def b_o_evac(n, po1):
            v_op(lambda v, n=n: v.tensor_copy(
                o_fm[:].rearrange("p (h t) -> p h t", h=8)[:, :, n * 64:(n + 1) * 64],
                psT32[:, 0:512].rearrange("p (h q) -> p h q", h=8)), waits=[(pe, po1)])
            bank_free[bank_key(psT)] = (ve, ve.n)

        pkde = mk_kde(0, [(ve, vchainA)])
        prev_o = None
        for n in range(CHT):
            par = n % 2
            tc = n // 2
            phs = slice(par * 64, par * 64 + 64)
            sb_v = Sb[:, par * 1024:par * 1024 + 1024]
            for h in range(H):
                w0w = ([(po, pkde)] + bank_wait(pB)) if h == 0 else []
                if n == 0 and h == 0:
                    w0w = w0w + [(ve, vvtm)]
                mm(pB[phs, h * 128:(h + 1) * 128],
                   kbdn_s[:, par * 512 + h * 64: par * 512 + (h + 1) * 64],
                   sb_v[:, h * 128:(h + 1) * 128], start=True, stop=False, tp=(0, par * 64),
                   waits=w0w)
                mm(pB[phs, h * 128:(h + 1) * 128], idc[phs, :],
                   vtm[phs, tc * D + h * 128: tc * D + (h + 1) * 128],
                   start=False, stop=True, tp=(par * 64, par * 64))
            p1 = penop()
            aw0 = a_op(lambda s, phs=phs: s.activation(w0tm[phs, 0:512], pB[phs, 0:512],
                                                       AF.Identity), waits=[(pe, p1)])
            vw0 = v_op(lambda v, phs=phs: v.tensor_copy(w0tm[phs, 512:1024], pB[phs, 512:1024]),
                       waits=[(pe, p1)])
            for h in range(H):
                mm(pB[phs, h * 128:(h + 1) * 128],
                   at_all[phs, (n // 2) * 512 + h * 64:(n // 2) * 512 + (h + 1) * 64],
                   w0tm[phs, h * 128:(h + 1) * 128], tp=(par * 64, par * 64),
                   waits=[(ac, aw0), (ve, vw0)] if h == 0 else ())
            p1 = penop()
            avn = a_op(lambda s, phs=phs: s.activation(vnewtm[phs, 0:512], pB[phs, 0:512],
                                                       AF.Identity), waits=[(pe, p1)])
            vvn = v_op(lambda v, phs=phs: v.tensor_copy(vnewtm[phs, 512:1024], pB[phs, 512:1024]),
                       waits=[(pe, p1)])
            if prev_o is not None:
                po1 = b_o_mms(n - 1)
            if n + 1 < CHT:
                nkde = mk_kde(n + 1, [(pe, pe.n)])
            for h in range(H):
                mm(pB[:, h * 128:(h + 1) * 128],
                   kdw_tm[phs, tc * D + h * 128: tc * D + (h + 1) * 128],
                   vnewtm[phs, h * 128:(h + 1) * 128], tp=(par * 64, 0),
                   waits=[(ac, avn), (ve, vvn)] if h == 0 else ())
            pds = penop()
            if prev_o is not None:
                b_o_evac(n - 1, po1)
            P.wait("vector", pe, pds)
            for h in range(H):
                v_op(lambda v, h=h: v.scalar_tensor_tensor(
                    S32[:, h * 128:(h + 1) * 128], S32[:, h * 128:(h + 1) * 128],
                    ctdcol[:, h:h + 1], pB[:, h * 128:(h + 1) * 128],
                    ALU.mult, ALU.add))
            vS = ve.n
            bank_free[bank_key(pB)] = (ve, vS)
            v_op(lambda v, n=n: v.tensor_copy(Sb[:, ((n + 1) % 2) * 1024:((n + 1) % 2) * 1024 + 1024],
                                              S32[:]))
            P.wait("tensor", ve, ve.n)
            prev_o = n
            pkde = nkde if n + 1 < CHT else pkde
            yield
        po1 = b_o_mms(CHT - 1)
        b_o_evac(CHT - 1, po1)
        vchunks = ve.n

        res1done = []
        for f in range(8):
            bank = pE if f % 2 == 0 else pF
            for k in range(8):
                mm(bank[:, 0:512],
                   wbuf[k // 4][:, (k % 4) * D + f * 128: (k % 4) * D + f * 128 + 128],
                   o_fm[:, k * TILE:(k + 1) * TILE], start=(k == 0), stop=(k == 7),
                   waits=(([(ve, vchunks), dwR0, dwR1] if f == 0 else [])
                          + bank_wait(bank)) if k == 0 else ())
            p1 = penop()
            ae1 = a_op(lambda s, bank=bank, f=f: s.activation(tq3[:, 0:512], bank[:, 0:512],
                                                              AF.Identity,
                                                              scale=gacol[:, f:f + 1]),
                       waits=[(pe, p1), (ve, ve.n)])
            bank_free[bank_key(bank)] = (ac, ae1)
            v_op(lambda v, f=f, xtile=xtile: v.tensor_add(xtile[:, f * TILE:(f + 1) * TILE],
                                             xtile[:, f * TILE:(f + 1) * TILE], tq3[:, 0:512]),
                 waits=[(ac, ae1)])
            res1done.append((ve, v_op(lambda v, f=f, xtile=xtile: v.scalar_tensor_tensor(
                xtile[:, f * TILE:(f + 1) * TILE], h_fm[:, f * TILE:(f + 1) * TILE],
                gacol[:, f:f + 1], xtile[:, f * TILE:(f + 1) * TILE], ALU.mult, ALU.add))))
            if f % 2 == 1:
                yield
        vres1 = ve.n
        wbuf_ms[0] = wbuf_ms[1] = pe.n
        P.wait("scalar", po, po.n)
        P.wait("vector", po, po.n)

        h2_ready = ln_pass(cbpcol, 24, vres1, ac.n, xtile=xtile, xdone=res1done)

        yield "q1barrier"
        if tile + 1 < NT:
            s_dma(xt[(tile + 1) % 2][:], xT[:, (tile + 1) * 4096:(tile + 2) * 4096],
                  waits=[(so, so.n)], inc=dxt, selfwait=True)
        v_op(lambda v: v.tensor_tensor(tq[:], h_fm[:, 0:512], h_fm[:, 512:1024], ALU.max),
             waits=[(ac, h2_ready), ffn_ms.get("dancepe", (pe, 0))])
        v_op(lambda v: v.tensor_tensor(tq[:], tq[:], h_fm[:, 1024:1536], ALU.max))
        v_op(lambda v: v.tensor_tensor(tq[:], tq[:], h_fm[:, 1536:2048], ALU.max))
        v_op(lambda v: v.tensor_tensor(sbc[:], h_fm[:, 0:512], h_fm[:, 512:1024], ALU.min))
        v_op(lambda v: v.tensor_tensor(sbc[:], sbc[:], h_fm[:, 1024:1536], ALU.min))
        v_op(lambda v: v.tensor_tensor(sbc[:], sbc[:], h_fm[:, 1536:2048], ALU.min))
        v_op(lambda v: v.tensor_tensor(tq2[:], h_fm[:, 2048:2560], h_fm[:, 2560:3072], ALU.max))
        v_op(lambda v: v.tensor_tensor(tq2[:], tq2[:], h_fm[:, 3072:3584], ALU.max))
        v_op(lambda v: v.tensor_tensor(tq2[:], tq2[:], h_fm[:, 3584:4096], ALU.max))
        v_op(lambda v: v.tensor_tensor(sibc[:], h_fm[:, 2048:2560], h_fm[:, 2560:3072], ALU.min))
        v_op(lambda v: v.tensor_tensor(sibc[:], sibc[:], h_fm[:, 3072:3584], ALU.min))
        v_op(lambda v: v.tensor_tensor(sibc[:], sibc[:], h_fm[:, 3584:4096], ALU.min))
        v_op(lambda v: v.tensor_tensor(tq[:], tq[:], tq2[:], ALU.max))
        v_op(lambda v: v.tensor_tensor(sbc[:], sbc[:], sibc[:], ALU.min))
        vq = v_op(lambda v: v.scalar_tensor_tensor(tq[:], sbc[:], -1.0,
                                                   tq[:], ALU.mult, ALU.max))

        def quant_dance(vq, col_xform):
            for cell in range(4):
                P.emit("tensor", lambda t, cell=cell: t.transpose(
                    psT[:, cell * 128:(cell + 1) * 128], tq[:, cell * 128:(cell + 1) * 128],
                    id128[:]),
                    waits=([(ve, vq)] + bank_wait(psT)) if cell == 0 else (), inc=pe)
            p1 = pe.n
            for cell in range(4):
                v_op(lambda v, cell=cell: v.tensor_reduce(
                    amaxc[:, cell:cell + 1], psT[:, cell * 128:(cell + 1) * 128], AX.X, ALU.max),
                    waits=[(pe, p1)] if cell == 0 else ())
            bank_free[bank_key(psT)] = (ve, ve.n)
            vq2 = col_xform()
            P.emit("tensor", lambda t: t.transpose(pD[0:8, 0:128], amaxc[:, 0:8], idf32[:]),
                   waits=[(ve, vq2)] + bank_wait(pD), inc=pe)
            p1 = pe.n
            vq2 = v_op(lambda v: v.tensor_copy(Mq[:], pD[0:8, 0:128]), waits=[(pe, p1)])
            bank_free[bank_key(pD)] = (ve, vq2)
            P.wait("tensor", ve, vq2)
            for cell in range(4):
                mm(pE[:, cell * 128:(cell + 1) * 128], e8[:, 4 + cell:5 + cell].broadcast_to([8, 128]),
                   Mq[:], waits=bank_wait(pE) if cell == 0 else ())
                mm(pF[:, cell * 128:(cell + 1) * 128], e8[:, cell:cell + 1].broadcast_to([8, 128]),
                   Mq[:], waits=bank_wait(pF) if cell == 0 else ())
            p1 = penop()
            v_op(lambda v: v.tensor_copy(sbc[:], pE[:, 0:512]), waits=[(pe, p1)])
            vbc = v_op(lambda v: v.tensor_copy(sibc[:], pF[:, 0:512]))
            bank_free[bank_key(pE)] = (ve, vbc)
            bank_free[bank_key(pF)] = (ve, vbc)
            P.wait("tensor", ve, vbc)
            return vbc

        def q1cols():
            v_op(lambda v: v.tensor_scalar_max(amaxc[:, 0:4], amaxc[:, 0:4], 1e-5))
            v_op(lambda v: v.reciprocal(amaxc[:, 4:8], amaxc[:, 0:4]))
            v_op(lambda v: v.tensor_scalar_mul(amaxc[:, 4:8], amaxc[:, 4:8], 127.0))
            return v_op(lambda v: v.tensor_scalar_mul(amaxc[:, 0:4], amaxc[:, 0:4], 1.0 / 127.0))

        quant_dance(vq, q1cols)
        for cp in range(4):
            sl = slice(cp * 1024, (cp + 1) * 1024)
            bc2 = sbc[:].unsqueeze(1).broadcast_to([128, 2, 512])
            bc2i = sibc[:].unsqueeze(1).broadcast_to([128, 2, 512])
            v_op(lambda v, sl=sl, bc2=bc2: v.tensor_mul(
                kpair[:, 0:1024].rearrange("p (a q) -> p a q", a=2),
                h_fm[:, sl].rearrange("p (a q) -> p a q", a=2), bc2))
            v_op(lambda v: v.tensor_scalar(kpair[:, 0:1024], kpair[:, 0:1024],
                                           MAGIC, -MAGIC, ALU.add, ALU.add))
            v_op(lambda v, sl=sl, bc2i=bc2i: v.tensor_mul(
                xq[:, sl].rearrange("p (a q) -> p a q", a=2),
                kpair[:, 0:1024].rearrange("p (a q) -> p a q", a=2), bc2i),
                waits=[ffn_ms["qb"]] if (cp == 0 and "qb" in ffn_ms) else [])
        vxq = ve.n
        yield "ffnstart"

        if tile == 0:
            P.wait("gpsimd", dstP[0], dstP[0].n)
            P.wait("gpsimd", dstP[1], dstP[1].n)
        q1pe = pe.n
        rot = [pA, pC]
        ri = 0
        qm = ffn_ms.get("qmul", (ve, 0))
        mml = {0: qm, 1: qm, 2: qm, 3: qm}
        for og in range(8):
            for which, scr in ((0, sc_g), (1, sc_u)):
                buf = wbuf[(og * 2 + which) % 2]
                dk = g_dma(buf[:], scr[og * 128:(og + 1) * 128, :],
                           waits=[(pe, wbuf_ms[(og * 2 + which) % 2])],
                           par=(og * 2 + which) % 2)
                for osub in range(4):
                    bank = rot[ri % 2]
                    ri += 1
                    for k in range(8):
                        mm(bank[:, 0:512], buf[:, k * 512 + osub * 128: k * 512 + osub * 128 + 128],
                           xq[:, k * TILE:(k + 1) * TILE], start=(k == 0), stop=(k == 7),
                           waits=([dk, (ve, vxq)] + bank_wait(bank)) if k == 0 else ())
                    p1 = penop()
                    if osub == 3:
                        wbuf_ms[(og * 2 + which) % 2] = p1
                    if which == 0:
                        aev = a_op(lambda s, osub=osub, bank=bank: s.activation(
                            gu[:, osub * 512:(osub + 1) * 512], bank[:, 0:512], AF.Silu,
                            scale=agcol[:, 0:1]), waits=[(pe, p1), mml[osub]])
                    else:
                        aev = a_op(lambda s, osub=osub, bank=bank: s.activation(
                            gu[:, 2048 + osub * 512:2048 + (osub + 1) * 512], bank[:, 0:512],
                            AF.Identity), waits=[(pe, p1)])
                        oc = og * 4 + osub
                        mml[osub] = (ve, v_op(lambda v, osub=osub, oc=oc: v.tensor_mul(
                            m_fm[:, oc * 512:(oc + 1) * 512], gu[:, osub * 512:(osub + 1) * 512],
                            gu[:, 2048 + osub * 512:2048 + (osub + 1) * 512]),
                            waits=[(ac, aev)]))
                        if oc == 1:
                            v_op(lambda v: v.tensor_tensor(
                                tq[:], m_fm[:, 0:512], m_fm[:, 512:1024], ALU.max),
                                waits=[(pe, q1pe)])
                            v_op(lambda v: v.tensor_tensor(
                                sbc[:], m_fm[:, 0:512], m_fm[:, 512:1024], ALU.min))
                        elif oc >= 2:
                            v_op(lambda v, oc=oc: v.tensor_tensor(
                                tq[:], tq[:], m_fm[:, oc * 512:(oc + 1) * 512], ALU.max))
                            v_op(lambda v, oc=oc: v.tensor_tensor(
                                sbc[:], sbc[:], m_fm[:, oc * 512:(oc + 1) * 512], ALU.min))
                    bank_free[bank_key(bank)] = (ac, aev)
            yield
        pm = ve.n

        vq = v_op(lambda v: v.scalar_tensor_tensor(tq[:], sbc[:], -1.0,
                                                   tq[:], ALU.mult, ALU.max),
                  waits=[(ve, pm)])

        def q2cols():
            v_op(lambda v: v.tensor_single_scalar(amaxc[:, 0:4], amaxc[:, 0:4],
                                                  aucol[:, 0:1], ALU.mult))
            v_op(lambda v: v.tensor_scalar_max(amaxc[:, 0:4], amaxc[:, 0:4], 1e-5))
            v_op(lambda v: v.reciprocal(amaxc[:, 4:8], amaxc[:, 0:4]))
            v_op(lambda v: v.tensor_scalar_mul(amaxc[:, 4:8], amaxc[:, 4:8], 127.0))
            v_op(lambda v: v.tensor_single_scalar(amaxc[:, 4:8], amaxc[:, 4:8],
                                                  aucol[:, 0:1], ALU.mult))
            v_op(lambda v: v.tensor_single_scalar(amaxc[:, 0:4], amaxc[:, 0:4],
                                                  adcol[:, 0:1], ALU.mult))
            return v_op(lambda v: v.tensor_scalar_mul(amaxc[:, 0:4], amaxc[:, 0:4], 1.0 / 127.0))

        quant_dance(vq, q2cols)
        ffn_ms["dancepe"] = (pe, pe.n)
        rms = []
        amag = [ac.n, ac.n]
        pegu = pe.n
        for ip in range(16):
            sl = slice(ip * 1024, (ip + 1) * 1024)
            qa = gu[:, (ip % 2) * 2048:(ip % 2) * 2048 + 1024]
            qb = xq32[:, (ip % 2) * 1024:(ip % 2) * 1024 + 1024]
            bc2 = sbc[:].unsqueeze(1).broadcast_to([128, 2, 512])
            v_op(lambda v, sl=sl, bc2=bc2, qa=qa: v.tensor_mul(
                qa.rearrange("p (a q) -> p a q", a=2),
                m_fm[:, sl].rearrange("p (a q) -> p a q", a=2), bc2),
                waits=[(ac, amag[ip % 2])])
            a_op(lambda s, qa=qa, qb=qb: s.activation(qb, qa, AF.Identity, bias=magc[:]),
                 waits=[(ve, ve.n)] + ([(pe, pegu)] if ip < 2 else []))
            amag[ip % 2] = a_op(lambda s, sl=sl, qb=qb: s.activation(
                m_fm[:, sl], qb, AF.Identity, bias=nmagc[:]))
            if ip % 4 == 3:
                rms.append((ac, amag[ip % 2]))
        ffn_ms["qb"] = (ac, ac.n)
        ffn_ms["qmul"] = (ve, ve.n)
        vq2v = ve.n
        yield

        for fg in range(2):
            banks = [pA, pC, pD, pE]
            for kg in range(4):
                buf = wbuf[kg % 2]
                dk = g_dma(buf[:], sc_d[(kg * 2 + fg) * 128:(kg * 2 + fg + 1) * 128, :],
                           waits=[(pe, wbuf_ms[kg % 2])], par=kg % 2)
                for fsub in range(4):
                    bw = (bank_wait(banks[0]) + bank_wait(banks[1]) + bank_wait(banks[2])
                          + bank_wait(banks[3])) if (kg == 0 and fsub == 0) else []
                    for k in range(8):
                        ic = kg * 8 + k
                        mm(banks[fsub][:, 0:512],
                           buf[:, k * 512 + fsub * 128: k * 512 + fsub * 128 + 128],
                           m_fm[:, ic * 512:(ic + 1) * 512],
                           start=(kg == 0 and k == 0), stop=(kg == 3 and k == 7),
                           waits=([dk, rms[kg]] + bw)
                           if (fsub == 0 and k == 0) else
                           (bw if k == 0 else ()))
                wbuf_ms[kg % 2] = P.inc_last("tensor", pe)
            p1 = pe.n
            for fsub in range(4):
                f = fg * 4 + fsub
                v_op(lambda v, fsub=fsub, banks=banks: v.tensor_mul(
                    tq[:], banks[fsub][:, 0:512], sibc[:]),
                    waits=[(pe, p1)] if fsub == 0 else [])
                vr = v_op(lambda v, f=f, xtile=xtile: v.scalar_tensor_tensor(
                    xtile[:, f * TILE:(f + 1) * TILE], tq[:], gbcol[:, f:f + 1],
                    xtile[:, f * TILE:(f + 1) * TILE], ALU.mult, ALU.add))
                bank_free[bank_key(banks[fsub])] = (ve, vr)
            P.wait("tensor", ve, ve.n)
        vfin = ve.n
        s_dma(outT[:, tile * 4096:(tile + 1) * 4096], xtile[:],
              waits=[(ve, vfin)], inc=so, selfwait=True)

    pend, rate = tern_gen(), 4
    for tile in range(NT):
        g = tile_gen(tile)
        while True:
            try:
                mark = next(g)
            except StopIteration:
                break
            if mark == "q1barrier":
                if pend is not None:
                    for _ in pend:
                        pass
                    pend = None
                continue
            if mark == "ffnstart":
                break
            if pend is not None:
                for _ in range(rate):
                    if next(pend, "END") == "END":
                        pend = None
                        break
        pend, rate = g, 1
    if pend is not None:
        for _ in pend:
            pass

    P.wait("sync", so, so.n)

    with nc.Block() as block:
        @block.sync
        def _(e):
            P.replay("sync", e)

        @block.tensor
        def _(e):
            P.replay("tensor", e)

        @block.vector
        def _(e):
            P.replay("vector", e)

        @block.scalar
        def _(e):
            P.replay("scalar", e)

        @block.gpsimd
        def _(e):
            P.replay("gpsimd", e)

    stack.close()
    return nc


def host_consts():
    bf = ml_dtypes.bfloat16
    iota = np.tile(np.arange(1, C + 1, dtype=np.float32), H)[None, :]
    i64 = np.eye(64, dtype=np.float32)
    ipacked = np.zeros((128, 512), np.float32)
    trilS = np.zeros((128, 512), np.float32)
    triuS = np.zeros((128, 512), np.float32)
    triuI = np.zeros((128, 512), np.float32)
    tl = np.tril(np.ones((64, 64), np.float32), -1)
    ti = np.tril(np.ones((64, 64), np.float32))
    for half in range(2):
        for h in range(H):
            slr = slice(half * 64, half * 64 + 64)
            slc = slice(h * 64, (h + 1) * 64)
            ipacked[slr, slc] = i64
            trilS[slr, slc] = tl
            triuS[slr, slc] = tl.T
            triuI[slr, slc] = ti.T
    return {
        "iota_rep": iota,
        "ones_row_f": np.ones((1, 128), np.float32),
        "ones_col_b": np.ones((128, 1), bf),
        "ident128": np.eye(128, dtype=bf),
        "identcol": np.vstack([np.eye(64), np.eye(64)]).astype(bf),
        "ipacked": ipacked.astype(bf),
        "trilS": trilS.astype(bf),
        "triuS": triuS.astype(bf),
        "triuI": triuI.astype(bf),
        "e16rep": np.eye(16, dtype=bf),
        "e8rep": np.eye(8, dtype=bf),
    }


def pack_w(w):
    return np.ascontiguousarray(w.reshape(8, 128, D).transpose(1, 0, 2).reshape(128, 8 * D))


def pack_x(xt):
    return np.ascontiguousarray(
        xt.reshape(8, 128, 8, 512).transpose(1, 2, 0, 3).reshape(128, 8 * 4096))


def unpack_out(o):
    return np.ascontiguousarray(
        o.reshape(128, 8, 8, 512).transpose(2, 0, 1, 3).reshape(D, T).T)


def kernel(L0_attn_out, L0_attn_W_write, L0_attn_W_read, L0_attn_W_beta, L0_attn_decay,
           L0_ffn_w_gate, L0_ffn_w_up, L0_ffn_w_down, L0_c, L0_wmod):
    shared = host_consts()
    shared.update({
        "wwT": pack_w(np.asarray(L0_attn_W_write).T),
        "wrT": pack_w(np.asarray(L0_attn_W_read).T),
        "wbT": np.ascontiguousarray(np.asarray(L0_attn_W_beta).T),
        "dec": np.ascontiguousarray(np.asarray(L0_attn_decay).reshape(1, H)),
        "wmod": np.ascontiguousarray(np.asarray(L0_wmod)),
        "wgT": np.ascontiguousarray(np.asarray(L0_ffn_w_gate).T),
        "wuT": np.ascontiguousarray(np.asarray(L0_ffn_w_up).T),
        "wdT": np.ascontiguousarray(np.asarray(L0_ffn_w_down).T),
    })
    x = np.asarray(L0_attn_out)
    cc = np.asarray(L0_c)
    in_maps = []
    for b in range(8):
        m = dict(shared)
        m["xT"] = pack_x(np.asarray(x[b]).T)
        m["c_row"] = np.ascontiguousarray(cc[b].reshape(1, D))
        in_maps.append(m)
    nc = build_nc()
    res = run_bass_kernel_spmd(nc, in_maps, core_ids=list(range(8)))
    out = np.stack([unpack_out(np.asarray(r["outT"])) for r in res.results], axis=0)
    return out.astype(np.float32)


# version 19
# speedup vs baseline: 1.0099x; 1.0099x over previous
import numpy as np
import ml_dtypes
import concourse.bass as bass
import concourse.mybir as mybir
from concourse.bass_utils import run_bass_kernel_spmd

F32 = mybir.dt.float32
BF16 = mybir.dt.bfloat16
AF = mybir.ActivationFunctionType
ALU = mybir.AluOpType
AX = mybir.AxisListType

T = 4096
D = 1024
H = 8
C = 64
TILE = 512
NT = T // TILE
CHT = TILE // C
INNER = 4096
MAGIC = 12582912.0


class Sem:
    def __init__(self, h):
        self.h = h
        self.n = 0


class Prog:
    ENGINES = ("sync", "tensor", "vector", "scalar", "gpsimd")

    def __init__(self):
        self.ops = {e: [] for e in self.ENGINES}
        self.self_sem = {}

    def emit(self, eng, fn, waits=(), inc=None, inc_k=1):
        waits = list(waits)
        ss = self.self_sem.get(eng)
        if ss is not None and ss.n > 0 and inc is ss:
            waits.append((ss, ss.n))
        wl = [(s.h, v) for (s, v) in waits]
        tgt = None
        if inc is not None:
            inc.n += inc_k
            tgt = (inc.h, inc_k)
        self.ops[eng].append([fn, wl, tgt])

    def inc_last(self, eng, sem, k=1):
        rec = self.ops[eng][-1]
        assert isinstance(rec, list) and rec[2] is None
        sem.n += k
        rec[2] = (sem.h, k)
        return sem.n

    def wait(self, eng, sem, val=None):
        v = sem.n if val is None else val
        h = sem.h
        self.ops[eng].append(lambda e: e.wait_ge(h, v))

    def replay(self, eng, e):
        for rec in self.ops[eng]:
            if isinstance(rec, list):
                fn, wl, tgt = rec
                for h, v in wl:
                    e.wait_ge(h, v)
                r = fn(e)
                if tgt is not None:
                    r.then_inc(tgt[0], tgt[1])
            else:
                rec(e)


def build_nc():
    nc = bass.Bass("TRN2", target_bir_lowering=False)

    def din(name, shape, dt=F32):
        return nc.dram_tensor(name, shape, dt, kind="ExternalInput").ap()

    xT = din("xT", [128, 8 * 4096])
    wwT = din("wwT", [128, 8 * D])
    wrT = din("wrT", [128, 8 * D])
    wbT = din("wbT", [D, H])
    dec = din("dec", [1, H])
    c_row = din("c_row", [1, D])
    wmod = din("wmod", [D, 6 * D])
    wgT = din("wgT", [D, INNER])
    wuT = din("wuT", [D, INNER])
    wdT = din("wdT", [INNER, D])
    iota_rep = din("iota_rep", [1, 512])
    ones_row_f = din("ones_row_f", [1, 128])
    ones_col_b = din("ones_col_b", [128, 1], BF16)
    ident128 = din("ident128", [128, 128], BF16)
    identcol = din("identcol", [128, 64], BF16)
    ipacked = din("ipacked", [128, 512], BF16)
    trilS = din("trilS", [128, 512], BF16)
    triuS = din("triuS", [128, 512], BF16)
    triuI = din("triuI", [128, 512], BF16)
    e16rep = din("e16rep", [16, 16], BF16)
    e8rep = din("e8rep", [8, 8], BF16)

    outT = nc.dram_tensor("outT", [128, 8 * 4096], F32, kind="ExternalOutput").ap()
    sc_g = nc.dram_tensor("sc_g", [D, INNER], BF16, kind="Internal").ap()
    sc_u = nc.dram_tensor("sc_u", [D, INNER], BF16, kind="Internal").ap()
    sc_d = nc.dram_tensor("sc_d", [D, INNER], BF16, kind="Internal").ap()

    import contextlib
    stack = contextlib.ExitStack()

    def sb(name, shape, dt=BF16):
        return stack.enter_context(nc.sbuf_tensor(name, shape, dt))

    def ps(name, shape, dt=F32):
        return stack.enter_context(nc.psum_tensor(name, shape, dt))

    xt = [sb("xt0", [128, 8 * TILE], F32), sb("xt1", [128, 8 * TILE], F32)]
    h_fm = sb("h_fm", [128, 8 * TILE])
    kdw_tm = sb("kdw_tm", [128, 4 * D])
    vtm = sb("vtm", [128, 4 * D])
    o_fm = sb("o_fm", [128, H * TILE])
    m_fm = sb("m_fm", [128, 32 * TILE])
    kpair_t = sb("kpair", [128, H * CHT * 128])
    kpair = kpair_t[:]
    kp32 = kpair_t[:].bitcast(F32)
    xq = sb("xq", [128, 8 * TILE])
    xq32 = xq[:].bitcast(F32)
    Xs = sb("Xs", [128, 512])
    Ys = sb("Ys", [128, 512])
    Ps = sb("Ps", [128, 512])
    at_all = sb("at_all", [128, 4 * 512])
    att_all = sb("att_all", [128, 4 * 512])
    w0tm = sb("w0tm", [128, H * 128])
    vnewtm = sb("vnewtm", [128, H * 128])
    kde_s = sb("kde_s", [128, 2 * H * C])
    kbdn_s = sb("kbdn_s", [128, 2 * H * C])
    Sb = sb("Sb", [128, 2 * H * 128])
    S32 = sb("S32", [128, H * 128], F32)
    wsh = sb("wsh", [128, 8 * D])
    wbT_sb = sb("wbT_sb", [128, 8 * H])
    wbuf = [sb("wbuf0", [128, 8 * 512]), sb("wbuf1", [128, 8 * 512])]
    gu = sb("gu", [128, 8 * 512])
    maskA = sb("maskA", [128, 512])
    maskAT = sb("maskAT", [128, 512])
    maskLTI = sb("maskLTI", [128, 512])
    de_pat = sb("de_pat", [128, H * C])
    ipk = sb("ipk", [128, 512])
    idc = sb("idc", [128, 64])
    id128 = sb("id128", [128, 128])
    idf32 = sb("idf32", [128, 128], F32)
    ocol = sb("ocol", [128, 1])
    orow_f = sb("orow_f", [1, 128], F32)
    e16 = sb("e16", [16, 16])
    e8 = sb("e8", [8, 8])
    modcols = sb("modcols", [128, 64], F32)
    gacol = sb("gacol", [128, 8], F32)
    gbcol = sb("gbcol", [128, 8], F32)
    capcol = sb("capcol", [128, 8], F32)
    cbpcol = sb("cbpcol", [128, 8], F32)
    sccol_b = sb("sccol_b", [128, 8])
    rows = xq[:].bitcast(F32)[0:1, 0:1024]
    ctdcol = sb("ctdcol", [128, H], F32)
    dwcol = sb("dwcol", [128, H], F32)
    betacol = sb("betacol", [128, 32], F32)
    f1col = sb("f1col", [128, 32], F32)
    f2col = sb("f2col", [128, 32], F32)
    kdwfcol = sb("kdwfcol", [128, 32], F32)
    fac2 = sb("fac2", [128, 64], F32)
    Mfac = sb("Mfac", [16, 512])
    Mq = sb("Mq", [8, 128])
    tq = sb("tq", [128, 512])
    tq2 = sb("tq2", [128, 512])
    tq3 = sb("tq3", [128, 512])
    sbc = sb("sbc", [128, 512])
    sibc = sb("sibc", [128, 512])
    amaxc = sb("amaxc", [128, 8], F32)
    aucol = sb("aucol", [128, 1], F32)
    adcol = sb("adcol", [128, 1], F32)
    agcol = sb("agcol", [128, 1], F32)
    thrg = sb("thrg", [128, 1], F32)
    thru = sb("thru", [128, 1], F32)
    thrd = sb("thrd", [128, 1], F32)
    acc_abs = sb("acc_abs", [128, 4], F32)
    stat = sb("stat", [1, 1536], F32)
    epsc = sb("epsc", [1, 1], F32)
    onef = sb("onef", [128, 1], F32)
    magc = sb("magc", [128, 1], F32)
    nmagc = sb("nmagc", [128, 1], F32)
    negt = sb("negt", [128, 1], F32)
    cumrow = stat[0:1, 0:512]
    derow = stat[0:1, 512:1024]
    dwrow = stat[0:1, 1024:1536]

    mf32 = m_fm[:].bitcast(F32)

    pA = ps("pA", [128, 512])
    pB = ps("pB", [128, 1024])
    pC = ps("pC", [128, 512])
    pD = ps("pD", [128, 512])
    pE = ps("pE", [128, 512])
    pF = ps("pF", [128, 512])
    psT = ps("psT", [128, 1024], BF16)
    psT32 = psT[:].bitcast(F32)

    dx = Sem(stack.enter_context(nc.semaphore("dx")))
    dxt = Sem(stack.enter_context(nc.semaphore("dxt")))
    dwmP = [Sem(stack.enter_context(nc.semaphore("dwm0"))),
            Sem(stack.enter_context(nc.semaphore("dwm1")))]
    dstP = [Sem(stack.enter_context(nc.semaphore("dst0"))),
            Sem(stack.enter_context(nc.semaphore("dst1")))]
    pe = Sem(stack.enter_context(nc.semaphore("pe")))
    ve = Sem(stack.enter_context(nc.semaphore("ve")))
    ac = Sem(stack.enter_context(nc.semaphore("ac")))
    po = Sem(stack.enter_context(nc.semaphore("po")))
    so = Sem(stack.enter_context(nc.semaphore("so")))

    P = Prog()
    P.self_sem = {"vector": ve, "scalar": ac, "gpsimd": po}

    def mm(out, lhsT, rhs, start=True, stop=True, tp=None, waits=(), inc=None):
        P.emit("tensor", lambda t: t.matmul(out, lhsT, rhs, start=start, stop=stop,
                                            tile_position=tp), waits=waits, inc=inc)

    def penop(waits=()):
        assert not waits
        return P.inc_last("tensor", pe)

    def v_op(fn, waits=()):
        P.emit("vector", fn, waits=waits, inc=ve)
        return ve.n

    def a_op(fn, waits=()):
        P.emit("scalar", fn, waits=waits, inc=ac)
        return ac.n

    def g_op(fn, waits=()):
        P.emit("gpsimd", fn, waits=waits, inc=po)
        return po.n

    def s_dma(out, in_, waits=(), inc=None, selfwait=False):
        inc = inc or dx
        waits = list(waits)
        if selfwait and inc.n > 0:
            waits.append((inc, inc.n))
        P.emit("sync", lambda s: s.dma_start(out=out, in_=in_), waits=waits, inc=inc, inc_k=16)
        return (inc, inc.n)

    def g_dma(out, in_, waits=(), par=0, inc=None):
        inc = inc if inc is not None else dwmP[par]
        waits = list(waits)
        if inc.n > 0:
            waits.append((inc, inc.n))
        P.emit("gpsimd", lambda g: g.dma_start(out=out, in_=in_), waits=waits, inc=inc, inc_k=16)
        return (inc, inc.n)

    bank_free = {}
    _bkey = {}

    def bank_key(bank):
        return _bkey.get(id(bank), id(bank))

    def bank_wait(bank):
        k = bank_key(bank)
        return [bank_free[k]] if k in bank_free else []

    s_dma(ocol[:], ones_col_b[:])
    s_dma(orow_f[:], ones_row_f[:])
    s_dma(id128[:], ident128[:])
    s_dma(idc[:], identcol[:])
    s_dma(ipk[:], ipacked[:])
    s_dma(e16[:], e16rep[:])
    s_dma(e8[:], e8rep[:])
    s_dma(rows[0:1, 0:512], iota_rep[:])
    s_dma(rows[0:1, 512:520], dec[:])
    s_dma(maskA[:], trilS[:])
    s_dma(maskAT[:], triuS[:])
    s_dma(maskLTI[:], triuI[:])

    s_dma(xt[0][:], xT[:, 0:4096], inc=dxt, selfwait=True)
    dma0 = (dx, dx.n)
    dwA = g_dma(wbT_sb[:].rearrange("p (c h) -> p c h", c=8),
                wbT.rearrange("(c p) h -> p c h", p=128), par=0)

    v_op(lambda v: v.tensor_copy(idf32[:], id128[:]), waits=[dma0])
    v_op(lambda v: v.memset(epsc[:], 1e-5))
    v_op(lambda v: v.memset(onef[:], 1.0))
    v_op(lambda v: v.memset(magc[:], MAGIC))
    v_op(lambda v: v.memset(nmagc[:], -MAGIC))
    a1 = a_op(lambda s: s.activation(rows[0:1, 520:528], rows[0:1, 512:520], AF.Sigmoid),
              waits=[dma0])
    a1 = a_op(lambda s: s.activation(rows[0:1, 528:536], rows[0:1, 520:528], AF.Ln))
    v1 = v_op(lambda v: v.tensor_copy(
        cumrow.rearrange("r (h q) -> r h q", h=8),
        rows[0:1, 528:536].unsqueeze(2).broadcast_to([1, 8, 64])),
        waits=[(ac, a1)])
    v1 = v_op(lambda v: v.tensor_mul(cumrow, cumrow, rows[0:1, 0:512]))
    aexp = a_op(lambda s: s.activation(derow, cumrow, AF.Exp), waits=[(ve, v1)])
    v1 = v_op(lambda v: v.tensor_copy(
        rows[0:1, 0:512].rearrange("r (h q) -> r h q", h=8),
        cumrow[0:1, 63::64].unsqueeze(2).broadcast_to([1, 8, 64])))
    v1 = v_op(lambda v: v.tensor_sub(rows[0:1, 512:1024], rows[0:1, 0:512], cumrow))
    adw = a_op(lambda s: s.activation(dwrow, rows[0:1, 512:1024], AF.Exp), waits=[(ve, v1)])
    vneg = v_op(lambda v: v.tensor_scalar_mul(rows[0:1, 0:512], cumrow, -1.0))
    vneg = max(vneg, v1)

    def build_mask(first_rhs_is_ones, dsts):
        for half in range(2):
            for h in range(H):
                o = pE[half * 64:half * 64 + 64, h * 64:(h + 1) * 64]
                w0 = [(ve, vneg), (ac, adw)] if (half == 0 and h == 0) else ()
                if first_rhs_is_ones:
                    mm(o, cumrow[0:1, h * 64:(h + 1) * 64], orow_f[0:1, 0:64],
                       start=True, stop=False, tp=(0, half * 64), waits=w0)
                    mm(o, orow_f[0:1, 0:64], rows[0:1, h * 64:(h + 1) * 64],
                       start=False, stop=True, tp=(0, half * 64))
                else:
                    mm(o, rows[0:1, h * 64:(h + 1) * 64], orow_f[0:1, 0:64],
                       start=True, stop=False, tp=(0, half * 64), waits=w0)
                    mm(o, orow_f[0:1, 0:64], cumrow[0:1, h * 64:(h + 1) * 64],
                       start=False, stop=True, tp=(0, half * 64))
        p1 = penop()
        vm = v_op(lambda v: v.tensor_scalar_min(pE[:, 0:512], pE[:, 0:512], 0.0),
                  waits=[(pe, p1)])
        am = a_op(lambda s: s.activation(pF[:, 0:512], pE[:, 0:512], AF.Exp), waits=[(ve, vm)])
        vlast = 0
        for dstt in dsts:
            vlast = v_op(lambda v, dstt=dstt: v.tensor_mul(dstt[:], pF[:, 0:512], dstt[:]),
                         waits=[(ac, am)])
        return vlast

    vmA = build_mask(True, [maskA])
    P.wait("tensor", ve, vmA)
    vmT = build_mask(False, [maskAT, maskLTI])
    P.wait("tensor", ve, vmT)

    for h in range(H):
        mm(pD[:, h * 64:(h + 1) * 64], orow_f[:], derow[0:1, h * 64:(h + 1) * 64],
           waits=[(ac, aexp)] if h == 0 else ())
    p1 = penop()
    vdep = v_op(lambda v: v.tensor_copy(de_pat[:], pD[:, 0:512]), waits=[(pe, p1)])
    vctd = v_op(lambda v: v.tensor_copy(rows[0:1, 512:520], derow[0:1, 63::64]))
    mm(pD[:, 64:72], orow_f[:], rows[0:1, 512:520], waits=[(ve, vctd), (ve, vdep)])
    p1 = penop()
    vctd = v_op(lambda v: v.tensor_copy(ctdcol[:], pD[:, 64:72]), waits=[(pe, p1)])
    vr2 = 0
    for h in range(H):
        vr2 = v_op(lambda v, h=h: v.tensor_copy(
            rows[0:1, h * 128:(h + 1) * 128].rearrange("r (p q) -> r p q", p=2),
            dwrow[0:1, h * 64:(h + 1) * 64].unsqueeze(1).broadcast_to([1, 2, 64])),
            waits=[(ac, adw)] if h == 0 else ())
    for h in range(H):
        P.emit("tensor", lambda t, h=h: t.transpose(
            pD[:, 88 + h:89 + h], rows[0:1, h * 128:(h + 1) * 128], idf32[0:1, 0:1]),
            waits=[(ve, vr2), (ve, vctd)] if h == 0 else (), inc=pe)
    p1 = pe.n
    vdwc = v_op(lambda v: v.tensor_copy(dwcol[:], pD[:, 88:96]), waits=[(pe, p1)])

    dcr = s_dma(rows[0:1, 0:1024], c_row[:], waits=[(pe, pe.n)], selfwait=True)
    for k in range(8):
        P.emit("tensor", lambda t, k=k: t.transpose(
            pD[:, 72 + k:73 + k], rows[0:1, k * 128:(k + 1) * 128], idf32[0:1, 0:1]),
            waits=[dcr, (ve, vdwc)] if k == 0 else (), inc=pe)
    p1 = pe.n
    vcc = v_op(lambda v: v.tensor_copy(modcols[:, 40:48], pD[:, 72:80]), waits=[(pe, p1)])
    asc = a_op(lambda s: s.activation(sccol_b[:], modcols[:, 40:48], AF.Silu),
               waits=[(ve, vcc)])
    pk_prev = 0
    for k in range(8):
        piece = m_fm[:, (k % 2) * 6144:(k % 2) * 6144 + 6144]
        wts = [dwA] if k == 0 else [(pe, pk_prev)]
        dk = g_dma(piece, wmod[k * 128:(k + 1) * 128, :], waits=wts, par=k % 2)
        for mc in range(48):
            mm(pF[:, mc * 8 + k: mc * 8 + k + 1],
               piece[:, mc * 128:(mc + 1) * 128], sccol_b[:, k:k + 1],
               waits=[dk, (ac, asc)] if mc == 0 else ())
        pk_prev = P.inc_last("tensor", pe)
    p1 = pk_prev
    vmod = v_op(lambda v: v.tensor_reduce(
        modcols[:, 0:48], pF[:, 0:384].rearrange("p (m k) -> p m k", k=8), AX.X, ALU.add),
        waits=[(pe, p1)])
    v_op(lambda v: v.tensor_scalar_add(capcol[:], modcols[:, 8:16], 1.0))
    v_op(lambda v: v.tensor_scalar_add(cbpcol[:], modcols[:, 32:40], 1.0))
    for dstc, off in ((gacol, 16), (gbcol, 40)):
        v_op(lambda v, off=off: v.tensor_scalar(fac2[:, 0:8], modcols[:, off:off + 8],
                                                0.0, 2.0, ALU.is_ge, ALU.mult))
        vg = v_op(lambda v: v.tensor_scalar_sub(fac2[:, 0:8], fac2[:, 0:8], 1.0))
        ag = a_op(lambda s, off=off: s.activation(fac2[:, 8:16], modcols[:, off:off + 8], AF.Abs),
                  waits=[(ve, vg)])
        v_op(lambda v: v.tensor_scalar_max(fac2[:, 8:16], fac2[:, 8:16], 0.25),
             waits=[(ac, ag)])
        v_op(lambda v, dstc=dstc: v.tensor_mul(dstc[:], fac2[:, 0:8], fac2[:, 8:16]))
    mod_ready = ve.n
    P.wait("tensor", ve, mod_ready)


    def tern_matrix(src, scr, thr_col, acol, store):
        R, CC = src.shape
        npieces = R // 128
        vlast = ve.n
        entry_w = [(ve, ve.n), (po, po.n), (pe, pe.n)]
        for p in range(npieces):
            buf = mf32[:, (p % 2) * CC:(p % 2) * CC + CC]
            dk = g_dma(buf, src[p * 128:(p + 1) * 128, :],
                       waits=[(ve, vlast)] if p >= 2 else entry_w, par=p % 2)
            vlast = v_op(lambda v, buf=buf, p=p: v.tensor_reduce(
                acc_abs[:, 1 + p % 2:2 + p % 2], buf, AX.X, ALU.add,
                apply_absolute_value=True), waits=[dk])
            if p == 0:
                vlast = v_op(lambda v: v.tensor_copy(acc_abs[:, 0:1], acc_abs[:, 1:2]))
            else:
                vlast = v_op(lambda v, p=p: v.tensor_add(
                    acc_abs[:, 0:1], acc_abs[:, 0:1], acc_abs[:, 1 + p % 2:2 + p % 2]))
            yield
        mm(pF[0:1, 0:1], acc_abs[:, 0:1], onef[:],
           waits=[(ve, vlast)] + bank_wait(pF))
        p1 = penop()
        vthr = v_op(lambda v: v.tensor_scalar(stat[0:1, 0:1], pF[0:1, 0:1],
                                              1.0 / (R * CC), 1e-5, ALU.mult, ALU.max),
                    waits=[(pe, p1)])
        mm(pF[:, 4:5], orow_f[:], stat[0:1, 0:1], waits=[(ve, vthr)])
        p1 = penop()
        v_op(lambda v: v.tensor_copy(acol[:], pF[:, 4:5]), waits=[(pe, p1)])
        v_op(lambda v: v.tensor_scalar_mul(thr_col[:], pF[:, 4:5], 0.5))
        vthr = v_op(lambda v: v.tensor_scalar_mul(negt[:], thr_col[:], -1.0))
        bank_free[bank_key(pF)] = (ve, vthr)
        plast = po.n
        entry_pe = pe.n
        for p in range(npieces):
            buf = mf32[:, (p % 2) * CC:(p % 2) * CC + CC]
            wts = [(ve, vthr), (pe, entry_pe)] if p < 2 else [(ve, plast)]
            dk = g_dma(buf, src[p * 128:(p + 1) * 128, :], waits=wts, par=p % 2)
            v_op(lambda v, buf=buf: v.tensor_single_scalar(gu[:, 0:CC], buf,
                                                           thr_col[:], ALU.is_gt),
                 waits=[dk, (dstP[0], dstP[0].n), (dstP[1], dstP[1].n)])
            v_op(lambda v, buf=buf: v.tensor_single_scalar(xq[:, 0:CC], buf,
                                                           negt[:], ALU.is_lt))
            plast = v_op(lambda v, CC=CC: v.tensor_sub(gu[:, 0:CC], gu[:, 0:CC],
                                                       xq[:, 0:CC]))
            store(scr, p, [(ve, plast)], dstP[p % 2])
            yield
        return

    def store_gu(scr, p, waits, inc):
        out = scr.rearrange("(og q) (k o) -> q og k o", og=8, k=8)[:, :, p, :]
        in_ = gu[:, 0:INNER].rearrange("q (og o) -> q og o", og=8)
        g_dma(out, in_, waits=waits, inc=inc)

    def store_d(scr, p, waits, inc):
        kg, k = p // 8, p % 8
        out = scr.rearrange("(b q) (k o) -> q b k o", b=8, k=8)[:, kg * 2:kg * 2 + 2, k, :]
        in_ = gu[:, 0:D].rearrange("q (b o) -> q b o", b=2)
        g_dma(out, in_, waits=waits, inc=inc)

    def tern_gen():
        for a in tern_matrix(wgT, sc_g, thrg, agcol, store_gu):
            yield
        for a in tern_matrix(wuT, sc_u, thru, aucol, store_gu):
            yield
        for a in tern_matrix(wdT, sc_d, thrd, adcol, store_d):
            yield
    tern_ve = ve.n

    v_op(lambda v: v.memset(S32[:], 0.0))
    init_ve = v_op(lambda v: v.memset(Sb[:], 0.0))

    dwWW = g_dma(wsh[:], wwT[:], par=0)

    wbuf_ms = [0, 0]

    ffn_ms = {}

    def tile_gen(tile):
        xtile = xt[tile % 2]
        t0 = tile * TILE
        P.wait("vector", dxt, 16 * (tile + 1))
        P.wait("scalar", dxt, 16 * (tile + 1))
        P.wait("tensor", dxt, 16 * (tile + 1))
        if tile == 0:
            P.wait("vector", *dma0)
            P.wait("scalar", *dma0)
            P.wait("gpsimd", ve, init_ve)

        def ln_pass(capc, shoff, vpre, apre, xtile=None, exw=(), xdone=None, fold=None):
            sqdone = []
            for c in range(8):
                sl = slice(c * TILE, (c + 1) * TILE)
                if c % 2 == 0:
                    a_op(lambda s, sl=sl, xtile=xtile: s.activation(kpair[:, sl], xtile[:, sl],
                                                                    AF.Square),
                         waits=(([(ac, apre)] + list(exw)) if c == 0 else [])
                         + ([xdone[c]] if xdone else []))
                    sqdone.append((ac, ac.n))
                else:
                    v_op(lambda v, sl=sl, xtile=xtile: v.tensor_mul(kpair[:, sl], xtile[:, sl],
                                                                    xtile[:, sl]),
                         waits=([(ve, vpre)] + list(exw)) if c == 1 else ())
                    sqdone.append((ve, ve.n))
            for c in range(8):
                mm(pE[0:1, 0:512], onef[:], xtile[:, c * TILE:(c + 1) * TILE],
                   start=(c == 0), stop=(c == 7),
                   waits=(bank_wait(pE) if c == 0 else [])
                   + ([xdone[c]] if xdone else []))
            for c in range(8):
                mm(pF[0:1, 0:512], ocol[:], kpair[:, c * TILE:(c + 1) * TILE],
                   start=(c == 0), stop=(c == 7),
                   waits=([sqdone[c]] + (bank_wait(pF) if c == 0 else [])))
            p1 = penop()
            v_op(lambda v: v.tensor_scalar_mul(stat[0:1, 0:512], pE[0:1, 0:512], 1.0 / D),
                 waits=[(pe, p1)])
            v_op(lambda v: v.tensor_scalar_mul(stat[0:1, 512:1024], pF[0:1, 0:512], 1.0 / D))
            v_op(lambda v: v.tensor_mul(stat[0:1, 1024:1536], stat[0:1, 0:512], stat[0:1, 0:512]))
            vv = v_op(lambda v: v.tensor_sub(stat[0:1, 512:1024], stat[0:1, 512:1024],
                                             stat[0:1, 1024:1536]))
            aa = a_op(lambda s: s.activation(stat[0:1, 1024:1536], stat[0:1, 512:1024],
                                             AF.Sqrt, bias=epsc[0:1, 0:1]), waits=[(ve, vv)])
            v_op(lambda v: v.reciprocal(stat[0:1, 512:1024], stat[0:1, 1024:1536]),
                 waits=[(ac, aa)])
            vv = v_op(lambda v: v.tensor_mul(stat[0:1, 1024:1536], stat[0:1, 0:512],
                                             stat[0:1, 512:1024]))
            mm(pE[:, 0:512], orow_f[:], stat[0:1, 512:1024], waits=[(ve, vv)])
            mm(pF[:, 0:512], orow_f[:], stat[0:1, 1024:1536])
            p1 = penop()
            alast = 0
            aprev = 0
            rbc = pE[:, 0:512].unsqueeze(1).broadcast_to([128, 2, 512])
            cbc = pF[:, 0:512].unsqueeze(1).broadcast_to([128, 2, 512])
            for cp in range(4):
                sl = slice(cp * 1024, (cp + 1) * 1024)
                aw = ([(pe, p1)] if cp == 0 else []) + ([(ac, aprev)] if aprev else [])
                v_op(lambda v, sl=sl, xtile=xtile: v.tensor_mul(
                    kp32[:, 0:1024].rearrange("p (a q) -> p a q", a=2),
                    xtile[:, sl].rearrange("p (a q) -> p a q", a=2), rbc), waits=aw)
                vap = v_op(lambda v: v.tensor_sub(
                    kp32[:, 0:1024].rearrange("p (a q) -> p a q", a=2),
                    kp32[:, 0:1024].rearrange("p (a q) -> p a q", a=2), cbc))
                for c in (2 * cp, 2 * cp + 1):
                    sl1 = slice(c * TILE, (c + 1) * TILE)
                    ssl = slice((c % 2) * TILE, (c % 2) * TILE + TILE)
                    alast = a_op(lambda s, sl1=sl1, ssl=ssl, c=c: s.activation(
                        h_fm[:, sl1], kp32[:, ssl], AF.Identity,
                        bias=modcols[:, shoff + c:shoff + c + 1],
                        scale=capc[:, c:c + 1]), waits=[(ve, vap)])
                aprev = alast
                if fold is not None:
                    fold(cp, alast)
            bank_free[bank_key(pE)] = (ve, ve.n)
            bank_free[bank_key(pF)] = (ve, ve.n)
            P.wait("tensor", ve, ve.n)
            P.wait("tensor", ac, alast)
            return alast

        yield
        if tile == 0:
            P.wait("vector", ve, mod_ready)
            h_ready = ln_pass(capcol, 0, tern_ve, ac.n, xtile=xtile)
        else:
            h_ready = ln_pass(capcol, 0, ve.n, ac.n, xtile=xtile,
                              exw=[(pe, pe.n), (po, po.n)])
        yield

        for tc in range(4):
            for k in range(8):
                mm(pE[:, tc * 8:(tc + 1) * 8],
                   h_fm[:, k * TILE + tc * 128: k * TILE + tc * 128 + 128],
                   wbT_sb[:, k * 8:(k + 1) * 8], start=(k == 0), stop=(k == 7),
                   waits=([(ac, h_ready), dwA] + bank_wait(pE))
                   if (tc == 0 and k == 0) else ())
        p1 = penop()
        ab = a_op(lambda s: s.activation(betacol[:], pE[:, 0:32], AF.Sigmoid),
                  waits=[(pe, p1)])
        bank_free[bank_key(pE)] = (ac, ab)

        hsqdone = []
        for c in range(8):
            sl = slice(c * TILE, (c + 1) * TILE)
            if c % 2 == 0:
                a_op(lambda s, sl=sl: s.activation(kpair[:, sl], h_fm[:, sl], AF.Square))
                hsqdone.append((ac, ac.n))
            else:
                v_op(lambda v, sl=sl: v.tensor_mul(kpair[:, sl], h_fm[:, sl], h_fm[:, sl]),
                     waits=[(ac, h_ready)] if c == 1 else [])
                hsqdone.append((ve, ve.n))
        for h in range(H):
            for tc in range(4):
                mm(pF[:, tc * 8 + h: tc * 8 + h + 1],
                   kpair[:, h * TILE + tc * 128: h * TILE + tc * 128 + 128], ocol[:],
                   waits=([hsqdone[h]] + (bank_wait(pF) if h == 0 else [])) if tc == 0 else ())
        p1 = penop()
        aa = a_op(lambda s: s.activation(f1col[:], pF[:, 0:32], AF.Sqrt),
                  waits=[(pe, p1)])
        v_op(lambda v: v.tensor_scalar_max(f1col[:], f1col[:], 1e-12),
             waits=[(ac, aa)])
        f1v = fac2[:, 0:64].rearrange("p (t f h) -> p t f h", t=4, f=2)[:, :, 0, :]
        f2v = fac2[:, 0:64].rearrange("p (t f h) -> p t f h", t=4, f=2)[:, :, 1, :]
        v_op(lambda v: v.reciprocal(f1v, f1col[:].rearrange("p (t h) -> p t h", t=4)))
        v_op(lambda v: v.scalar_tensor_tensor(f2v, betacol[:].rearrange("p (t h) -> p t h", t=4),
                                              -1.0, f1v, ALU.mult, ALU.mult))
        vfac = v_op(lambda v: v.tensor_mul(
            kdwfcol[:].rearrange("p (t h) -> p t h", t=4), f1v,
            dwcol[:].unsqueeze(1).broadcast_to([128, 4, 8])))
        bank_free[bank_key(pF)] = (ve, vfac)
        yield

        for tc in range(4):
            for k in range(8):
                lh = h_fm[:, k * TILE + tc * 128: k * TILE + tc * 128 + 128]
                for half in range(2):
                    mm(pB[:, half * 512:(half + 1) * 512], lh,
                       wsh[:, k * D + half * 512: k * D + (half + 1) * 512],
                       start=(k == 0), stop=(k == 7),
                       waits=([dwWW] + bank_wait(pB))
                       if (tc == 0 and k == 0 and half == 0) else ())
            p1 = penop()
            for h in range(H):
                v_op(lambda v, tc=tc, h=h: v.tensor_single_scalar(
                    vtm[:, tc * D + h * 128: tc * D + (h + 1) * 128],
                    pB[:, h * 128:(h + 1) * 128],
                    betacol[:, tc * 8 + h: tc * 8 + h + 1], ALU.mult),
                    waits=[(ac, ab), (pe, p1)] if h == 0 else ())
            bank_free[bank_key(pB)] = (ve, ve.n)
            P.wait("tensor", ve, ve.n)
            if tc == 1:
                for fc in range(4):
                    inp = fac2[:, fc * 16:(fc + 1) * 16]
                    P.emit("tensor", lambda t, inp=inp, fc=fc: t.transpose(
                        pD[0:16, fc * 128:(fc + 1) * 128], inp, idf32[:]),
                        waits=([(ve, vfac)] + bank_wait(pD)) if fc == 0 else (), inc=pe)
                p1t = pe.n
                vM = v_op(lambda v: v.tensor_copy(Mfac[:], pD[0:16, 0:512]),
                          waits=[(pe, p1t)])
                bank_free[bank_key(pD)] = (ve, vM)
            yield
        vvtm = ve.n
        P.wait("tensor", ve, vM)
        yield
        pBa = pB[:, 0:512]
        pBb = pB[:, 512:1024]
        _bkey[id(pBa)] = bank_key(pB)
        _bkey[id(pBb)] = bank_key(pB)
        _bkey[id(psT32)] = bank_key(psT)
        kbanks = [(pE, pF), (pBa, pBb)]
        for h in range(H):
            bE, bF = kbanks[h % 2]
            mm(bE[:, 0:512], e16[:, h:h + 1].broadcast_to([16, 128]), Mfac[:],
               waits=bank_wait(bE))
            mm(bF[:, 0:512], e16[:, 8 + h:9 + h].broadcast_to([16, 128]), Mfac[:],
               waits=bank_wait(bF))
            p1 = penop()
            hsl = h_fm[:, h * TILE:(h + 1) * TILE].rearrange("p (c q) -> p c q", c=8)
            kcell = kpair[:, h * 8 * 128:(h + 1) * 8 * 128].rearrange("p (c q) -> p c q", c=8)
            v_op(lambda v, hsl=hsl, kcell=kcell, bE=bE: v.tensor_mul(
                kcell[:, :, 0:64], hsl, bE[:, 0:512].rearrange("p (c q) -> p c q", c=8)),
                waits=[(pe, p1)])
            vkb = v_op(lambda v, hsl=hsl, kcell=kcell, bF=bF: v.tensor_mul(
                kcell[:, :, 64:128], hsl, bF[:, 0:512].rearrange("p (c q) -> p c q", c=8)))
            bank_free[bank_key(bE)] = (ve, vkb)
            bank_free[bank_key(bF)] = (ve, vkb)
            if h % 2 == 1:
                yield
        P.wait("tensor", ve, vkb)
        vkp = ve.n

        for tc in range(4):
            for h in range(H):
                P.emit("tensor", lambda t, tc=tc, h=h: t.transpose(
                    psT[:, h * 128:(h + 1) * 128],
                    h_fm[:, h * TILE + tc * 128: h * TILE + tc * 128 + 128], id128[:]),
                    waits=([(ac, h_ready)] + bank_wait(psT)) if (tc == 0 and h == 0) else (),
                    inc=pe)
            p1 = pe.n
            for h in range(H):
                v_op(lambda v, tc=tc, h=h: v.tensor_single_scalar(
                    kdw_tm[:, tc * D + h * 128: tc * D + (h + 1) * 128],
                    psT[:, h * 128:(h + 1) * 128],
                    kdwfcol[:, tc * 8 + h: tc * 8 + h + 1], ALU.mult),
                    waits=[(pe, p1)] if h == 0 else ())
            bank_free[bank_key(psT)] = (ve, ve.n)
            P.wait("tensor", ve, ve.n)
            yield

        pBa = pB[:, 0:512]
        pBb = pB[:, 512:1024]
        _bkey[id(pBa)] = bank_key(pB)
        _bkey[id(pBb)] = bank_key(pB)
        _bkey[id(psT32)] = bank_key(psT)

        def a_raws(n, b0, b1, b2):
            par = n % 2
            phs = slice(par * 64, par * 64 + 64)
            for h in range(H):
                cell = kpair[:, (h * 8 + n) * 128:(h * 8 + n + 1) * 128]
                mm(b0[phs, h * 64:(h + 1) * 64], cell[:, 64:128], cell[:, 0:64],
                   tp=(0, par * 64),
                   waits=(([(ve, vkp)] if n == 0 else []) + bank_wait(b0)
                          + bank_wait(b1) + bank_wait(b2)) if h == 0 else ())
            for h in range(H):
                cell = kpair[:, (h * 8 + n) * 128:(h * 8 + n + 1) * 128]
                mm(b1[phs, h * 64:(h + 1) * 64], cell[:, 0:64], cell[:, 0:64],
                   tp=(0, par * 64))
            for h in range(H):
                cell = kpair[:, (h * 8 + n) * 128:(h * 8 + n + 1) * 128]
                mm(b2[phs, h * 64:(h + 1) * 64], cell[:, 0:64], cell[:, 64:128],
                   tp=(0, par * 64))
            return penop()

        def a_inits(n, b0, b1, b2, praw):
            par = n % 2
            phs = slice(par * 64, par * 64 + 64)
            v_op(lambda v, phs=phs, b0=b0: v.tensor_mul(Xs[phs, :], b0[phs, 0:512],
                                                        maskA[phs, :]), waits=[(pe, praw)])
            v_op(lambda v, phs=phs, n=n, b1=b1: v.tensor_mul(
                att_all[phs, (n // 2) * 512:(n // 2 + 1) * 512], b1[phs, 0:512],
                maskLTI[phs, :]))
            v_op(lambda v, phs=phs, b2=b2: v.tensor_mul(Ys[phs, :], b2[phs, 0:512],
                                                        maskAT[phs, :]))
            return v_op(lambda v, phs=phs: v.tensor_add(Ps[phs, :], Ys[phs, :], ipk[phs, :]))

        def a_xy(n, j, b0, b2, vrdy):
            par = n % 2
            phs = slice(par * 64, par * 64 + 64)
            for h in range(H):
                mm(b0[phs, h * 64:(h + 1) * 64], Ys[phs, h * 64:(h + 1) * 64],
                   Xs[phs, h * 64:(h + 1) * 64], tp=(par * 64, par * 64),
                   waits=[(ve, vrdy), (ac, ac.n)] if h == 0 else ())
            if j < 5:
                for h in range(H):
                    mm(b2[phs, h * 64:(h + 1) * 64], Xs[phs, h * 64:(h + 1) * 64],
                       Ys[phs, h * 64:(h + 1) * 64], tp=(par * 64, par * 64))
            return penop()

        def a_xyev(n, j, b0, b2, pxy):
            par = n % 2
            phs = slice(par * 64, par * 64 + 64)
            aX = a_op(lambda s, phs=phs, b0=b0: s.activation(Xs[phs, :], b0[phs, 0:512],
                                                             AF.Identity), waits=[(pe, pxy)])
            if j < 5:
                aX = a_op(lambda s, phs=phs, b2=b2: s.activation(Ys[phs, :], b2[phs, 0:512],
                                                                 AF.Identity))
            return aX

        def a_pmm(n, b1, aX):
            par = n % 2
            phs = slice(par * 64, par * 64 + 64)
            for h in range(H):
                mm(b1[phs, h * 64:(h + 1) * 64], Xs[phs, h * 64:(h + 1) * 64],
                   Ps[phs, h * 64:(h + 1) * 64], tp=(par * 64, par * 64),
                   waits=[(ac, aX)] if h == 0 else ())
            return penop()

        def a_pev(n, j, b1, ppm):
            par = n % 2
            phs = slice(par * 64, par * 64 + 64)
            if j < 5:
                return v_op(lambda v, phs=phs, b1=b1: v.tensor_add(Ps[phs, :], Ps[phs, :],
                                                                   b1[phs, 0:512]),
                            waits=[(pe, ppm)])
            return v_op(lambda v, phs=phs, n=n, b1=b1: v.tensor_add(
                at_all[phs, (n // 2) * 512:(n // 2 + 1) * 512], Ps[phs, :],
                b1[phs, 0:512]), waits=[(pe, ppm)])

        bt_even = (pD, pE, pF)
        bt_odd = (pBa, psT32, pBb)
        for pair in range(4):
            c0, c1 = 2 * pair, 2 * pair + 1
            praw0 = a_raws(c0, *bt_even)
            praw1 = a_raws(c1, *bt_odd)
            v0 = a_inits(c0, *bt_even, praw0)
            v1 = a_inits(c1, *bt_odd, praw1)
            for j in range(1, 6):
                pxy0 = a_xy(c0, j, bt_even[0], bt_even[2], v0)
                pxy1 = a_xy(c1, j, bt_odd[0], bt_odd[2], v1)
                aX0 = a_xyev(c0, j, bt_even[0], bt_even[2], pxy0)
                aX1 = a_xyev(c1, j, bt_odd[0], bt_odd[2], pxy1)
                ppm0 = a_pmm(c0, bt_even[1], aX0)
                ppm1 = a_pmm(c1, bt_odd[1], aX1)
                v0 = a_pev(c0, j, bt_even[1], ppm0)
                v1 = a_pev(c1, j, bt_odd[1], ppm1)
            for b in bt_even:
                bank_free[bank_key(b)] = (ve, v0)
            for b in bt_odd:
                bank_free[bank_key(b)] = (ve, v1)
            yield
        bank_free[bank_key(pB)] = (ve, v1)
        vchainA = ve.n

        dwR0 = g_dma(wbuf[0][:], wrT[:, 0:4096], waits=[(pe, wbuf_ms[0])], par=0)
        dwR1 = g_dma(wbuf[1][:], wrT[:, 4096:8192], waits=[(pe, wbuf_ms[1])], par=1)
        def mk_kde(n, wts):
            kf_c = kpair.rearrange("p (h c q) -> p h c q", h=8, c=8)[:, :, n, 0:64]
            kb_c = kpair.rearrange("p (h c q) -> p h c q", h=8, c=8)[:, :, n, 64:128]
            dep = de_pat[:].rearrange("p (h q) -> p h q", h=8)
            pb = n % 2
            g_op(lambda g, kf_c=kf_c, dep=dep, pb=pb: g.tensor_mul(
                kde_s[:, pb * 512:(pb + 1) * 512].rearrange("p (h q) -> p h q", h=8),
                kf_c, dep), waits=wts)
            return g_op(lambda g, kb_c=kb_c, dep=dep, pb=pb: g.tensor_mul(
                kbdn_s[:, pb * 512:(pb + 1) * 512].rearrange("p (h q) -> p h q", h=8),
                kb_c, dep))

        def b_o_mms(n):
            par = n % 2
            tc = n // 2
            phs = slice(par * 64, par * 64 + 64)
            sb_v = Sb[:, (par) * 1024:(par) * 1024 + 1024]
            for h in range(H):
                o1w = bank_wait(psT) if h == 0 else ()
                mm(psT32[:, h * 64:(h + 1) * 64], sb_v[:, h * 128:(h + 1) * 128],
                   kde_s[:, par * 512 + h * 64: par * 512 + (h + 1) * 64],
                   start=True, stop=False, waits=o1w)
                mm(psT32[:, h * 64:(h + 1) * 64], vnewtm[phs, h * 128:(h + 1) * 128],
                   att_all[phs, (n // 2) * 512 + h * 64:(n // 2) * 512 + (h + 1) * 64],
                   start=False, stop=True, tp=(par * 64, 0))
            return penop()

        def b_o_evac(n, po1):
            a_op(lambda s, n=n: s.activation(
                o_fm[:].rearrange("p (h t) -> p h t", h=8)[:, :, n * 64:(n + 1) * 64],
                psT32[:, 0:512].rearrange("p (h q) -> p h q", h=8), AF.Identity),
                waits=[(pe, po1)])
            bank_free[bank_key(psT)] = (ac, ac.n)

        pkde = mk_kde(0, [(ve, vchainA)])
        prev_o = None
        for n in range(CHT):
            par = n % 2
            tc = n // 2
            phs = slice(par * 64, par * 64 + 64)
            sb_v = Sb[:, par * 1024:par * 1024 + 1024]
            for h in range(H):
                w0w = ([(po, pkde)] + bank_wait(pB)) if h == 0 else []
                if n == 0 and h == 0:
                    w0w = w0w + [(ve, vvtm)]
                mm(pB[phs, h * 128:(h + 1) * 128],
                   kbdn_s[:, par * 512 + h * 64: par * 512 + (h + 1) * 64],
                   sb_v[:, h * 128:(h + 1) * 128], start=True, stop=False, tp=(0, par * 64),
                   waits=w0w)
                mm(pB[phs, h * 128:(h + 1) * 128], idc[phs, :],
                   vtm[phs, tc * D + h * 128: tc * D + (h + 1) * 128],
                   start=False, stop=True, tp=(par * 64, par * 64))
            p1 = penop()
            aw0 = a_op(lambda s, phs=phs: s.activation(w0tm[phs, 0:512], pB[phs, 0:512],
                                                       AF.Identity), waits=[(pe, p1)])
            vw0 = v_op(lambda v, phs=phs: v.tensor_copy(w0tm[phs, 512:1024], pB[phs, 512:1024]),
                       waits=[(pe, p1)])
            for h in range(H):
                mm(pB[phs, h * 128:(h + 1) * 128],
                   at_all[phs, (n // 2) * 512 + h * 64:(n // 2) * 512 + (h + 1) * 64],
                   w0tm[phs, h * 128:(h + 1) * 128], tp=(par * 64, par * 64),
                   waits=[(ac, aw0), (ve, vw0)] if h == 0 else ())
            p1 = penop()
            avn = a_op(lambda s, phs=phs: s.activation(vnewtm[phs, 0:512], pB[phs, 0:512],
                                                       AF.Identity), waits=[(pe, p1)])
            vvn = v_op(lambda v, phs=phs: v.tensor_copy(vnewtm[phs, 512:1024], pB[phs, 512:1024]),
                       waits=[(pe, p1)])
            if prev_o is not None:
                po1 = b_o_mms(n - 1)
            if n + 1 < CHT:
                nkde = mk_kde(n + 1, [(pe, pe.n)])
            for h in range(H):
                mm(pB[:, h * 128:(h + 1) * 128],
                   kdw_tm[phs, tc * D + h * 128: tc * D + (h + 1) * 128],
                   vnewtm[phs, h * 128:(h + 1) * 128], tp=(par * 64, 0),
                   waits=[(ac, avn), (ve, vvn)] if h == 0 else ())
            pds = penop()
            if prev_o is not None:
                b_o_evac(n - 1, po1)
            P.wait("vector", pe, pds)
            for h in range(H):
                v_op(lambda v, h=h: v.scalar_tensor_tensor(
                    S32[:, h * 128:(h + 1) * 128], S32[:, h * 128:(h + 1) * 128],
                    ctdcol[:, h:h + 1], pB[:, h * 128:(h + 1) * 128],
                    ALU.mult, ALU.add))
            vS = ve.n
            bank_free[bank_key(pB)] = (ve, vS)
            sb2 = ((n + 1) % 2) * 1024
            a_op(lambda s, sb2=sb2: s.activation(Sb[:, sb2:sb2 + 512], S32[:, 0:512],
                                                 AF.Identity), waits=[(ve, vS)])
            v_op(lambda v, sb2=sb2: v.tensor_copy(Sb[:, sb2 + 512:sb2 + 1024],
                                                  S32[:, 512:1024]))
            P.wait("tensor", ve, ve.n)
            P.wait("tensor", ac, ac.n)
            prev_o = n
            pkde = nkde if n + 1 < CHT else pkde
            yield
        po1 = b_o_mms(CHT - 1)
        b_o_evac(CHT - 1, po1)
        vchunks = ve.n
        achunks = ac.n

        res1done = []
        for f in range(8):
            bank = pE if f % 2 == 0 else pF
            for k in range(8):
                mm(bank[:, 0:512],
                   wbuf[k // 4][:, (k % 4) * D + f * 128: (k % 4) * D + f * 128 + 128],
                   o_fm[:, k * TILE:(k + 1) * TILE], start=(k == 0), stop=(k == 7),
                   waits=(([(ve, vchunks), (ac, achunks), dwR0, dwR1] if f == 0 else [])
                          + bank_wait(bank)) if k == 0 else ())
            p1 = penop()
            ae1 = a_op(lambda s, bank=bank, f=f: s.activation(tq3[:, 0:512], bank[:, 0:512],
                                                              AF.Identity,
                                                              scale=gacol[:, f:f + 1]),
                       waits=[(pe, p1), (ve, ve.n)])
            bank_free[bank_key(bank)] = (ac, ae1)
            v_op(lambda v, f=f, xtile=xtile: v.tensor_add(xtile[:, f * TILE:(f + 1) * TILE],
                                             xtile[:, f * TILE:(f + 1) * TILE], tq3[:, 0:512]),
                 waits=[(ac, ae1)])
            res1done.append((ve, v_op(lambda v, f=f, xtile=xtile: v.scalar_tensor_tensor(
                xtile[:, f * TILE:(f + 1) * TILE], h_fm[:, f * TILE:(f + 1) * TILE],
                gacol[:, f:f + 1], xtile[:, f * TILE:(f + 1) * TILE], ALU.mult, ALU.add))))
            if f % 2 == 1:
                yield
        vres1 = ve.n
        wbuf_ms[0] = wbuf_ms[1] = pe.n
        P.wait("scalar", po, po.n)
        P.wait("vector", po, po.n)

        h2_ready = ln_pass(cbpcol, 24, vres1, ac.n, xtile=xtile, xdone=res1done)

        yield "q1barrier"
        if tile + 1 < NT:
            s_dma(xt[(tile + 1) % 2][:], xT[:, (tile + 1) * 4096:(tile + 2) * 4096],
                  waits=[(so, so.n)], inc=dxt, selfwait=True)
        v_op(lambda v: v.tensor_tensor(tq[:], h_fm[:, 0:512], h_fm[:, 512:1024], ALU.max),
             waits=[(ac, h2_ready), ffn_ms.get("dancepe", (pe, 0))])
        v_op(lambda v: v.tensor_tensor(tq[:], tq[:], h_fm[:, 1024:1536], ALU.max))
        v_op(lambda v: v.tensor_tensor(tq[:], tq[:], h_fm[:, 1536:2048], ALU.max))
        v_op(lambda v: v.tensor_tensor(sbc[:], h_fm[:, 0:512], h_fm[:, 512:1024], ALU.min))
        v_op(lambda v: v.tensor_tensor(sbc[:], sbc[:], h_fm[:, 1024:1536], ALU.min))
        v_op(lambda v: v.tensor_tensor(sbc[:], sbc[:], h_fm[:, 1536:2048], ALU.min))
        v_op(lambda v: v.tensor_tensor(tq2[:], h_fm[:, 2048:2560], h_fm[:, 2560:3072], ALU.max))
        v_op(lambda v: v.tensor_tensor(tq2[:], tq2[:], h_fm[:, 3072:3584], ALU.max))
        v_op(lambda v: v.tensor_tensor(tq2[:], tq2[:], h_fm[:, 3584:4096], ALU.max))
        v_op(lambda v: v.tensor_tensor(sibc[:], h_fm[:, 2048:2560], h_fm[:, 2560:3072], ALU.min))
        v_op(lambda v: v.tensor_tensor(sibc[:], sibc[:], h_fm[:, 3072:3584], ALU.min))
        v_op(lambda v: v.tensor_tensor(sibc[:], sibc[:], h_fm[:, 3584:4096], ALU.min))
        v_op(lambda v: v.tensor_tensor(tq[:], tq[:], tq2[:], ALU.max))
        v_op(lambda v: v.tensor_tensor(sbc[:], sbc[:], sibc[:], ALU.min))
        vq = v_op(lambda v: v.scalar_tensor_tensor(tq[:], sbc[:], -1.0,
                                                   tq[:], ALU.mult, ALU.max))

        def quant_dance(vq, col_xform):
            for cell in range(4):
                P.emit("tensor", lambda t, cell=cell: t.transpose(
                    psT[:, cell * 128:(cell + 1) * 128], tq[:, cell * 128:(cell + 1) * 128],
                    id128[:]),
                    waits=([(ve, vq)] + bank_wait(psT)) if cell == 0 else (), inc=pe)
            p1 = pe.n
            for cell in range(4):
                v_op(lambda v, cell=cell: v.tensor_reduce(
                    amaxc[:, cell:cell + 1], psT[:, cell * 128:(cell + 1) * 128], AX.X, ALU.max),
                    waits=[(pe, p1)] if cell == 0 else ())
            bank_free[bank_key(psT)] = (ve, ve.n)
            vq2 = col_xform()
            P.emit("tensor", lambda t: t.transpose(pD[0:8, 0:128], amaxc[:, 0:8], idf32[:]),
                   waits=[(ve, vq2)] + bank_wait(pD), inc=pe)
            p1 = pe.n
            vq2 = v_op(lambda v: v.tensor_copy(Mq[:], pD[0:8, 0:128]), waits=[(pe, p1)])
            bank_free[bank_key(pD)] = (ve, vq2)
            P.wait("tensor", ve, vq2)
            for cell in range(4):
                mm(pE[:, cell * 128:(cell + 1) * 128], e8[:, 4 + cell:5 + cell].broadcast_to([8, 128]),
                   Mq[:], waits=bank_wait(pE) if cell == 0 else ())
                mm(pF[:, cell * 128:(cell + 1) * 128], e8[:, cell:cell + 1].broadcast_to([8, 128]),
                   Mq[:], waits=bank_wait(pF) if cell == 0 else ())
            p1 = penop()
            v_op(lambda v: v.tensor_copy(sbc[:], pE[:, 0:512]), waits=[(pe, p1)])
            vbc = v_op(lambda v: v.tensor_copy(sibc[:], pF[:, 0:512]))
            bank_free[bank_key(pE)] = (ve, vbc)
            bank_free[bank_key(pF)] = (ve, vbc)
            P.wait("tensor", ve, vbc)
            return vbc

        def q1cols():
            v_op(lambda v: v.tensor_scalar_max(amaxc[:, 0:4], amaxc[:, 0:4], 1e-5))
            v_op(lambda v: v.reciprocal(amaxc[:, 4:8], amaxc[:, 0:4]))
            v_op(lambda v: v.tensor_scalar_mul(amaxc[:, 4:8], amaxc[:, 4:8], 127.0))
            return v_op(lambda v: v.tensor_scalar_mul(amaxc[:, 0:4], amaxc[:, 0:4], 1.0 / 127.0))

        quant_dance(vq, q1cols)
        for cp in range(4):
            sl = slice(cp * 1024, (cp + 1) * 1024)
            bc2 = sbc[:].unsqueeze(1).broadcast_to([128, 2, 512])
            bc2i = sibc[:].unsqueeze(1).broadcast_to([128, 2, 512])
            v_op(lambda v, sl=sl, bc2=bc2: v.tensor_mul(
                kpair[:, 0:1024].rearrange("p (a q) -> p a q", a=2),
                h_fm[:, sl].rearrange("p (a q) -> p a q", a=2), bc2))
            v_op(lambda v: v.tensor_scalar(kpair[:, 0:1024], kpair[:, 0:1024],
                                           MAGIC, -MAGIC, ALU.add, ALU.add))
            v_op(lambda v, sl=sl, bc2i=bc2i: v.tensor_mul(
                xq[:, sl].rearrange("p (a q) -> p a q", a=2),
                kpair[:, 0:1024].rearrange("p (a q) -> p a q", a=2), bc2i),
                waits=[ffn_ms["qb"]] if (cp == 0 and "qb" in ffn_ms) else [])
        vxq = ve.n
        yield "ffnstart"

        if tile == 0:
            P.wait("gpsimd", dstP[0], dstP[0].n)
            P.wait("gpsimd", dstP[1], dstP[1].n)
        q1pe = pe.n
        rot = [pA, pC]
        ri = 0
        qm = ffn_ms.get("qmul", (ve, 0))
        mml = {0: qm, 1: qm, 2: qm, 3: qm}
        for og in range(8):
            for which, scr in ((0, sc_g), (1, sc_u)):
                buf = wbuf[(og * 2 + which) % 2]
                dk = g_dma(buf[:], scr[og * 128:(og + 1) * 128, :],
                           waits=[(pe, wbuf_ms[(og * 2 + which) % 2])],
                           par=(og * 2 + which) % 2)
                for osub in range(4):
                    bank = rot[ri % 2]
                    ri += 1
                    for k in range(8):
                        mm(bank[:, 0:512], buf[:, k * 512 + osub * 128: k * 512 + osub * 128 + 128],
                           xq[:, k * TILE:(k + 1) * TILE], start=(k == 0), stop=(k == 7),
                           waits=([dk, (ve, vxq)] + bank_wait(bank)) if k == 0 else ())
                    p1 = penop()
                    if osub == 3:
                        wbuf_ms[(og * 2 + which) % 2] = p1
                    if which == 0:
                        aev = a_op(lambda s, osub=osub, bank=bank: s.activation(
                            gu[:, osub * 512:(osub + 1) * 512], bank[:, 0:512], AF.Silu,
                            scale=agcol[:, 0:1]), waits=[(pe, p1), mml[osub]])
                    else:
                        aev = a_op(lambda s, osub=osub, bank=bank: s.activation(
                            gu[:, 2048 + osub * 512:2048 + (osub + 1) * 512], bank[:, 0:512],
                            AF.Identity), waits=[(pe, p1)])
                        oc = og * 4 + osub
                        mml[osub] = (ve, v_op(lambda v, osub=osub, oc=oc: v.tensor_mul(
                            m_fm[:, oc * 512:(oc + 1) * 512], gu[:, osub * 512:(osub + 1) * 512],
                            gu[:, 2048 + osub * 512:2048 + (osub + 1) * 512]),
                            waits=[(ac, aev)]))
                        if oc == 1:
                            v_op(lambda v: v.tensor_tensor(
                                tq[:], m_fm[:, 0:512], m_fm[:, 512:1024], ALU.max),
                                waits=[(pe, q1pe)])
                            v_op(lambda v: v.tensor_tensor(
                                sbc[:], m_fm[:, 0:512], m_fm[:, 512:1024], ALU.min))
                        elif oc >= 2:
                            v_op(lambda v, oc=oc: v.tensor_tensor(
                                tq[:], tq[:], m_fm[:, oc * 512:(oc + 1) * 512], ALU.max))
                            v_op(lambda v, oc=oc: v.tensor_tensor(
                                sbc[:], sbc[:], m_fm[:, oc * 512:(oc + 1) * 512], ALU.min))
                    bank_free[bank_key(bank)] = (ac, aev)
            yield
        pm = ve.n

        vq = v_op(lambda v: v.scalar_tensor_tensor(tq[:], sbc[:], -1.0,
                                                   tq[:], ALU.mult, ALU.max),
                  waits=[(ve, pm)])

        def q2cols():
            v_op(lambda v: v.tensor_single_scalar(amaxc[:, 0:4], amaxc[:, 0:4],
                                                  aucol[:, 0:1], ALU.mult))
            v_op(lambda v: v.tensor_scalar_max(amaxc[:, 0:4], amaxc[:, 0:4], 1e-5))
            v_op(lambda v: v.reciprocal(amaxc[:, 4:8], amaxc[:, 0:4]))
            v_op(lambda v: v.tensor_scalar_mul(amaxc[:, 4:8], amaxc[:, 4:8], 127.0))
            v_op(lambda v: v.tensor_single_scalar(amaxc[:, 4:8], amaxc[:, 4:8],
                                                  aucol[:, 0:1], ALU.mult))
            v_op(lambda v: v.tensor_single_scalar(amaxc[:, 0:4], amaxc[:, 0:4],
                                                  adcol[:, 0:1], ALU.mult))
            return v_op(lambda v: v.tensor_scalar_mul(amaxc[:, 0:4], amaxc[:, 0:4], 1.0 / 127.0))

        quant_dance(vq, q2cols)
        ffn_ms["dancepe"] = (pe, pe.n)
        rms = []
        amag = [ac.n, ac.n]
        pegu = pe.n
        for ip in range(16):
            sl = slice(ip * 1024, (ip + 1) * 1024)
            qa = gu[:, (ip % 2) * 2048:(ip % 2) * 2048 + 1024]
            qb = xq32[:, (ip % 2) * 1024:(ip % 2) * 1024 + 1024]
            bc2 = sbc[:].unsqueeze(1).broadcast_to([128, 2, 512])
            v_op(lambda v, sl=sl, bc2=bc2, qa=qa: v.tensor_mul(
                qa.rearrange("p (a q) -> p a q", a=2),
                m_fm[:, sl].rearrange("p (a q) -> p a q", a=2), bc2),
                waits=[(ac, amag[ip % 2])])
            a_op(lambda s, qa=qa, qb=qb: s.activation(qb, qa, AF.Identity, bias=magc[:]),
                 waits=[(ve, ve.n)] + ([(pe, pegu)] if ip < 2 else []))
            amag[ip % 2] = a_op(lambda s, sl=sl, qb=qb: s.activation(
                m_fm[:, sl], qb, AF.Identity, bias=nmagc[:]))
            if ip % 4 == 3:
                rms.append((ac, amag[ip % 2]))
        ffn_ms["qb"] = (ac, ac.n)
        ffn_ms["qmul"] = (ve, ve.n)
        vq2v = ve.n
        yield

        for fg in range(2):
            banks = [pA, pC, pD, pE]
            for kg in range(4):
                buf = wbuf[kg % 2]
                dk = g_dma(buf[:], sc_d[(kg * 2 + fg) * 128:(kg * 2 + fg + 1) * 128, :],
                           waits=[(pe, wbuf_ms[kg % 2])], par=kg % 2)
                for fsub in range(4):
                    bw = (bank_wait(banks[0]) + bank_wait(banks[1]) + bank_wait(banks[2])
                          + bank_wait(banks[3])) if (kg == 0 and fsub == 0) else []
                    for k in range(8):
                        ic = kg * 8 + k
                        mm(banks[fsub][:, 0:512],
                           buf[:, k * 512 + fsub * 128: k * 512 + fsub * 128 + 128],
                           m_fm[:, ic * 512:(ic + 1) * 512],
                           start=(kg == 0 and k == 0), stop=(kg == 3 and k == 7),
                           waits=([dk, rms[kg]] + bw)
                           if (fsub == 0 and k == 0) else
                           (bw if k == 0 else ()))
                wbuf_ms[kg % 2] = P.inc_last("tensor", pe)
            p1 = pe.n
            for fsub in range(4):
                f = fg * 4 + fsub
                v_op(lambda v, fsub=fsub, banks=banks: v.tensor_mul(
                    tq[:], banks[fsub][:, 0:512], sibc[:]),
                    waits=[(pe, p1)] if fsub == 0 else [])
                vr = v_op(lambda v, f=f, xtile=xtile: v.scalar_tensor_tensor(
                    xtile[:, f * TILE:(f + 1) * TILE], tq[:], gbcol[:, f:f + 1],
                    xtile[:, f * TILE:(f + 1) * TILE], ALU.mult, ALU.add))
                bank_free[bank_key(banks[fsub])] = (ve, vr)
            P.wait("tensor", ve, ve.n)
        vfin = ve.n
        s_dma(outT[:, tile * 4096:(tile + 1) * 4096], xtile[:],
              waits=[(ve, vfin)], inc=so, selfwait=True)

    pend, rate = tern_gen(), 4
    for tile in range(NT):
        g = tile_gen(tile)
        while True:
            try:
                mark = next(g)
            except StopIteration:
                break
            if mark == "q1barrier":
                if pend is not None:
                    for _ in pend:
                        pass
                    pend = None
                continue
            if mark == "ffnstart":
                break
            if pend is not None:
                for _ in range(rate):
                    if next(pend, "END") == "END":
                        pend = None
                        break
        pend, rate = g, 1
    if pend is not None:
        for _ in pend:
            pass

    P.wait("sync", so, so.n)

    with nc.Block() as block:
        @block.sync
        def _(e):
            P.replay("sync", e)

        @block.tensor
        def _(e):
            P.replay("tensor", e)

        @block.vector
        def _(e):
            P.replay("vector", e)

        @block.scalar
        def _(e):
            P.replay("scalar", e)

        @block.gpsimd
        def _(e):
            P.replay("gpsimd", e)

    stack.close()
    return nc


def host_consts():
    bf = ml_dtypes.bfloat16
    iota = np.tile(np.arange(1, C + 1, dtype=np.float32), H)[None, :]
    i64 = np.eye(64, dtype=np.float32)
    ipacked = np.zeros((128, 512), np.float32)
    trilS = np.zeros((128, 512), np.float32)
    triuS = np.zeros((128, 512), np.float32)
    triuI = np.zeros((128, 512), np.float32)
    tl = np.tril(np.ones((64, 64), np.float32), -1)
    ti = np.tril(np.ones((64, 64), np.float32))
    for half in range(2):
        for h in range(H):
            slr = slice(half * 64, half * 64 + 64)
            slc = slice(h * 64, (h + 1) * 64)
            ipacked[slr, slc] = i64
            trilS[slr, slc] = tl
            triuS[slr, slc] = tl.T
            triuI[slr, slc] = ti.T
    return {
        "iota_rep": iota,
        "ones_row_f": np.ones((1, 128), np.float32),
        "ones_col_b": np.ones((128, 1), bf),
        "ident128": np.eye(128, dtype=bf),
        "identcol": np.vstack([np.eye(64), np.eye(64)]).astype(bf),
        "ipacked": ipacked.astype(bf),
        "trilS": trilS.astype(bf),
        "triuS": triuS.astype(bf),
        "triuI": triuI.astype(bf),
        "e16rep": np.eye(16, dtype=bf),
        "e8rep": np.eye(8, dtype=bf),
    }


def pack_w(w):
    return np.ascontiguousarray(w.reshape(8, 128, D).transpose(1, 0, 2).reshape(128, 8 * D))


def pack_x(xt):
    return np.ascontiguousarray(
        xt.reshape(8, 128, 8, 512).transpose(1, 2, 0, 3).reshape(128, 8 * 4096))


def unpack_out(o):
    return np.ascontiguousarray(
        o.reshape(128, 8, 8, 512).transpose(2, 0, 1, 3).reshape(D, T).T)


def kernel(L0_attn_out, L0_attn_W_write, L0_attn_W_read, L0_attn_W_beta, L0_attn_decay,
           L0_ffn_w_gate, L0_ffn_w_up, L0_ffn_w_down, L0_c, L0_wmod):
    shared = host_consts()
    shared.update({
        "wwT": pack_w(np.asarray(L0_attn_W_write).T),
        "wrT": pack_w(np.asarray(L0_attn_W_read).T),
        "wbT": np.ascontiguousarray(np.asarray(L0_attn_W_beta).T),
        "dec": np.ascontiguousarray(np.asarray(L0_attn_decay).reshape(1, H)),
        "wmod": np.ascontiguousarray(np.asarray(L0_wmod)),
        "wgT": np.ascontiguousarray(np.asarray(L0_ffn_w_gate).T),
        "wuT": np.ascontiguousarray(np.asarray(L0_ffn_w_up).T),
        "wdT": np.ascontiguousarray(np.asarray(L0_ffn_w_down).T),
    })
    x = np.asarray(L0_attn_out)
    cc = np.asarray(L0_c)
    in_maps = []
    for b in range(8):
        m = dict(shared)
        m["xT"] = pack_x(np.asarray(x[b]).T)
        m["c_row"] = np.ascontiguousarray(cc[b].reshape(1, D))
        in_maps.append(m)
    nc = build_nc()
    res = run_bass_kernel_spmd(nc, in_maps, core_ids=list(range(8)))
    out = np.stack([unpack_out(np.asarray(r["outT"])) for r in res.results], axis=0)
    return out.astype(np.float32)


# version 20
# speedup vs baseline: 1.0158x; 1.0058x over previous
import numpy as np
import ml_dtypes
import concourse.bass as bass
import concourse.mybir as mybir
from concourse.bass_utils import run_bass_kernel_spmd

F32 = mybir.dt.float32
BF16 = mybir.dt.bfloat16
AF = mybir.ActivationFunctionType
ALU = mybir.AluOpType
AX = mybir.AxisListType

T = 4096
D = 1024
H = 8
C = 64
TILE = 512
NT = T // TILE
CHT = TILE // C
INNER = 4096
MAGIC = 12582912.0


class Sem:
    def __init__(self, h):
        self.h = h
        self.n = 0


class Prog:
    ENGINES = ("sync", "tensor", "vector", "scalar", "gpsimd")

    def __init__(self):
        self.ops = {e: [] for e in self.ENGINES}
        self.self_sem = {}

    def emit(self, eng, fn, waits=(), inc=None, inc_k=1):
        waits = list(waits)
        ss = self.self_sem.get(eng)
        if ss is not None and ss.n > 0 and inc is ss:
            waits.append((ss, ss.n))
        wl = [(s.h, v) for (s, v) in waits]
        tgt = None
        if inc is not None:
            inc.n += inc_k
            tgt = (inc.h, inc_k)
        self.ops[eng].append([fn, wl, tgt])

    def inc_last(self, eng, sem, k=1):
        rec = self.ops[eng][-1]
        assert isinstance(rec, list) and rec[2] is None
        sem.n += k
        rec[2] = (sem.h, k)
        return sem.n

    def wait(self, eng, sem, val=None):
        v = sem.n if val is None else val
        h = sem.h
        self.ops[eng].append(lambda e: e.wait_ge(h, v))

    def replay(self, eng, e):
        for rec in self.ops[eng]:
            if isinstance(rec, list):
                fn, wl, tgt = rec
                for h, v in wl:
                    e.wait_ge(h, v)
                r = fn(e)
                if tgt is not None:
                    r.then_inc(tgt[0], tgt[1])
            else:
                rec(e)


def build_nc():
    nc = bass.Bass("TRN2", target_bir_lowering=False)

    def din(name, shape, dt=F32):
        return nc.dram_tensor(name, shape, dt, kind="ExternalInput").ap()

    xT = din("xT", [128, 8 * 4096])
    wwT = din("wwT", [128, 8 * D])
    wrT = din("wrT", [128, 8 * D])
    wbT = din("wbT", [D, H])
    dec = din("dec", [1, H])
    c_row = din("c_row", [1, D])
    wmod = din("wmod", [D, 6 * D])
    wgT = din("wgT", [D, INNER])
    wuT = din("wuT", [D, INNER])
    wdT = din("wdT", [INNER, D])
    iota_rep = din("iota_rep", [1, 512])
    ones_row_f = din("ones_row_f", [1, 128])
    ones_col_b = din("ones_col_b", [128, 1], BF16)
    ident128 = din("ident128", [128, 128], BF16)
    identcol = din("identcol", [128, 64], BF16)
    ipacked = din("ipacked", [128, 512], BF16)
    trilS = din("trilS", [128, 512], BF16)
    triuS = din("triuS", [128, 512], BF16)
    triuI = din("triuI", [128, 512], BF16)
    e16rep = din("e16rep", [16, 16], BF16)
    e8rep = din("e8rep", [8, 8], BF16)

    outT = nc.dram_tensor("outT", [128, 8 * 4096], F32, kind="ExternalOutput").ap()
    sc_g = nc.dram_tensor("sc_g", [D, INNER], BF16, kind="Internal").ap()
    sc_u = nc.dram_tensor("sc_u", [D, INNER], BF16, kind="Internal").ap()
    sc_d = nc.dram_tensor("sc_d", [D, INNER], BF16, kind="Internal").ap()

    import contextlib
    stack = contextlib.ExitStack()

    def sb(name, shape, dt=BF16):
        return stack.enter_context(nc.sbuf_tensor(name, shape, dt))

    def ps(name, shape, dt=F32):
        return stack.enter_context(nc.psum_tensor(name, shape, dt))

    xt = [sb("xt0", [128, 8 * TILE], F32), sb("xt1", [128, 8 * TILE], F32)]
    h_fm = sb("h_fm", [128, 8 * TILE])
    kdw_tm = sb("kdw_tm", [128, 4 * D])
    vtm = sb("vtm", [128, 4 * D])
    o_fm = sb("o_fm", [128, H * TILE])
    m_fm = sb("m_fm", [128, 32 * TILE])
    kpair_t = sb("kpair", [128, H * CHT * 128])
    kpair = kpair_t[:]
    kp32 = kpair_t[:].bitcast(F32)
    xq = sb("xq", [128, 8 * TILE])
    xq32 = xq[:].bitcast(F32)
    Xs = sb("Xs", [128, 512])
    Ys = sb("Ys", [128, 512])
    Ps = sb("Ps", [128, 512])
    at_all = sb("at_all", [128, 4 * 512])
    att_all = sb("att_all", [128, 4 * 512])
    w0tm = sb("w0tm", [128, H * 128])
    vnewtm = sb("vnewtm", [128, H * 128])
    kde_s = sb("kde_s", [128, 2 * H * C])
    kbdn_s = sb("kbdn_s", [128, 2 * H * C])
    Sb = sb("Sb", [128, 2 * H * 128])
    S32 = sb("S32", [128, H * 128], F32)
    wsh = sb("wsh", [128, 8 * D])
    wbT_sb = sb("wbT_sb", [128, 8 * H])
    wbuf = [sb("wbuf0", [128, 8 * 512]), sb("wbuf1", [128, 8 * 512])]
    gu = sb("gu", [128, 8 * 512])
    maskA = sb("maskA", [128, 512])
    maskAT = sb("maskAT", [128, 512])
    maskLTI = sb("maskLTI", [128, 512])
    de_pat = sb("de_pat", [128, H * C])
    ipk = sb("ipk", [128, 512])
    idc = sb("idc", [128, 64])
    id128 = sb("id128", [128, 128])
    idf32 = sb("idf32", [128, 128], F32)
    ocol = sb("ocol", [128, 1])
    orow_f = sb("orow_f", [1, 128], F32)
    e16 = sb("e16", [16, 16])
    e8 = sb("e8", [8, 8])
    modcols = sb("modcols", [128, 64], F32)
    gacol = sb("gacol", [128, 8], F32)
    gbcol = sb("gbcol", [128, 8], F32)
    capcol = sb("capcol", [128, 8], F32)
    cbpcol = sb("cbpcol", [128, 8], F32)
    sccol_b = sb("sccol_b", [128, 8])
    rows = xq[:].bitcast(F32)[0:1, 0:1024]
    ctdcol = sb("ctdcol", [128, H], F32)
    dwcol = sb("dwcol", [128, H], F32)
    betacol = sb("betacol", [128, 32], F32)
    f1col = sb("f1col", [128, 32], F32)
    f2col = sb("f2col", [128, 32], F32)
    kdwfcol = sb("kdwfcol", [128, 32], F32)
    fac2 = sb("fac2", [128, 64], F32)
    Mfac = sb("Mfac", [16, 512])
    Mq = sb("Mq", [8, 128])
    tq = sb("tq", [128, 512])
    tq2 = sb("tq2", [128, 512])
    tq3 = sb("tq3", [128, 512])
    sbc = sb("sbc", [128, 512])
    sibc = sb("sibc", [128, 512])
    amaxc = sb("amaxc", [128, 8], F32)
    aucol = sb("aucol", [128, 1], F32)
    adcol = sb("adcol", [128, 1], F32)
    agcol = sb("agcol", [128, 1], F32)
    thrg = sb("thrg", [128, 1], F32)
    thru = sb("thru", [128, 1], F32)
    thrd = sb("thrd", [128, 1], F32)
    acc_abs = sb("acc_abs", [128, 4], F32)
    stat = sb("stat", [1, 1536], F32)
    epsc = sb("epsc", [1, 1], F32)
    onef = sb("onef", [128, 1], F32)
    magc = sb("magc", [128, 1], F32)
    nmagc = sb("nmagc", [128, 1], F32)
    negt = sb("negt", [128, 1], F32)
    cumrow = stat[0:1, 0:512]
    derow = stat[0:1, 512:1024]
    dwrow = stat[0:1, 1024:1536]

    mf32 = m_fm[:].bitcast(F32)

    pA = ps("pA", [128, 512])
    pB = ps("pB", [128, 1024])
    pC = ps("pC", [128, 512])
    pD = ps("pD", [128, 512])
    pE = ps("pE", [128, 512])
    pF = ps("pF", [128, 512])
    psT = ps("psT", [128, 1024], BF16)
    psT32 = psT[:].bitcast(F32)

    dx = Sem(stack.enter_context(nc.semaphore("dx")))
    dxt = Sem(stack.enter_context(nc.semaphore("dxt")))
    dwmP = [Sem(stack.enter_context(nc.semaphore("dwm0"))),
            Sem(stack.enter_context(nc.semaphore("dwm1")))]
    dstP = [Sem(stack.enter_context(nc.semaphore("dst0"))),
            Sem(stack.enter_context(nc.semaphore("dst1")))]
    pe = Sem(stack.enter_context(nc.semaphore("pe")))
    ve = Sem(stack.enter_context(nc.semaphore("ve")))
    ac = Sem(stack.enter_context(nc.semaphore("ac")))
    po = Sem(stack.enter_context(nc.semaphore("po")))
    so = Sem(stack.enter_context(nc.semaphore("so")))

    P = Prog()
    P.self_sem = {"vector": ve, "scalar": ac, "gpsimd": po}

    def mm(out, lhsT, rhs, start=True, stop=True, tp=None, waits=(), inc=None):
        P.emit("tensor", lambda t: t.matmul(out, lhsT, rhs, start=start, stop=stop,
                                            tile_position=tp), waits=waits, inc=inc)

    def penop(waits=()):
        assert not waits
        return P.inc_last("tensor", pe)

    def v_op(fn, waits=()):
        P.emit("vector", fn, waits=waits, inc=ve)
        return ve.n

    def a_op(fn, waits=()):
        P.emit("scalar", fn, waits=waits, inc=ac)
        return ac.n

    def g_op(fn, waits=()):
        P.emit("gpsimd", fn, waits=waits, inc=po)
        return po.n

    def s_dma(out, in_, waits=(), inc=None, selfwait=False):
        inc = inc or dx
        waits = list(waits)
        if selfwait and inc.n > 0:
            waits.append((inc, inc.n))
        P.emit("sync", lambda s: s.dma_start(out=out, in_=in_), waits=waits, inc=inc, inc_k=16)
        return (inc, inc.n)

    def g_dma(out, in_, waits=(), par=0, inc=None):
        inc = inc if inc is not None else dwmP[par]
        waits = list(waits)
        if inc.n > 0:
            waits.append((inc, inc.n))
        P.emit("gpsimd", lambda g: g.dma_start(out=out, in_=in_), waits=waits, inc=inc, inc_k=16)
        return (inc, inc.n)

    bank_free = {}
    _bkey = {}

    def bank_key(bank):
        return _bkey.get(id(bank), id(bank))

    def bank_wait(bank):
        k = bank_key(bank)
        return [bank_free[k]] if k in bank_free else []

    s_dma(ocol[:], ones_col_b[:])
    s_dma(orow_f[:], ones_row_f[:])
    s_dma(id128[:], ident128[:])
    s_dma(idc[:], identcol[:])
    s_dma(ipk[:], ipacked[:])
    s_dma(e16[:], e16rep[:])
    s_dma(e8[:], e8rep[:])
    s_dma(rows[0:1, 0:512], iota_rep[:])
    s_dma(rows[0:1, 512:520], dec[:])
    s_dma(maskA[:], trilS[:])
    s_dma(maskAT[:], triuS[:])
    s_dma(maskLTI[:], triuI[:])

    s_dma(xt[0][:], xT[:, 0:4096], inc=dxt, selfwait=True)
    dma0 = (dx, dx.n)
    dwA = g_dma(wbT_sb[:].rearrange("p (c h) -> p c h", c=8),
                wbT.rearrange("(c p) h -> p c h", p=128), par=0)

    v_op(lambda v: v.tensor_copy(idf32[:], id128[:]), waits=[dma0])
    v_op(lambda v: v.memset(epsc[:], 1e-5))
    v_op(lambda v: v.memset(onef[:], 1.0))
    v_op(lambda v: v.memset(magc[:], MAGIC))
    v_op(lambda v: v.memset(nmagc[:], -MAGIC))
    a1 = a_op(lambda s: s.activation(rows[0:1, 520:528], rows[0:1, 512:520], AF.Sigmoid),
              waits=[dma0])
    a1 = a_op(lambda s: s.activation(rows[0:1, 528:536], rows[0:1, 520:528], AF.Ln))
    v1 = v_op(lambda v: v.tensor_copy(
        cumrow.rearrange("r (h q) -> r h q", h=8),
        rows[0:1, 528:536].unsqueeze(2).broadcast_to([1, 8, 64])),
        waits=[(ac, a1)])
    v1 = v_op(lambda v: v.tensor_mul(cumrow, cumrow, rows[0:1, 0:512]))
    aexp = a_op(lambda s: s.activation(derow, cumrow, AF.Exp), waits=[(ve, v1)])
    v1 = v_op(lambda v: v.tensor_copy(
        rows[0:1, 0:512].rearrange("r (h q) -> r h q", h=8),
        cumrow[0:1, 63::64].unsqueeze(2).broadcast_to([1, 8, 64])))
    v1 = v_op(lambda v: v.tensor_sub(rows[0:1, 512:1024], rows[0:1, 0:512], cumrow))
    adw = a_op(lambda s: s.activation(dwrow, rows[0:1, 512:1024], AF.Exp), waits=[(ve, v1)])
    vneg = v_op(lambda v: v.tensor_scalar_mul(rows[0:1, 0:512], cumrow, -1.0))
    vneg = max(vneg, v1)

    def build_mask(first_rhs_is_ones, dsts):
        for half in range(2):
            for h in range(H):
                o = pE[half * 64:half * 64 + 64, h * 64:(h + 1) * 64]
                w0 = [(ve, vneg), (ac, adw)] if (half == 0 and h == 0) else ()
                if first_rhs_is_ones:
                    mm(o, cumrow[0:1, h * 64:(h + 1) * 64], orow_f[0:1, 0:64],
                       start=True, stop=False, tp=(0, half * 64), waits=w0)
                    mm(o, orow_f[0:1, 0:64], rows[0:1, h * 64:(h + 1) * 64],
                       start=False, stop=True, tp=(0, half * 64))
                else:
                    mm(o, rows[0:1, h * 64:(h + 1) * 64], orow_f[0:1, 0:64],
                       start=True, stop=False, tp=(0, half * 64), waits=w0)
                    mm(o, orow_f[0:1, 0:64], cumrow[0:1, h * 64:(h + 1) * 64],
                       start=False, stop=True, tp=(0, half * 64))
        p1 = penop()
        vm = v_op(lambda v: v.tensor_scalar_min(pE[:, 0:512], pE[:, 0:512], 0.0),
                  waits=[(pe, p1)])
        am = a_op(lambda s: s.activation(pF[:, 0:512], pE[:, 0:512], AF.Exp), waits=[(ve, vm)])
        vlast = 0
        for dstt in dsts:
            vlast = v_op(lambda v, dstt=dstt: v.tensor_mul(dstt[:], pF[:, 0:512], dstt[:]),
                         waits=[(ac, am)])
        return vlast

    vmA = build_mask(True, [maskA])
    P.wait("tensor", ve, vmA)
    vmT = build_mask(False, [maskAT, maskLTI])
    P.wait("tensor", ve, vmT)

    for h in range(H):
        mm(pD[:, h * 64:(h + 1) * 64], orow_f[:], derow[0:1, h * 64:(h + 1) * 64],
           waits=[(ac, aexp)] if h == 0 else ())
    p1 = penop()
    vdep = v_op(lambda v: v.tensor_copy(de_pat[:], pD[:, 0:512]), waits=[(pe, p1)])
    vctd = v_op(lambda v: v.tensor_copy(rows[0:1, 512:520], derow[0:1, 63::64]))
    mm(pD[:, 64:72], orow_f[:], rows[0:1, 512:520], waits=[(ve, vctd), (ve, vdep)])
    p1 = penop()
    vctd = v_op(lambda v: v.tensor_copy(ctdcol[:], pD[:, 64:72]), waits=[(pe, p1)])
    vr2 = 0
    for h in range(H):
        vr2 = v_op(lambda v, h=h: v.tensor_copy(
            rows[0:1, h * 128:(h + 1) * 128].rearrange("r (p q) -> r p q", p=2),
            dwrow[0:1, h * 64:(h + 1) * 64].unsqueeze(1).broadcast_to([1, 2, 64])),
            waits=[(ac, adw)] if h == 0 else ())
    for h in range(H):
        P.emit("tensor", lambda t, h=h: t.transpose(
            pD[:, 88 + h:89 + h], rows[0:1, h * 128:(h + 1) * 128], idf32[0:1, 0:1]),
            waits=[(ve, vr2), (ve, vctd)] if h == 0 else (), inc=pe)
    p1 = pe.n
    vdwc = v_op(lambda v: v.tensor_copy(dwcol[:], pD[:, 88:96]), waits=[(pe, p1)])

    dcr = s_dma(rows[0:1, 0:1024], c_row[:], waits=[(pe, pe.n)], selfwait=True)
    for k in range(8):
        P.emit("tensor", lambda t, k=k: t.transpose(
            pD[:, 72 + k:73 + k], rows[0:1, k * 128:(k + 1) * 128], idf32[0:1, 0:1]),
            waits=[dcr, (ve, vdwc)] if k == 0 else (), inc=pe)
    p1 = pe.n
    vcc = v_op(lambda v: v.tensor_copy(modcols[:, 40:48], pD[:, 72:80]), waits=[(pe, p1)])
    asc = a_op(lambda s: s.activation(sccol_b[:], modcols[:, 40:48], AF.Silu),
               waits=[(ve, vcc)])
    pk_prev = 0
    for k in range(8):
        piece = m_fm[:, (k % 2) * 6144:(k % 2) * 6144 + 6144]
        wts = [dwA] if k == 0 else [(pe, pk_prev)]
        dk = g_dma(piece, wmod[k * 128:(k + 1) * 128, :], waits=wts, par=k % 2)
        for mc in range(48):
            mm(pF[:, mc * 8 + k: mc * 8 + k + 1],
               piece[:, mc * 128:(mc + 1) * 128], sccol_b[:, k:k + 1],
               waits=[dk, (ac, asc)] if mc == 0 else ())
        pk_prev = P.inc_last("tensor", pe)
    p1 = pk_prev
    vmod = v_op(lambda v: v.tensor_reduce(
        modcols[:, 0:48], pF[:, 0:384].rearrange("p (m k) -> p m k", k=8), AX.X, ALU.add),
        waits=[(pe, p1)])
    v_op(lambda v: v.tensor_scalar_add(capcol[:], modcols[:, 8:16], 1.0))
    v_op(lambda v: v.tensor_scalar_add(cbpcol[:], modcols[:, 32:40], 1.0))
    for dstc, off in ((gacol, 16), (gbcol, 40)):
        v_op(lambda v, off=off: v.tensor_scalar(fac2[:, 0:8], modcols[:, off:off + 8],
                                                0.0, 2.0, ALU.is_ge, ALU.mult))
        vg = v_op(lambda v: v.tensor_scalar_sub(fac2[:, 0:8], fac2[:, 0:8], 1.0))
        ag = a_op(lambda s, off=off: s.activation(fac2[:, 8:16], modcols[:, off:off + 8], AF.Abs),
                  waits=[(ve, vg)])
        v_op(lambda v: v.tensor_scalar_max(fac2[:, 8:16], fac2[:, 8:16], 0.25),
             waits=[(ac, ag)])
        v_op(lambda v, dstc=dstc: v.tensor_mul(dstc[:], fac2[:, 0:8], fac2[:, 8:16]))
    mod_ready = ve.n
    P.wait("tensor", ve, mod_ready)


    def tern_matrix(src, scr, thr_col, acol, store):
        R, CC = src.shape
        npieces = R // 128
        vlast = ve.n
        entry_w = [(ve, ve.n), (po, po.n), (pe, pe.n)]
        for p in range(npieces):
            buf = mf32[:, (p % 2) * CC:(p % 2) * CC + CC]
            dk = g_dma(buf, src[p * 128:(p + 1) * 128, :],
                       waits=[(ve, vlast)] if p >= 2 else entry_w, par=p % 2)
            vlast = v_op(lambda v, buf=buf, p=p: v.tensor_reduce(
                acc_abs[:, 1 + p % 2:2 + p % 2], buf, AX.X, ALU.add,
                apply_absolute_value=True), waits=[dk])
            if p == 0:
                vlast = v_op(lambda v: v.tensor_copy(acc_abs[:, 0:1], acc_abs[:, 1:2]))
            else:
                vlast = v_op(lambda v, p=p: v.tensor_add(
                    acc_abs[:, 0:1], acc_abs[:, 0:1], acc_abs[:, 1 + p % 2:2 + p % 2]))
            yield
        mm(pF[0:1, 0:1], acc_abs[:, 0:1], onef[:],
           waits=[(ve, vlast)] + bank_wait(pF))
        p1 = penop()
        vthr = v_op(lambda v: v.tensor_scalar(stat[0:1, 0:1], pF[0:1, 0:1],
                                              1.0 / (R * CC), 1e-5, ALU.mult, ALU.max),
                    waits=[(pe, p1)])
        mm(pF[:, 4:5], orow_f[:], stat[0:1, 0:1], waits=[(ve, vthr)])
        p1 = penop()
        v_op(lambda v: v.tensor_copy(acol[:], pF[:, 4:5]), waits=[(pe, p1)])
        v_op(lambda v: v.tensor_scalar_mul(thr_col[:], pF[:, 4:5], 0.5))
        vthr = v_op(lambda v: v.tensor_scalar_mul(negt[:], thr_col[:], -1.0))
        bank_free[bank_key(pF)] = (ve, vthr)
        plast = po.n
        entry_pe = pe.n
        for p in range(npieces):
            buf = mf32[:, (p % 2) * CC:(p % 2) * CC + CC]
            wts = [(ve, vthr), (pe, entry_pe)] if p < 2 else [(ve, plast)]
            dk = g_dma(buf, src[p * 128:(p + 1) * 128, :], waits=wts, par=p % 2)
            v_op(lambda v, buf=buf: v.tensor_single_scalar(gu[:, 0:CC], buf,
                                                           thr_col[:], ALU.is_gt),
                 waits=[dk, (dstP[0], dstP[0].n), (dstP[1], dstP[1].n)])
            v_op(lambda v, buf=buf: v.tensor_single_scalar(xq[:, 0:CC], buf,
                                                           negt[:], ALU.is_lt))
            plast = v_op(lambda v, CC=CC: v.tensor_sub(gu[:, 0:CC], gu[:, 0:CC],
                                                       xq[:, 0:CC]))
            store(scr, p, [(ve, plast)], dstP[p % 2])
            yield
        return

    def store_gu(scr, p, waits, inc):
        out = scr.rearrange("(og q) (k o) -> q og k o", og=8, k=8)[:, :, p, :]
        in_ = gu[:, 0:INNER].rearrange("q (og o) -> q og o", og=8)
        g_dma(out, in_, waits=waits, inc=inc)

    def store_d(scr, p, waits, inc):
        kg, k = p // 8, p % 8
        out = scr.rearrange("(b q) (k o) -> q b k o", b=8, k=8)[:, kg * 2:kg * 2 + 2, k, :]
        in_ = gu[:, 0:D].rearrange("q (b o) -> q b o", b=2)
        g_dma(out, in_, waits=waits, inc=inc)

    def tern_gen():
        for a in tern_matrix(wgT, sc_g, thrg, agcol, store_gu):
            yield
        for a in tern_matrix(wuT, sc_u, thru, aucol, store_gu):
            yield
        for a in tern_matrix(wdT, sc_d, thrd, adcol, store_d):
            yield
    tern_ve = ve.n

    v_op(lambda v: v.memset(S32[:], 0.0))
    init_ve = v_op(lambda v: v.memset(Sb[:], 0.0))

    dwWW = g_dma(wsh[:], wwT[:], par=0)

    wbuf_ms = [0, 0]

    ffn_ms = {}

    def tile_gen(tile):
        xtile = xt[tile % 2]
        t0 = tile * TILE
        P.wait("vector", dxt, 16 * (tile + 1))
        P.wait("scalar", dxt, 16 * (tile + 1))
        P.wait("tensor", dxt, 16 * (tile + 1))
        if tile == 0:
            P.wait("vector", *dma0)
            P.wait("scalar", *dma0)
            P.wait("gpsimd", ve, init_ve)

        def ln_pass(capc, shoff, vpre, apre, xtile=None, exw=(), xdone=None, fold=None):
            sqdone = []
            for c in range(8):
                sl = slice(c * TILE, (c + 1) * TILE)
                if c % 2 == 0:
                    a_op(lambda s, sl=sl, xtile=xtile: s.activation(kpair[:, sl], xtile[:, sl],
                                                                    AF.Square),
                         waits=(([(ac, apre)] + list(exw)) if c == 0 else [])
                         + ([xdone[c]] if xdone else []))
                    sqdone.append((ac, ac.n))
                else:
                    v_op(lambda v, sl=sl, xtile=xtile: v.tensor_mul(kpair[:, sl], xtile[:, sl],
                                                                    xtile[:, sl]),
                         waits=([(ve, vpre)] + list(exw)) if c == 1 else ())
                    sqdone.append((ve, ve.n))
            for c in range(8):
                mm(pE[0:1, 0:512], onef[:], xtile[:, c * TILE:(c + 1) * TILE],
                   start=(c == 0), stop=(c == 7),
                   waits=(bank_wait(pE) if c == 0 else [])
                   + ([xdone[c]] if xdone else []))
            for c in range(8):
                mm(pF[0:1, 0:512], ocol[:], kpair[:, c * TILE:(c + 1) * TILE],
                   start=(c == 0), stop=(c == 7),
                   waits=([sqdone[c]] + (bank_wait(pF) if c == 0 else [])))
            p1 = penop()
            v_op(lambda v: v.tensor_scalar_mul(stat[0:1, 0:512], pE[0:1, 0:512], 1.0 / D),
                 waits=[(pe, p1)])
            v_op(lambda v: v.tensor_scalar_mul(stat[0:1, 512:1024], pF[0:1, 0:512], 1.0 / D))
            v_op(lambda v: v.tensor_mul(stat[0:1, 1024:1536], stat[0:1, 0:512], stat[0:1, 0:512]))
            vv = v_op(lambda v: v.tensor_sub(stat[0:1, 512:1024], stat[0:1, 512:1024],
                                             stat[0:1, 1024:1536]))
            aa = a_op(lambda s: s.activation(stat[0:1, 1024:1536], stat[0:1, 512:1024],
                                             AF.Sqrt, bias=epsc[0:1, 0:1]), waits=[(ve, vv)])
            v_op(lambda v: v.reciprocal(stat[0:1, 512:1024], stat[0:1, 1024:1536]),
                 waits=[(ac, aa)])
            vv = v_op(lambda v: v.tensor_mul(stat[0:1, 1024:1536], stat[0:1, 0:512],
                                             stat[0:1, 512:1024]))
            mm(pE[:, 0:512], orow_f[:], stat[0:1, 512:1024], waits=[(ve, vv)])
            mm(pF[:, 0:512], orow_f[:], stat[0:1, 1024:1536])
            p1 = penop()
            alast = 0
            aprev = 0
            rbc = pE[:, 0:512].unsqueeze(1).broadcast_to([128, 2, 512])
            cbc = pF[:, 0:512].unsqueeze(1).broadcast_to([128, 2, 512])
            for cp in range(4):
                sl = slice(cp * 1024, (cp + 1) * 1024)
                aw = ([(pe, p1)] if cp == 0 else []) + ([(ac, aprev)] if aprev else [])
                v_op(lambda v, sl=sl, xtile=xtile: v.tensor_mul(
                    kp32[:, 0:1024].rearrange("p (a q) -> p a q", a=2),
                    xtile[:, sl].rearrange("p (a q) -> p a q", a=2), rbc), waits=aw)
                vap = v_op(lambda v: v.tensor_sub(
                    kp32[:, 0:1024].rearrange("p (a q) -> p a q", a=2),
                    kp32[:, 0:1024].rearrange("p (a q) -> p a q", a=2), cbc))
                for c in (2 * cp, 2 * cp + 1):
                    sl1 = slice(c * TILE, (c + 1) * TILE)
                    ssl = slice((c % 2) * TILE, (c % 2) * TILE + TILE)
                    alast = a_op(lambda s, sl1=sl1, ssl=ssl, c=c: s.activation(
                        h_fm[:, sl1], kp32[:, ssl], AF.Identity,
                        bias=modcols[:, shoff + c:shoff + c + 1],
                        scale=capc[:, c:c + 1]), waits=[(ve, vap)])
                aprev = alast
                if fold is not None:
                    fold(cp, alast)
            bank_free[bank_key(pE)] = (ve, ve.n)
            bank_free[bank_key(pF)] = (ve, ve.n)
            P.wait("tensor", ve, ve.n)
            P.wait("tensor", ac, alast)
            return alast

        yield
        if tile == 0:
            P.wait("vector", ve, mod_ready)
            h_ready = ln_pass(capcol, 0, tern_ve, ac.n, xtile=xtile)
        else:
            h_ready = ln_pass(capcol, 0, ve.n, ac.n, xtile=xtile,
                              exw=[(pe, pe.n), (po, po.n)])
        yield

        for tc in range(4):
            for k in range(8):
                mm(pE[:, tc * 8:(tc + 1) * 8],
                   h_fm[:, k * TILE + tc * 128: k * TILE + tc * 128 + 128],
                   wbT_sb[:, k * 8:(k + 1) * 8], start=(k == 0), stop=(k == 7),
                   waits=([(ac, h_ready), dwA] + bank_wait(pE))
                   if (tc == 0 and k == 0) else ())
        p1 = penop()
        ab = a_op(lambda s: s.activation(betacol[:], pE[:, 0:32], AF.Sigmoid),
                  waits=[(pe, p1)])
        bank_free[bank_key(pE)] = (ac, ab)

        hsqdone = []
        for c in range(8):
            sl = slice(c * TILE, (c + 1) * TILE)
            if c % 2 == 0:
                a_op(lambda s, sl=sl: s.activation(kpair[:, sl], h_fm[:, sl], AF.Square))
                hsqdone.append((ac, ac.n))
            else:
                v_op(lambda v, sl=sl: v.tensor_mul(kpair[:, sl], h_fm[:, sl], h_fm[:, sl]),
                     waits=[(ac, h_ready)] if c == 1 else [])
                hsqdone.append((ve, ve.n))
        for h in range(H):
            for tc in range(4):
                mm(pF[:, tc * 8 + h: tc * 8 + h + 1],
                   kpair[:, h * TILE + tc * 128: h * TILE + tc * 128 + 128], ocol[:],
                   waits=([hsqdone[h]] + (bank_wait(pF) if h == 0 else [])) if tc == 0 else ())
        p1 = penop()
        aa = a_op(lambda s: s.activation(f1col[:], pF[:, 0:32], AF.Sqrt),
                  waits=[(pe, p1)])
        v_op(lambda v: v.tensor_scalar_max(f1col[:], f1col[:], 1e-12),
             waits=[(ac, aa)])
        f1v = fac2[:, 0:64].rearrange("p (t f h) -> p t f h", t=4, f=2)[:, :, 0, :]
        f2v = fac2[:, 0:64].rearrange("p (t f h) -> p t f h", t=4, f=2)[:, :, 1, :]
        v_op(lambda v: v.reciprocal(f1v, f1col[:].rearrange("p (t h) -> p t h", t=4)))
        v_op(lambda v: v.scalar_tensor_tensor(f2v, betacol[:].rearrange("p (t h) -> p t h", t=4),
                                              -1.0, f1v, ALU.mult, ALU.mult))
        vfac = v_op(lambda v: v.tensor_mul(
            kdwfcol[:].rearrange("p (t h) -> p t h", t=4), f1v,
            dwcol[:].unsqueeze(1).broadcast_to([128, 4, 8])))
        bank_free[bank_key(pF)] = (ve, vfac)
        yield

        for tc in range(4):
            for k in range(8):
                lh = h_fm[:, k * TILE + tc * 128: k * TILE + tc * 128 + 128]
                for half in range(2):
                    mm(pB[:, half * 512:(half + 1) * 512], lh,
                       wsh[:, k * D + half * 512: k * D + (half + 1) * 512],
                       start=(k == 0), stop=(k == 7),
                       waits=([dwWW] + bank_wait(pB))
                       if (tc == 0 and k == 0 and half == 0) else ())
            p1 = penop()
            for h in range(H):
                v_op(lambda v, tc=tc, h=h: v.tensor_single_scalar(
                    vtm[:, tc * D + h * 128: tc * D + (h + 1) * 128],
                    pB[:, h * 128:(h + 1) * 128],
                    betacol[:, tc * 8 + h: tc * 8 + h + 1], ALU.mult),
                    waits=[(ac, ab), (pe, p1)] if h == 0 else ())
            bank_free[bank_key(pB)] = (ve, ve.n)
            P.wait("tensor", ve, ve.n)
            if tc == 1:
                for fc in range(4):
                    inp = fac2[:, fc * 16:(fc + 1) * 16]
                    P.emit("tensor", lambda t, inp=inp, fc=fc: t.transpose(
                        pD[0:16, fc * 128:(fc + 1) * 128], inp, idf32[:]),
                        waits=([(ve, vfac)] + bank_wait(pD)) if fc == 0 else (), inc=pe)
                p1t = pe.n
                vM = v_op(lambda v: v.tensor_copy(Mfac[:], pD[0:16, 0:512]),
                          waits=[(pe, p1t)])
                bank_free[bank_key(pD)] = (ve, vM)
            yield
        vvtm = ve.n
        P.wait("tensor", ve, vM)
        yield
        pBa = pB[:, 0:512]
        pBb = pB[:, 512:1024]
        _bkey[id(pBa)] = bank_key(pB)
        _bkey[id(pBb)] = bank_key(pB)
        _bkey[id(psT32)] = bank_key(psT)
        kbanks = [(pE, pF), (pBa, pBb)]
        for h in range(H):
            bE, bF = kbanks[h % 2]
            mm(bE[:, 0:512], e16[:, h:h + 1].broadcast_to([16, 128]), Mfac[:],
               waits=bank_wait(bE))
            mm(bF[:, 0:512], e16[:, 8 + h:9 + h].broadcast_to([16, 128]), Mfac[:],
               waits=bank_wait(bF))
            p1 = penop()
            hsl = h_fm[:, h * TILE:(h + 1) * TILE].rearrange("p (c q) -> p c q", c=8)
            kcell = kpair[:, h * 8 * 128:(h + 1) * 8 * 128].rearrange("p (c q) -> p c q", c=8)
            v_op(lambda v, hsl=hsl, kcell=kcell, bE=bE: v.tensor_mul(
                kcell[:, :, 0:64], hsl, bE[:, 0:512].rearrange("p (c q) -> p c q", c=8)),
                waits=[(pe, p1)])
            vkb = v_op(lambda v, hsl=hsl, kcell=kcell, bF=bF: v.tensor_mul(
                kcell[:, :, 64:128], hsl, bF[:, 0:512].rearrange("p (c q) -> p c q", c=8)))
            bank_free[bank_key(bE)] = (ve, vkb)
            bank_free[bank_key(bF)] = (ve, vkb)
            if h % 2 == 1:
                yield
        P.wait("tensor", ve, vkb)
        vkp = ve.n

        for tc in range(4):
            for h in range(H):
                P.emit("tensor", lambda t, tc=tc, h=h: t.transpose(
                    psT[:, h * 128:(h + 1) * 128],
                    h_fm[:, h * TILE + tc * 128: h * TILE + tc * 128 + 128], id128[:]),
                    waits=([(ac, h_ready)] + bank_wait(psT)) if (tc == 0 and h == 0) else (),
                    inc=pe)
            p1 = pe.n
            for h in range(H):
                v_op(lambda v, tc=tc, h=h: v.tensor_single_scalar(
                    kdw_tm[:, tc * D + h * 128: tc * D + (h + 1) * 128],
                    psT[:, h * 128:(h + 1) * 128],
                    kdwfcol[:, tc * 8 + h: tc * 8 + h + 1], ALU.mult),
                    waits=[(pe, p1)] if h == 0 else ())
            bank_free[bank_key(psT)] = (ve, ve.n)
            P.wait("tensor", ve, ve.n)
            yield

        pBa = pB[:, 0:512]
        pBb = pB[:, 512:1024]
        _bkey[id(pBa)] = bank_key(pB)
        _bkey[id(pBb)] = bank_key(pB)
        _bkey[id(psT32)] = bank_key(psT)

        def a_raws(n, b0, b1, b2):
            par = n % 2
            phs = slice(par * 64, par * 64 + 64)
            for h in range(H):
                cell = kpair[:, (h * 8 + n) * 128:(h * 8 + n + 1) * 128]
                mm(b0[phs, h * 64:(h + 1) * 64], cell[:, 64:128], cell[:, 0:64],
                   tp=(0, par * 64),
                   waits=(([(ve, vkp)] if n == 0 else []) + bank_wait(b0)
                          + bank_wait(b1) + bank_wait(b2)) if h == 0 else ())
            for h in range(H):
                cell = kpair[:, (h * 8 + n) * 128:(h * 8 + n + 1) * 128]
                mm(b1[phs, h * 64:(h + 1) * 64], cell[:, 0:64], cell[:, 0:64],
                   tp=(0, par * 64))
            for h in range(H):
                cell = kpair[:, (h * 8 + n) * 128:(h * 8 + n + 1) * 128]
                mm(b2[phs, h * 64:(h + 1) * 64], cell[:, 0:64], cell[:, 64:128],
                   tp=(0, par * 64))
            return penop()

        def a_inits(n, b0, b1, b2, praw):
            par = n % 2
            phs = slice(par * 64, par * 64 + 64)
            v_op(lambda v, phs=phs, b0=b0: v.tensor_mul(Xs[phs, :], b0[phs, 0:512],
                                                        maskA[phs, :]), waits=[(pe, praw)])
            v_op(lambda v, phs=phs, n=n, b1=b1: v.tensor_mul(
                att_all[phs, (n // 2) * 512:(n // 2 + 1) * 512], b1[phs, 0:512],
                maskLTI[phs, :]))
            v_op(lambda v, phs=phs, b2=b2: v.tensor_mul(Ys[phs, :], b2[phs, 0:512],
                                                        maskAT[phs, :]))
            return v_op(lambda v, phs=phs: v.tensor_add(Ps[phs, :], Ys[phs, :], ipk[phs, :]))

        def a_xy(n, j, b0, b2, vrdy):
            par = n % 2
            phs = slice(par * 64, par * 64 + 64)
            for h in range(H):
                mm(b0[phs, h * 64:(h + 1) * 64], Ys[phs, h * 64:(h + 1) * 64],
                   Xs[phs, h * 64:(h + 1) * 64], tp=(par * 64, par * 64),
                   waits=[(ve, vrdy), (ac, ac.n)] if h == 0 else ())
            if j < 5:
                for h in range(H):
                    mm(b2[phs, h * 64:(h + 1) * 64], Xs[phs, h * 64:(h + 1) * 64],
                       Ys[phs, h * 64:(h + 1) * 64], tp=(par * 64, par * 64))
            return penop()

        def a_xyev(n, j, b0, b2, pxy):
            par = n % 2
            phs = slice(par * 64, par * 64 + 64)
            aX = a_op(lambda s, phs=phs, b0=b0: s.activation(Xs[phs, :], b0[phs, 0:512],
                                                             AF.Identity), waits=[(pe, pxy)])
            if j < 5:
                aX = a_op(lambda s, phs=phs, b2=b2: s.activation(Ys[phs, :], b2[phs, 0:512],
                                                                 AF.Identity))
            return aX

        def a_pmm(n, b1, aX):
            par = n % 2
            phs = slice(par * 64, par * 64 + 64)
            for h in range(H):
                mm(b1[phs, h * 64:(h + 1) * 64], Xs[phs, h * 64:(h + 1) * 64],
                   Ps[phs, h * 64:(h + 1) * 64], tp=(par * 64, par * 64),
                   waits=[(ac, aX)] if h == 0 else ())
            return penop()

        def a_pev(n, j, b1, ppm):
            par = n % 2
            phs = slice(par * 64, par * 64 + 64)
            if j < 5:
                return v_op(lambda v, phs=phs, b1=b1: v.tensor_add(Ps[phs, :], Ps[phs, :],
                                                                   b1[phs, 0:512]),
                            waits=[(pe, ppm)])
            return v_op(lambda v, phs=phs, n=n, b1=b1: v.tensor_add(
                at_all[phs, (n // 2) * 512:(n // 2 + 1) * 512], Ps[phs, :],
                b1[phs, 0:512]), waits=[(pe, ppm)])

        bt_even = (pD, pE, pF)
        bt_odd = (pBa, psT32, pBb)
        for pair in range(4):
            c0, c1 = 2 * pair, 2 * pair + 1
            praw0 = a_raws(c0, *bt_even)
            praw1 = a_raws(c1, *bt_odd)
            v0 = a_inits(c0, *bt_even, praw0)
            v1 = a_inits(c1, *bt_odd, praw1)
            for j in range(1, 6):
                pxy0 = a_xy(c0, j, bt_even[0], bt_even[2], v0)
                pxy1 = a_xy(c1, j, bt_odd[0], bt_odd[2], v1)
                aX0 = a_xyev(c0, j, bt_even[0], bt_even[2], pxy0)
                aX1 = a_xyev(c1, j, bt_odd[0], bt_odd[2], pxy1)
                ppm0 = a_pmm(c0, bt_even[1], aX0)
                ppm1 = a_pmm(c1, bt_odd[1], aX1)
                v0 = a_pev(c0, j, bt_even[1], ppm0)
                v1 = a_pev(c1, j, bt_odd[1], ppm1)
            for b in bt_even:
                bank_free[bank_key(b)] = (ve, v0)
            for b in bt_odd:
                bank_free[bank_key(b)] = (ve, v1)
            yield
        bank_free[bank_key(pB)] = (ve, v1)
        vchainA = ve.n

        dwR0 = g_dma(wbuf[0][:], wrT[:, 0:4096], waits=[(pe, wbuf_ms[0])], par=0)
        dwR1 = g_dma(wbuf[1][:], wrT[:, 4096:8192], waits=[(pe, wbuf_ms[1])], par=1)
        def mk_kde(n, wts):
            kf_c = kpair.rearrange("p (h c q) -> p h c q", h=8, c=8)[:, :, n, 0:64]
            kb_c = kpair.rearrange("p (h c q) -> p h c q", h=8, c=8)[:, :, n, 64:128]
            dep = de_pat[:].rearrange("p (h q) -> p h q", h=8)
            pb = n % 2
            g_op(lambda g, kf_c=kf_c, dep=dep, pb=pb: g.tensor_mul(
                kde_s[:, pb * 512:(pb + 1) * 512].rearrange("p (h q) -> p h q", h=8),
                kf_c, dep), waits=wts)
            return g_op(lambda g, kb_c=kb_c, dep=dep, pb=pb: g.tensor_mul(
                kbdn_s[:, pb * 512:(pb + 1) * 512].rearrange("p (h q) -> p h q", h=8),
                kb_c, dep))

        def b_o_mms(n):
            par = n % 2
            tc = n // 2
            phs = slice(par * 64, par * 64 + 64)
            sb_v = Sb[:, (par) * 1024:(par) * 1024 + 1024]
            for h in range(H):
                o1w = bank_wait(psT) if h == 0 else ()
                mm(psT32[:, h * 64:(h + 1) * 64], sb_v[:, h * 128:(h + 1) * 128],
                   kde_s[:, par * 512 + h * 64: par * 512 + (h + 1) * 64],
                   start=True, stop=False, waits=o1w)
                mm(psT32[:, h * 64:(h + 1) * 64], vnewtm[phs, h * 128:(h + 1) * 128],
                   att_all[phs, (n // 2) * 512 + h * 64:(n // 2) * 512 + (h + 1) * 64],
                   start=False, stop=True, tp=(par * 64, 0))
            return penop()

        def b_o_evac(n, po1):
            v_op(lambda v, n=n: v.tensor_copy(
                o_fm[:].rearrange("p (h t) -> p h t", h=8)[:, :, n * 64:(n + 1) * 64],
                psT32[:, 0:512].rearrange("p (h q) -> p h q", h=8)), waits=[(pe, po1)])
            bank_free[bank_key(psT)] = (ve, ve.n)

        pkde = mk_kde(0, [(ve, vchainA)])
        prev_o = None
        for n in range(CHT):
            par = n % 2
            tc = n // 2
            phs = slice(par * 64, par * 64 + 64)
            sb_v = Sb[:, par * 1024:par * 1024 + 1024]
            for h in range(H):
                w0w = ([(po, pkde)] + bank_wait(pB)) if h == 0 else []
                if n == 0 and h == 0:
                    w0w = w0w + [(ve, vvtm)]
                mm(pB[phs, h * 128:(h + 1) * 128],
                   kbdn_s[:, par * 512 + h * 64: par * 512 + (h + 1) * 64],
                   sb_v[:, h * 128:(h + 1) * 128], start=True, stop=False, tp=(0, par * 64),
                   waits=w0w)
                mm(pB[phs, h * 128:(h + 1) * 128], idc[phs, :],
                   vtm[phs, tc * D + h * 128: tc * D + (h + 1) * 128],
                   start=False, stop=True, tp=(par * 64, par * 64))
            p1 = penop()
            aw0 = a_op(lambda s, phs=phs: s.activation(w0tm[phs, 0:512], pB[phs, 0:512],
                                                       AF.Identity), waits=[(pe, p1)])
            vw0 = v_op(lambda v, phs=phs: v.tensor_copy(w0tm[phs, 512:1024], pB[phs, 512:1024]),
                       waits=[(pe, p1)])
            for h in range(H):
                mm(pB[phs, h * 128:(h + 1) * 128],
                   at_all[phs, (n // 2) * 512 + h * 64:(n // 2) * 512 + (h + 1) * 64],
                   w0tm[phs, h * 128:(h + 1) * 128], tp=(par * 64, par * 64),
                   waits=[(ac, aw0), (ve, vw0)] if h == 0 else ())
            p1 = penop()
            avn = a_op(lambda s, phs=phs: s.activation(vnewtm[phs, 0:512], pB[phs, 0:512],
                                                       AF.Identity), waits=[(pe, p1)])
            vvn = v_op(lambda v, phs=phs: v.tensor_copy(vnewtm[phs, 512:1024], pB[phs, 512:1024]),
                       waits=[(pe, p1)])
            if prev_o is not None:
                po1 = b_o_mms(n - 1)
            if n + 1 < CHT:
                nkde = mk_kde(n + 1, [(pe, pe.n)])
            for h in range(H):
                mm(pB[:, h * 128:(h + 1) * 128],
                   kdw_tm[phs, tc * D + h * 128: tc * D + (h + 1) * 128],
                   vnewtm[phs, h * 128:(h + 1) * 128], tp=(par * 64, 0),
                   waits=[(ac, avn), (ve, vvn)] if h == 0 else ())
            pds = penop()
            if prev_o is not None:
                b_o_evac(n - 1, po1)
            P.wait("vector", pe, pds)
            for h in range(H):
                v_op(lambda v, h=h: v.scalar_tensor_tensor(
                    S32[:, h * 128:(h + 1) * 128], S32[:, h * 128:(h + 1) * 128],
                    ctdcol[:, h:h + 1], pB[:, h * 128:(h + 1) * 128],
                    ALU.mult, ALU.add))
            vS = ve.n
            bank_free[bank_key(pB)] = (ve, vS)
            v_op(lambda v, n=n: v.tensor_copy(Sb[:, ((n + 1) % 2) * 1024:((n + 1) % 2) * 1024 + 1024],
                                              S32[:]))
            P.wait("tensor", ve, ve.n)
            prev_o = n
            pkde = nkde if n + 1 < CHT else pkde
            yield
        po1 = b_o_mms(CHT - 1)
        b_o_evac(CHT - 1, po1)
        vchunks = ve.n
        achunks = ac.n

        res1done = []
        for f in range(8):
            bank = pE if f % 2 == 0 else pF
            for k in range(8):
                mm(bank[:, 0:512],
                   wbuf[k // 4][:, (k % 4) * D + f * 128: (k % 4) * D + f * 128 + 128],
                   o_fm[:, k * TILE:(k + 1) * TILE], start=(k == 0), stop=(k == 7),
                   waits=(([(ve, vchunks), (ac, achunks), dwR0, dwR1] if f == 0 else [])
                          + bank_wait(bank)) if k == 0 else ())
            p1 = penop()
            ae1 = a_op(lambda s, bank=bank, f=f: s.activation(tq3[:, 0:512], bank[:, 0:512],
                                                              AF.Identity,
                                                              scale=gacol[:, f:f + 1]),
                       waits=[(pe, p1), (ve, ve.n)])
            bank_free[bank_key(bank)] = (ac, ae1)
            v_op(lambda v, f=f, xtile=xtile: v.tensor_add(xtile[:, f * TILE:(f + 1) * TILE],
                                             xtile[:, f * TILE:(f + 1) * TILE], tq3[:, 0:512]),
                 waits=[(ac, ae1)])
            res1done.append((ve, v_op(lambda v, f=f, xtile=xtile: v.scalar_tensor_tensor(
                xtile[:, f * TILE:(f + 1) * TILE], h_fm[:, f * TILE:(f + 1) * TILE],
                gacol[:, f:f + 1], xtile[:, f * TILE:(f + 1) * TILE], ALU.mult, ALU.add))))
            if f % 2 == 1:
                yield
        vres1 = ve.n
        wbuf_ms[0] = wbuf_ms[1] = pe.n
        P.wait("scalar", po, po.n)
        P.wait("vector", po, po.n)

        h2_ready = ln_pass(cbpcol, 24, vres1, ac.n, xtile=xtile, xdone=res1done)

        yield "q1barrier"
        if tile + 1 < NT:
            s_dma(xt[(tile + 1) % 2][:], xT[:, (tile + 1) * 4096:(tile + 2) * 4096],
                  waits=[(so, so.n)], inc=dxt, selfwait=True)
        v_op(lambda v: v.tensor_tensor(tq[:], h_fm[:, 0:512], h_fm[:, 512:1024], ALU.max),
             waits=[(ac, h2_ready), ffn_ms.get("dancepe", (pe, 0))])
        v_op(lambda v: v.tensor_tensor(tq[:], tq[:], h_fm[:, 1024:1536], ALU.max))
        v_op(lambda v: v.tensor_tensor(tq[:], tq[:], h_fm[:, 1536:2048], ALU.max))
        v_op(lambda v: v.tensor_tensor(sbc[:], h_fm[:, 0:512], h_fm[:, 512:1024], ALU.min))
        v_op(lambda v: v.tensor_tensor(sbc[:], sbc[:], h_fm[:, 1024:1536], ALU.min))
        v_op(lambda v: v.tensor_tensor(sbc[:], sbc[:], h_fm[:, 1536:2048], ALU.min))
        v_op(lambda v: v.tensor_tensor(tq2[:], h_fm[:, 2048:2560], h_fm[:, 2560:3072], ALU.max))
        v_op(lambda v: v.tensor_tensor(tq2[:], tq2[:], h_fm[:, 3072:3584], ALU.max))
        v_op(lambda v: v.tensor_tensor(tq2[:], tq2[:], h_fm[:, 3584:4096], ALU.max))
        v_op(lambda v: v.tensor_tensor(sibc[:], h_fm[:, 2048:2560], h_fm[:, 2560:3072], ALU.min))
        v_op(lambda v: v.tensor_tensor(sibc[:], sibc[:], h_fm[:, 3072:3584], ALU.min))
        v_op(lambda v: v.tensor_tensor(sibc[:], sibc[:], h_fm[:, 3584:4096], ALU.min))
        v_op(lambda v: v.tensor_tensor(tq[:], tq[:], tq2[:], ALU.max))
        v_op(lambda v: v.tensor_tensor(sbc[:], sbc[:], sibc[:], ALU.min))
        vq = v_op(lambda v: v.scalar_tensor_tensor(tq[:], sbc[:], -1.0,
                                                   tq[:], ALU.mult, ALU.max))

        def quant_dance(vq, col_xform):
            for cell in range(4):
                P.emit("tensor", lambda t, cell=cell: t.transpose(
                    psT[:, cell * 128:(cell + 1) * 128], tq[:, cell * 128:(cell + 1) * 128],
                    id128[:]),
                    waits=([(ve, vq)] + bank_wait(psT)) if cell == 0 else (), inc=pe)
            p1 = pe.n
            for cell in range(4):
                v_op(lambda v, cell=cell: v.tensor_reduce(
                    amaxc[:, cell:cell + 1], psT[:, cell * 128:(cell + 1) * 128], AX.X, ALU.max),
                    waits=[(pe, p1)] if cell == 0 else ())
            bank_free[bank_key(psT)] = (ve, ve.n)
            vq2 = col_xform()
            P.emit("tensor", lambda t: t.transpose(pD[0:8, 0:128], amaxc[:, 0:8], idf32[:]),
                   waits=[(ve, vq2)] + bank_wait(pD), inc=pe)
            p1 = pe.n
            vq2 = v_op(lambda v: v.tensor_copy(Mq[:], pD[0:8, 0:128]), waits=[(pe, p1)])
            bank_free[bank_key(pD)] = (ve, vq2)
            P.wait("tensor", ve, vq2)
            for cell in range(4):
                mm(pE[:, cell * 128:(cell + 1) * 128], e8[:, 4 + cell:5 + cell].broadcast_to([8, 128]),
                   Mq[:], waits=bank_wait(pE) if cell == 0 else ())
                mm(pF[:, cell * 128:(cell + 1) * 128], e8[:, cell:cell + 1].broadcast_to([8, 128]),
                   Mq[:], waits=bank_wait(pF) if cell == 0 else ())
            p1 = penop()
            v_op(lambda v: v.tensor_copy(sbc[:], pE[:, 0:512]), waits=[(pe, p1)])
            vbc = v_op(lambda v: v.tensor_copy(sibc[:], pF[:, 0:512]))
            bank_free[bank_key(pE)] = (ve, vbc)
            bank_free[bank_key(pF)] = (ve, vbc)
            P.wait("tensor", ve, vbc)
            return vbc

        def q1cols():
            v_op(lambda v: v.tensor_scalar_max(amaxc[:, 0:4], amaxc[:, 0:4], 1e-5))
            v_op(lambda v: v.reciprocal(amaxc[:, 4:8], amaxc[:, 0:4]))
            v_op(lambda v: v.tensor_scalar_mul(amaxc[:, 4:8], amaxc[:, 4:8], 127.0))
            return v_op(lambda v: v.tensor_scalar_mul(amaxc[:, 0:4], amaxc[:, 0:4], 1.0 / 127.0))

        quant_dance(vq, q1cols)
        for cp in range(4):
            sl = slice(cp * 1024, (cp + 1) * 1024)
            bc2 = sbc[:].unsqueeze(1).broadcast_to([128, 2, 512])
            bc2i = sibc[:].unsqueeze(1).broadcast_to([128, 2, 512])
            v_op(lambda v, sl=sl, bc2=bc2: v.tensor_mul(
                kpair[:, 0:1024].rearrange("p (a q) -> p a q", a=2),
                h_fm[:, sl].rearrange("p (a q) -> p a q", a=2), bc2))
            v_op(lambda v: v.tensor_scalar(kpair[:, 0:1024], kpair[:, 0:1024],
                                           MAGIC, -MAGIC, ALU.add, ALU.add))
            v_op(lambda v, sl=sl, bc2i=bc2i: v.tensor_mul(
                xq[:, sl].rearrange("p (a q) -> p a q", a=2),
                kpair[:, 0:1024].rearrange("p (a q) -> p a q", a=2), bc2i),
                waits=[ffn_ms["qb"]] if (cp == 0 and "qb" in ffn_ms) else [])
        vxq = ve.n
        yield "ffnstart"

        if tile == 0:
            P.wait("gpsimd", dstP[0], dstP[0].n)
            P.wait("gpsimd", dstP[1], dstP[1].n)
        q1pe = pe.n
        rot = [pA, pC]
        ri = 0
        qm = ffn_ms.get("qmul", (ve, 0))
        mml = {0: qm, 1: qm, 2: qm, 3: qm}
        for og in range(8):
            for which, scr in ((0, sc_g), (1, sc_u)):
                buf = wbuf[(og * 2 + which) % 2]
                dk = g_dma(buf[:], scr[og * 128:(og + 1) * 128, :],
                           waits=[(pe, wbuf_ms[(og * 2 + which) % 2])],
                           par=(og * 2 + which) % 2)
                for osub in range(4):
                    bank = rot[ri % 2]
                    ri += 1
                    for k in range(8):
                        mm(bank[:, 0:512], buf[:, k * 512 + osub * 128: k * 512 + osub * 128 + 128],
                           xq[:, k * TILE:(k + 1) * TILE], start=(k == 0), stop=(k == 7),
                           waits=([dk, (ve, vxq)] + bank_wait(bank)) if k == 0 else ())
                    p1 = penop()
                    if osub == 3:
                        wbuf_ms[(og * 2 + which) % 2] = p1
                    if which == 0:
                        aev = a_op(lambda s, osub=osub, bank=bank: s.activation(
                            gu[:, osub * 512:(osub + 1) * 512], bank[:, 0:512], AF.Silu,
                            scale=agcol[:, 0:1]), waits=[(pe, p1), mml[osub]])
                    else:
                        aev = a_op(lambda s, osub=osub, bank=bank: s.activation(
                            gu[:, 2048 + osub * 512:2048 + (osub + 1) * 512], bank[:, 0:512],
                            AF.Identity), waits=[(pe, p1)])
                        oc = og * 4 + osub
                        mml[osub] = (ve, v_op(lambda v, osub=osub, oc=oc: v.tensor_mul(
                            m_fm[:, oc * 512:(oc + 1) * 512], gu[:, osub * 512:(osub + 1) * 512],
                            gu[:, 2048 + osub * 512:2048 + (osub + 1) * 512]),
                            waits=[(ac, aev)]))
                        if oc == 1:
                            v_op(lambda v: v.tensor_tensor(
                                tq[:], m_fm[:, 0:512], m_fm[:, 512:1024], ALU.max),
                                waits=[(pe, q1pe)])
                            v_op(lambda v: v.tensor_tensor(
                                sbc[:], m_fm[:, 0:512], m_fm[:, 512:1024], ALU.min))
                        elif oc >= 2:
                            v_op(lambda v, oc=oc: v.tensor_tensor(
                                tq[:], tq[:], m_fm[:, oc * 512:(oc + 1) * 512], ALU.max))
                            v_op(lambda v, oc=oc: v.tensor_tensor(
                                sbc[:], sbc[:], m_fm[:, oc * 512:(oc + 1) * 512], ALU.min))
                    bank_free[bank_key(bank)] = (ac, aev)
            yield
        pm = ve.n

        vq = v_op(lambda v: v.scalar_tensor_tensor(tq[:], sbc[:], -1.0,
                                                   tq[:], ALU.mult, ALU.max),
                  waits=[(ve, pm)])

        def q2cols():
            v_op(lambda v: v.tensor_single_scalar(amaxc[:, 0:4], amaxc[:, 0:4],
                                                  aucol[:, 0:1], ALU.mult))
            v_op(lambda v: v.tensor_scalar_max(amaxc[:, 0:4], amaxc[:, 0:4], 1e-5))
            v_op(lambda v: v.reciprocal(amaxc[:, 4:8], amaxc[:, 0:4]))
            v_op(lambda v: v.tensor_scalar_mul(amaxc[:, 4:8], amaxc[:, 4:8], 127.0))
            v_op(lambda v: v.tensor_single_scalar(amaxc[:, 4:8], amaxc[:, 4:8],
                                                  aucol[:, 0:1], ALU.mult))
            v_op(lambda v: v.tensor_single_scalar(amaxc[:, 0:4], amaxc[:, 0:4],
                                                  adcol[:, 0:1], ALU.mult))
            return v_op(lambda v: v.tensor_scalar_mul(amaxc[:, 0:4], amaxc[:, 0:4], 1.0 / 127.0))

        quant_dance(vq, q2cols)
        ffn_ms["dancepe"] = (pe, pe.n)
        rms = []
        amag = [ac.n, ac.n]
        pegu = pe.n
        for ip in range(16):
            sl = slice(ip * 1024, (ip + 1) * 1024)
            qa = gu[:, (ip % 2) * 2048:(ip % 2) * 2048 + 1024]
            qb = xq32[:, (ip % 2) * 1024:(ip % 2) * 1024 + 1024]
            bc2 = sbc[:].unsqueeze(1).broadcast_to([128, 2, 512])
            v_op(lambda v, sl=sl, bc2=bc2, qa=qa: v.tensor_mul(
                qa.rearrange("p (a q) -> p a q", a=2),
                m_fm[:, sl].rearrange("p (a q) -> p a q", a=2), bc2),
                waits=[(ac, amag[ip % 2])])
            a_op(lambda s, qa=qa, qb=qb: s.activation(qb, qa, AF.Identity, bias=magc[:]),
                 waits=[(ve, ve.n)] + ([(pe, pegu)] if ip < 2 else []))
            amag[ip % 2] = a_op(lambda s, sl=sl, qb=qb: s.activation(
                m_fm[:, sl], qb, AF.Identity, bias=nmagc[:]))
            if ip % 4 == 3:
                rms.append((ac, amag[ip % 2]))
        ffn_ms["qb"] = (ac, ac.n)
        ffn_ms["qmul"] = (ve, ve.n)
        vq2v = ve.n
        yield

        for fg in range(2):
            banks = [pA, pC, pD, pE]
            for kg in range(4):
                buf = wbuf[kg % 2]
                dk = g_dma(buf[:], sc_d[(kg * 2 + fg) * 128:(kg * 2 + fg + 1) * 128, :],
                           waits=[(pe, wbuf_ms[kg % 2])], par=kg % 2)
                for fsub in range(4):
                    bw = (bank_wait(banks[0]) + bank_wait(banks[1]) + bank_wait(banks[2])
                          + bank_wait(banks[3])) if (kg == 0 and fsub == 0) else []
                    for k in range(8):
                        ic = kg * 8 + k
                        mm(banks[fsub][:, 0:512],
                           buf[:, k * 512 + fsub * 128: k * 512 + fsub * 128 + 128],
                           m_fm[:, ic * 512:(ic + 1) * 512],
                           start=(kg == 0 and k == 0), stop=(kg == 3 and k == 7),
                           waits=([dk, rms[kg]] + bw)
                           if (fsub == 0 and k == 0) else
                           (bw if k == 0 else ()))
                wbuf_ms[kg % 2] = P.inc_last("tensor", pe)
            p1 = pe.n
            for fsub in range(4):
                f = fg * 4 + fsub
                v_op(lambda v, fsub=fsub, banks=banks: v.tensor_mul(
                    tq[:], banks[fsub][:, 0:512], sibc[:]),
                    waits=[(pe, p1)] if fsub == 0 else [])
                vr = v_op(lambda v, f=f, xtile=xtile: v.scalar_tensor_tensor(
                    xtile[:, f * TILE:(f + 1) * TILE], tq[:], gbcol[:, f:f + 1],
                    xtile[:, f * TILE:(f + 1) * TILE], ALU.mult, ALU.add))
                bank_free[bank_key(banks[fsub])] = (ve, vr)
            P.wait("tensor", ve, ve.n)
        vfin = ve.n
        s_dma(outT[:, tile * 4096:(tile + 1) * 4096], xtile[:],
              waits=[(ve, vfin)], inc=so, selfwait=True)

    pend, rate = tern_gen(), 2
    for tile in range(NT):
        g = tile_gen(tile)
        while True:
            try:
                mark = next(g)
            except StopIteration:
                break
            if mark == "q1barrier":
                if pend is not None:
                    for _ in pend:
                        pass
                    pend = None
                continue
            if mark == "ffnstart":
                break
            if pend is not None:
                for _ in range(rate):
                    if next(pend, "END") == "END":
                        pend = None
                        break
        pend, rate = g, 1
    if pend is not None:
        for _ in pend:
            pass

    P.wait("sync", so, so.n)

    with nc.Block() as block:
        @block.sync
        def _(e):
            P.replay("sync", e)

        @block.tensor
        def _(e):
            P.replay("tensor", e)

        @block.vector
        def _(e):
            P.replay("vector", e)

        @block.scalar
        def _(e):
            P.replay("scalar", e)

        @block.gpsimd
        def _(e):
            P.replay("gpsimd", e)

    stack.close()
    return nc


def host_consts():
    bf = ml_dtypes.bfloat16
    iota = np.tile(np.arange(1, C + 1, dtype=np.float32), H)[None, :]
    i64 = np.eye(64, dtype=np.float32)
    ipacked = np.zeros((128, 512), np.float32)
    trilS = np.zeros((128, 512), np.float32)
    triuS = np.zeros((128, 512), np.float32)
    triuI = np.zeros((128, 512), np.float32)
    tl = np.tril(np.ones((64, 64), np.float32), -1)
    ti = np.tril(np.ones((64, 64), np.float32))
    for half in range(2):
        for h in range(H):
            slr = slice(half * 64, half * 64 + 64)
            slc = slice(h * 64, (h + 1) * 64)
            ipacked[slr, slc] = i64
            trilS[slr, slc] = tl
            triuS[slr, slc] = tl.T
            triuI[slr, slc] = ti.T
    return {
        "iota_rep": iota,
        "ones_row_f": np.ones((1, 128), np.float32),
        "ones_col_b": np.ones((128, 1), bf),
        "ident128": np.eye(128, dtype=bf),
        "identcol": np.vstack([np.eye(64), np.eye(64)]).astype(bf),
        "ipacked": ipacked.astype(bf),
        "trilS": trilS.astype(bf),
        "triuS": triuS.astype(bf),
        "triuI": triuI.astype(bf),
        "e16rep": np.eye(16, dtype=bf),
        "e8rep": np.eye(8, dtype=bf),
    }


def pack_w(w):
    return np.ascontiguousarray(w.reshape(8, 128, D).transpose(1, 0, 2).reshape(128, 8 * D))


def pack_x(xt):
    return np.ascontiguousarray(
        xt.reshape(8, 128, 8, 512).transpose(1, 2, 0, 3).reshape(128, 8 * 4096))


def unpack_out(o):
    return np.ascontiguousarray(
        o.reshape(128, 8, 8, 512).transpose(2, 0, 1, 3).reshape(D, T).T)


def kernel(L0_attn_out, L0_attn_W_write, L0_attn_W_read, L0_attn_W_beta, L0_attn_decay,
           L0_ffn_w_gate, L0_ffn_w_up, L0_ffn_w_down, L0_c, L0_wmod):
    shared = host_consts()
    shared.update({
        "wwT": pack_w(np.asarray(L0_attn_W_write).T),
        "wrT": pack_w(np.asarray(L0_attn_W_read).T),
        "wbT": np.ascontiguousarray(np.asarray(L0_attn_W_beta).T),
        "dec": np.ascontiguousarray(np.asarray(L0_attn_decay).reshape(1, H)),
        "wmod": np.ascontiguousarray(np.asarray(L0_wmod)),
        "wgT": np.ascontiguousarray(np.asarray(L0_ffn_w_gate).T),
        "wuT": np.ascontiguousarray(np.asarray(L0_ffn_w_up).T),
        "wdT": np.ascontiguousarray(np.asarray(L0_ffn_w_down).T),
    })
    x = np.asarray(L0_attn_out)
    cc = np.asarray(L0_c)
    in_maps = []
    for b in range(8):
        m = dict(shared)
        m["xT"] = pack_x(np.asarray(x[b]).T)
        m["c_row"] = np.ascontiguousarray(cc[b].reshape(1, D))
        in_maps.append(m)
    nc = build_nc()
    res = run_bass_kernel_spmd(nc, in_maps, core_ids=list(range(8)))
    out = np.stack([unpack_out(np.asarray(r["outT"])) for r in res.results], axis=0)
    return out.astype(np.float32)


# version 21
# speedup vs baseline: 1.0190x; 1.0032x over previous
import numpy as np
import ml_dtypes
import concourse.bass as bass
import concourse.mybir as mybir
from concourse.bass_utils import run_bass_kernel_spmd

F32 = mybir.dt.float32
BF16 = mybir.dt.bfloat16
AF = mybir.ActivationFunctionType
ALU = mybir.AluOpType
AX = mybir.AxisListType

T = 4096
D = 1024
H = 8
C = 64
TILE = 512
NT = T // TILE
CHT = TILE // C
INNER = 4096
MAGIC = 12582912.0


class Sem:
    def __init__(self, h):
        self.h = h
        self.n = 0


class Prog:
    ENGINES = ("sync", "tensor", "vector", "scalar", "gpsimd")

    def __init__(self):
        self.ops = {e: [] for e in self.ENGINES}
        self.self_sem = {}

    def emit(self, eng, fn, waits=(), inc=None, inc_k=1):
        waits = list(waits)
        ss = self.self_sem.get(eng)
        if ss is not None and ss.n > 0 and inc is ss:
            waits.append((ss, ss.n))
        wl = [(s.h, v) for (s, v) in waits]
        tgt = None
        if inc is not None:
            inc.n += inc_k
            tgt = (inc.h, inc_k)
        self.ops[eng].append([fn, wl, tgt])

    def inc_last(self, eng, sem, k=1):
        rec = self.ops[eng][-1]
        assert isinstance(rec, list) and rec[2] is None
        sem.n += k
        rec[2] = (sem.h, k)
        return sem.n

    def wait(self, eng, sem, val=None):
        v = sem.n if val is None else val
        h = sem.h
        self.ops[eng].append(lambda e: e.wait_ge(h, v))

    def replay(self, eng, e):
        for rec in self.ops[eng]:
            if isinstance(rec, list):
                fn, wl, tgt = rec
                for h, v in wl:
                    e.wait_ge(h, v)
                r = fn(e)
                if tgt is not None:
                    r.then_inc(tgt[0], tgt[1])
            else:
                rec(e)


def build_nc():
    nc = bass.Bass("TRN2", target_bir_lowering=False)

    def din(name, shape, dt=F32):
        return nc.dram_tensor(name, shape, dt, kind="ExternalInput").ap()

    xT = din("xT", [128, 8 * 4096])
    wwT = din("wwT", [128, 8 * D])
    wrT = din("wrT", [128, 8 * D])
    wbT = din("wbT", [D, H])
    dec = din("dec", [1, H])
    c_row = din("c_row", [1, D])
    wmod = din("wmod", [D, 6 * D])
    wgT = din("wgT", [D, INNER])
    wuT = din("wuT", [D, INNER])
    wdT = din("wdT", [INNER, D])
    iota_rep = din("iota_rep", [1, 512])
    ones_row_f = din("ones_row_f", [1, 128])
    ones_col_b = din("ones_col_b", [128, 1], BF16)
    ident128 = din("ident128", [128, 128], BF16)
    identcol = din("identcol", [128, 64], BF16)
    ipacked = din("ipacked", [128, 512], BF16)
    trilS = din("trilS", [128, 512], BF16)
    triuS = din("triuS", [128, 512], BF16)
    triuI = din("triuI", [128, 512], BF16)
    e16rep = din("e16rep", [16, 16], BF16)
    e8rep = din("e8rep", [8, 8], BF16)

    outT = nc.dram_tensor("outT", [128, 8 * 4096], F32, kind="ExternalOutput").ap()
    sc_g = nc.dram_tensor("sc_g", [D, INNER], BF16, kind="Internal").ap()
    sc_u = nc.dram_tensor("sc_u", [D, INNER], BF16, kind="Internal").ap()
    sc_d = nc.dram_tensor("sc_d", [D, INNER], BF16, kind="Internal").ap()

    import contextlib
    stack = contextlib.ExitStack()

    def sb(name, shape, dt=BF16):
        return stack.enter_context(nc.sbuf_tensor(name, shape, dt))

    def ps(name, shape, dt=F32):
        return stack.enter_context(nc.psum_tensor(name, shape, dt))

    xt = [sb("xt0", [128, 8 * TILE], F32), sb("xt1", [128, 8 * TILE], F32)]
    h_fm = sb("h_fm", [128, 8 * TILE])
    kdw_tm = sb("kdw_tm", [128, 4 * D])
    vtm = sb("vtm", [128, 4 * D])
    o_fm = sb("o_fm", [128, H * TILE])
    m_fm = sb("m_fm", [128, 32 * TILE])
    kpair_t = sb("kpair", [128, H * CHT * 128])
    kpair = kpair_t[:]
    kp32 = kpair_t[:].bitcast(F32)
    xq = sb("xq", [128, 8 * TILE])
    xq32 = xq[:].bitcast(F32)
    Xs = sb("Xs", [128, 512])
    Ys = sb("Ys", [128, 512])
    Ps = sb("Ps", [128, 512])
    at_all = sb("at_all", [128, 4 * 512])
    att_all = sb("att_all", [128, 4 * 512])
    w0tm = sb("w0tm", [128, H * 128])
    vnewtm = sb("vnewtm", [128, H * 128])
    kde_s = sb("kde_s", [128, 2 * H * C])
    kbdn_s = sb("kbdn_s", [128, 2 * H * C])
    Sb = sb("Sb", [128, 2 * H * 128])
    S32 = sb("S32", [128, H * 128], F32)
    wsh = sb("wsh", [128, 8 * D])
    wbT_sb = sb("wbT_sb", [128, 8 * H])
    wbuf = [sb("wbuf0", [128, 8 * 512]), sb("wbuf1", [128, 8 * 512])]
    gu = sb("gu", [128, 8 * 512])
    maskA = sb("maskA", [128, 512])
    maskAT = sb("maskAT", [128, 512])
    maskLTI = sb("maskLTI", [128, 512])
    de_pat = sb("de_pat", [128, H * C])
    ipk = sb("ipk", [128, 512])
    idc = sb("idc", [128, 64])
    id128 = sb("id128", [128, 128])
    idf32 = sb("idf32", [128, 128], F32)
    ocol = sb("ocol", [128, 1])
    orow_f = sb("orow_f", [1, 128], F32)
    e16 = sb("e16", [16, 16])
    e8 = sb("e8", [8, 8])
    modcols = sb("modcols", [128, 64], F32)
    gacol = sb("gacol", [128, 8], F32)
    gbcol = sb("gbcol", [128, 8], F32)
    capcol = sb("capcol", [128, 8], F32)
    cbpcol = sb("cbpcol", [128, 8], F32)
    sccol_b = sb("sccol_b", [128, 8])
    rows = xq[:].bitcast(F32)[0:1, 0:1024]
    ctdcol = sb("ctdcol", [128, H], F32)
    dwcol = sb("dwcol", [128, H], F32)
    betacol = sb("betacol", [128, 32], F32)
    f1col = sb("f1col", [128, 32], F32)
    f2col = sb("f2col", [128, 32], F32)
    kdwfcol = sb("kdwfcol", [128, 32], F32)
    fac2 = sb("fac2", [128, 64], F32)
    Mfac = sb("Mfac", [16, 512])
    Mq = sb("Mq", [8, 128])
    tq = sb("tq", [128, 512])
    tq2 = sb("tq2", [128, 512])
    tq3 = sb("tq3", [128, 512])
    sbc = sb("sbc", [128, 512])
    sibc = sb("sibc", [128, 512])
    amaxc = sb("amaxc", [128, 8], F32)
    aucol = sb("aucol", [128, 1], F32)
    adcol = sb("adcol", [128, 1], F32)
    agcol = sb("agcol", [128, 1], F32)
    thrg = sb("thrg", [128, 1], F32)
    thru = sb("thru", [128, 1], F32)
    thrd = sb("thrd", [128, 1], F32)
    acc_abs = sb("acc_abs", [128, 4], F32)
    stat = sb("stat", [1, 1536], F32)
    epsc = sb("epsc", [1, 1], F32)
    onef = sb("onef", [128, 1], F32)
    magc = sb("magc", [128, 1], F32)
    nmagc = sb("nmagc", [128, 1], F32)
    negt = sb("negt", [128, 1], F32)
    cumrow = stat[0:1, 0:512]
    derow = stat[0:1, 512:1024]
    dwrow = stat[0:1, 1024:1536]

    mf32 = m_fm[:].bitcast(F32)

    pA = ps("pA", [128, 512])
    pB = ps("pB", [128, 1024])
    pC = ps("pC", [128, 512])
    pD = ps("pD", [128, 512])
    pE = ps("pE", [128, 512])
    pF = ps("pF", [128, 512])
    psT = ps("psT", [128, 1024], BF16)
    psT32 = psT[:].bitcast(F32)

    dx = Sem(stack.enter_context(nc.semaphore("dx")))
    dxt = Sem(stack.enter_context(nc.semaphore("dxt")))
    dwmP = [Sem(stack.enter_context(nc.semaphore("dwm0"))),
            Sem(stack.enter_context(nc.semaphore("dwm1")))]
    dstP = [Sem(stack.enter_context(nc.semaphore("dst0"))),
            Sem(stack.enter_context(nc.semaphore("dst1")))]
    pe = Sem(stack.enter_context(nc.semaphore("pe")))
    ve = Sem(stack.enter_context(nc.semaphore("ve")))
    ac = Sem(stack.enter_context(nc.semaphore("ac")))
    po = Sem(stack.enter_context(nc.semaphore("po")))
    so = Sem(stack.enter_context(nc.semaphore("so")))

    P = Prog()
    P.self_sem = {"vector": ve, "scalar": ac, "gpsimd": po}

    def mm(out, lhsT, rhs, start=True, stop=True, tp=None, waits=(), inc=None):
        P.emit("tensor", lambda t: t.matmul(out, lhsT, rhs, start=start, stop=stop,
                                            tile_position=tp), waits=waits, inc=inc)

    def penop(waits=()):
        assert not waits
        return P.inc_last("tensor", pe)

    def v_op(fn, waits=()):
        P.emit("vector", fn, waits=waits, inc=ve)
        return ve.n

    def a_op(fn, waits=()):
        P.emit("scalar", fn, waits=waits, inc=ac)
        return ac.n

    def g_op(fn, waits=()):
        P.emit("gpsimd", fn, waits=waits, inc=po)
        return po.n

    def s_dma(out, in_, waits=(), inc=None, selfwait=False):
        inc = inc or dx
        waits = list(waits)
        if selfwait and inc.n > 0:
            waits.append((inc, inc.n))
        P.emit("sync", lambda s: s.dma_start(out=out, in_=in_), waits=waits, inc=inc, inc_k=16)
        return (inc, inc.n)

    def g_dma(out, in_, waits=(), par=0, inc=None):
        inc = inc if inc is not None else dwmP[par]
        waits = list(waits)
        if inc.n > 0:
            waits.append((inc, inc.n))
        P.emit("gpsimd", lambda g: g.dma_start(out=out, in_=in_), waits=waits, inc=inc, inc_k=16)
        return (inc, inc.n)

    bank_free = {}
    _bkey = {}

    def bank_key(bank):
        return _bkey.get(id(bank), id(bank))

    def bank_wait(bank):
        k = bank_key(bank)
        return [bank_free[k]] if k in bank_free else []

    s_dma(ocol[:], ones_col_b[:])
    s_dma(orow_f[:], ones_row_f[:])
    s_dma(id128[:], ident128[:])
    s_dma(idc[:], identcol[:])
    s_dma(ipk[:], ipacked[:])
    s_dma(e16[:], e16rep[:])
    s_dma(e8[:], e8rep[:])
    s_dma(rows[0:1, 0:512], iota_rep[:])
    s_dma(rows[0:1, 512:520], dec[:])
    s_dma(maskA[:], trilS[:])
    s_dma(maskAT[:], triuS[:])
    s_dma(maskLTI[:], triuI[:])

    s_dma(xt[0][:], xT[:, 0:4096], inc=dxt, selfwait=True)
    dma0 = (dx, dx.n)
    dwA = g_dma(wbT_sb[:].rearrange("p (c h) -> p c h", c=8),
                wbT.rearrange("(c p) h -> p c h", p=128), par=0)

    v_op(lambda v: v.tensor_copy(idf32[:], id128[:]), waits=[dma0])
    v_op(lambda v: v.memset(epsc[:], 1e-5))
    v_op(lambda v: v.memset(onef[:], 1.0))
    v_op(lambda v: v.memset(magc[:], MAGIC))
    v_op(lambda v: v.memset(nmagc[:], -MAGIC))
    a1 = a_op(lambda s: s.activation(rows[0:1, 520:528], rows[0:1, 512:520], AF.Sigmoid),
              waits=[dma0])
    a1 = a_op(lambda s: s.activation(rows[0:1, 528:536], rows[0:1, 520:528], AF.Ln))
    v1 = v_op(lambda v: v.tensor_copy(
        cumrow.rearrange("r (h q) -> r h q", h=8),
        rows[0:1, 528:536].unsqueeze(2).broadcast_to([1, 8, 64])),
        waits=[(ac, a1)])
    v1 = v_op(lambda v: v.tensor_mul(cumrow, cumrow, rows[0:1, 0:512]))
    aexp = a_op(lambda s: s.activation(derow, cumrow, AF.Exp), waits=[(ve, v1)])
    v1 = v_op(lambda v: v.tensor_copy(
        rows[0:1, 0:512].rearrange("r (h q) -> r h q", h=8),
        cumrow[0:1, 63::64].unsqueeze(2).broadcast_to([1, 8, 64])))
    v1 = v_op(lambda v: v.tensor_sub(rows[0:1, 512:1024], rows[0:1, 0:512], cumrow))
    adw = a_op(lambda s: s.activation(dwrow, rows[0:1, 512:1024], AF.Exp), waits=[(ve, v1)])
    vneg = v_op(lambda v: v.tensor_scalar_mul(rows[0:1, 0:512], cumrow, -1.0))
    vneg = max(vneg, v1)

    def build_mask(first_rhs_is_ones, dsts):
        for half in range(2):
            for h in range(H):
                o = pE[half * 64:half * 64 + 64, h * 64:(h + 1) * 64]
                w0 = [(ve, vneg), (ac, adw)] if (half == 0 and h == 0) else ()
                if first_rhs_is_ones:
                    mm(o, cumrow[0:1, h * 64:(h + 1) * 64], orow_f[0:1, 0:64],
                       start=True, stop=False, tp=(0, half * 64), waits=w0)
                    mm(o, orow_f[0:1, 0:64], rows[0:1, h * 64:(h + 1) * 64],
                       start=False, stop=True, tp=(0, half * 64))
                else:
                    mm(o, rows[0:1, h * 64:(h + 1) * 64], orow_f[0:1, 0:64],
                       start=True, stop=False, tp=(0, half * 64), waits=w0)
                    mm(o, orow_f[0:1, 0:64], cumrow[0:1, h * 64:(h + 1) * 64],
                       start=False, stop=True, tp=(0, half * 64))
        p1 = penop()
        vm = v_op(lambda v: v.tensor_scalar_min(pE[:, 0:512], pE[:, 0:512], 0.0),
                  waits=[(pe, p1)])
        am = a_op(lambda s: s.activation(pF[:, 0:512], pE[:, 0:512], AF.Exp), waits=[(ve, vm)])
        vlast = 0
        for dstt in dsts:
            vlast = v_op(lambda v, dstt=dstt: v.tensor_mul(dstt[:], pF[:, 0:512], dstt[:]),
                         waits=[(ac, am)])
        return vlast

    vmA = build_mask(True, [maskA])
    P.wait("tensor", ve, vmA)
    vmT = build_mask(False, [maskAT, maskLTI])
    P.wait("tensor", ve, vmT)

    for h in range(H):
        mm(pD[:, h * 64:(h + 1) * 64], orow_f[:], derow[0:1, h * 64:(h + 1) * 64],
           waits=[(ac, aexp)] if h == 0 else ())
    p1 = penop()
    vdep = v_op(lambda v: v.tensor_copy(de_pat[:], pD[:, 0:512]), waits=[(pe, p1)])
    vctd = v_op(lambda v: v.tensor_copy(rows[0:1, 512:520], derow[0:1, 63::64]))
    mm(pD[:, 64:72], orow_f[:], rows[0:1, 512:520], waits=[(ve, vctd), (ve, vdep)])
    p1 = penop()
    vctd = v_op(lambda v: v.tensor_copy(ctdcol[:], pD[:, 64:72]), waits=[(pe, p1)])
    vr2 = 0
    for h in range(H):
        vr2 = v_op(lambda v, h=h: v.tensor_copy(
            rows[0:1, h * 128:(h + 1) * 128].rearrange("r (p q) -> r p q", p=2),
            dwrow[0:1, h * 64:(h + 1) * 64].unsqueeze(1).broadcast_to([1, 2, 64])),
            waits=[(ac, adw)] if h == 0 else ())
    for h in range(H):
        P.emit("tensor", lambda t, h=h: t.transpose(
            pD[:, 88 + h:89 + h], rows[0:1, h * 128:(h + 1) * 128], idf32[0:1, 0:1]),
            waits=[(ve, vr2), (ve, vctd)] if h == 0 else (), inc=pe)
    p1 = pe.n
    vdwc = v_op(lambda v: v.tensor_copy(dwcol[:], pD[:, 88:96]), waits=[(pe, p1)])

    dcr = s_dma(rows[0:1, 0:1024], c_row[:], waits=[(pe, pe.n)], selfwait=True)
    for k in range(8):
        P.emit("tensor", lambda t, k=k: t.transpose(
            pD[:, 72 + k:73 + k], rows[0:1, k * 128:(k + 1) * 128], idf32[0:1, 0:1]),
            waits=[dcr, (ve, vdwc)] if k == 0 else (), inc=pe)
    p1 = pe.n
    vcc = v_op(lambda v: v.tensor_copy(modcols[:, 40:48], pD[:, 72:80]), waits=[(pe, p1)])
    asc = a_op(lambda s: s.activation(sccol_b[:], modcols[:, 40:48], AF.Silu),
               waits=[(ve, vcc)])
    pk_prev = 0
    for k in range(8):
        piece = m_fm[:, (k % 2) * 6144:(k % 2) * 6144 + 6144]
        wts = [dwA] if k == 0 else [(pe, pk_prev)]
        dk = g_dma(piece, wmod[k * 128:(k + 1) * 128, :], waits=wts, par=k % 2)
        for mc in range(48):
            mm(pF[:, mc * 8 + k: mc * 8 + k + 1],
               piece[:, mc * 128:(mc + 1) * 128], sccol_b[:, k:k + 1],
               waits=[dk, (ac, asc)] if mc == 0 else ())
        pk_prev = P.inc_last("tensor", pe)
    p1 = pk_prev
    vmod = v_op(lambda v: v.tensor_reduce(
        modcols[:, 0:48], pF[:, 0:384].rearrange("p (m k) -> p m k", k=8), AX.X, ALU.add),
        waits=[(pe, p1)])
    v_op(lambda v: v.tensor_scalar_add(capcol[:], modcols[:, 8:16], 1.0))
    v_op(lambda v: v.tensor_scalar_add(cbpcol[:], modcols[:, 32:40], 1.0))
    for dstc, off in ((gacol, 16), (gbcol, 40)):
        v_op(lambda v, off=off: v.tensor_scalar(fac2[:, 0:8], modcols[:, off:off + 8],
                                                0.0, 2.0, ALU.is_ge, ALU.mult))
        vg = v_op(lambda v: v.tensor_scalar_sub(fac2[:, 0:8], fac2[:, 0:8], 1.0))
        ag = a_op(lambda s, off=off: s.activation(fac2[:, 8:16], modcols[:, off:off + 8], AF.Abs),
                  waits=[(ve, vg)])
        v_op(lambda v: v.tensor_scalar_max(fac2[:, 8:16], fac2[:, 8:16], 0.25),
             waits=[(ac, ag)])
        v_op(lambda v, dstc=dstc: v.tensor_mul(dstc[:], fac2[:, 0:8], fac2[:, 8:16]))
    mod_ready = ve.n
    P.wait("tensor", ve, mod_ready)


    def tern_matrix(src, scr, thr_col, acol, store):
        R, CC = src.shape
        npieces = R // 128
        vlast = ve.n
        entry_w = [(ve, ve.n), (po, po.n), (pe, pe.n)]
        for p in range(npieces):
            buf = mf32[:, (p % 2) * CC:(p % 2) * CC + CC]
            dk = g_dma(buf, src[p * 128:(p + 1) * 128, :],
                       waits=[(ve, vlast)] if p >= 2 else entry_w, par=p % 2)
            vlast = v_op(lambda v, buf=buf, p=p: v.tensor_reduce(
                acc_abs[:, 1 + p % 2:2 + p % 2], buf, AX.X, ALU.add,
                apply_absolute_value=True), waits=[dk])
            if p == 0:
                vlast = v_op(lambda v: v.tensor_copy(acc_abs[:, 0:1], acc_abs[:, 1:2]))
            else:
                vlast = v_op(lambda v, p=p: v.tensor_add(
                    acc_abs[:, 0:1], acc_abs[:, 0:1], acc_abs[:, 1 + p % 2:2 + p % 2]))
            yield
        mm(pF[0:1, 0:1], acc_abs[:, 0:1], onef[:],
           waits=[(ve, vlast)] + bank_wait(pF))
        p1 = penop()
        vthr = v_op(lambda v: v.tensor_scalar(stat[0:1, 0:1], pF[0:1, 0:1],
                                              1.0 / (R * CC), 1e-5, ALU.mult, ALU.max),
                    waits=[(pe, p1)])
        mm(pF[:, 4:5], orow_f[:], stat[0:1, 0:1], waits=[(ve, vthr)])
        p1 = penop()
        v_op(lambda v: v.tensor_copy(acol[:], pF[:, 4:5]), waits=[(pe, p1)])
        v_op(lambda v: v.tensor_scalar_mul(thr_col[:], pF[:, 4:5], 0.5))
        vthr = v_op(lambda v: v.tensor_scalar_mul(negt[:], thr_col[:], -1.0))
        bank_free[bank_key(pF)] = (ve, vthr)
        plast = po.n
        entry_pe = pe.n
        for p in range(npieces):
            buf = mf32[:, (p % 2) * CC:(p % 2) * CC + CC]
            wts = [(ve, vthr), (pe, entry_pe)] if p < 2 else [(ve, plast)]
            dk = g_dma(buf, src[p * 128:(p + 1) * 128, :], waits=wts, par=p % 2)
            v_op(lambda v, buf=buf: v.tensor_single_scalar(gu[:, 0:CC], buf,
                                                           thr_col[:], ALU.is_gt),
                 waits=[dk, (dstP[0], dstP[0].n), (dstP[1], dstP[1].n)])
            v_op(lambda v, buf=buf: v.tensor_single_scalar(xq[:, 0:CC], buf,
                                                           negt[:], ALU.is_lt))
            plast = v_op(lambda v, CC=CC: v.tensor_sub(gu[:, 0:CC], gu[:, 0:CC],
                                                       xq[:, 0:CC]))
            store(scr, p, [(ve, plast)], dstP[p % 2])
            yield
        return

    def store_gu(scr, p, waits, inc):
        out = scr.rearrange("(og q) (k o) -> q og k o", og=8, k=8)[:, :, p, :]
        in_ = gu[:, 0:INNER].rearrange("q (og o) -> q og o", og=8)
        g_dma(out, in_, waits=waits, inc=inc)

    def store_d(scr, p, waits, inc):
        kg, k = p // 8, p % 8
        out = scr.rearrange("(b q) (k o) -> q b k o", b=8, k=8)[:, kg * 2:kg * 2 + 2, k, :]
        in_ = gu[:, 0:D].rearrange("q (b o) -> q b o", b=2)
        g_dma(out, in_, waits=waits, inc=inc)

    def tern_gen():
        for a in tern_matrix(wgT, sc_g, thrg, agcol, store_gu):
            yield
        for a in tern_matrix(wuT, sc_u, thru, aucol, store_gu):
            yield
        for a in tern_matrix(wdT, sc_d, thrd, adcol, store_d):
            yield
    tern_ve = ve.n

    v_op(lambda v: v.memset(S32[:], 0.0))
    init_ve = v_op(lambda v: v.memset(Sb[:], 0.0))

    dwWW = g_dma(wsh[:], wwT[:], par=0)

    wbuf_ms = [0, 0]

    ffn_ms = {}

    def tile_gen(tile):
        xtile = xt[tile % 2]
        t0 = tile * TILE
        P.wait("vector", dxt, 16 * (tile + 1))
        P.wait("scalar", dxt, 16 * (tile + 1))
        P.wait("tensor", dxt, 16 * (tile + 1))
        if tile == 0:
            P.wait("vector", *dma0)
            P.wait("scalar", *dma0)
            P.wait("gpsimd", ve, init_ve)

        def ln_pass(capc, shoff, vpre, apre, xtile=None, exw=(), xdone=None, fold=None):
            sqdone = []
            for c in range(8):
                sl = slice(c * TILE, (c + 1) * TILE)
                if c % 2 == 0:
                    a_op(lambda s, sl=sl, xtile=xtile: s.activation(kpair[:, sl], xtile[:, sl],
                                                                    AF.Square),
                         waits=(([(ac, apre)] + list(exw)) if c == 0 else [])
                         + ([xdone[c]] if xdone else []))
                    sqdone.append((ac, ac.n))
                else:
                    v_op(lambda v, sl=sl, xtile=xtile: v.tensor_mul(kpair[:, sl], xtile[:, sl],
                                                                    xtile[:, sl]),
                         waits=([(ve, vpre)] + list(exw)) if c == 1 else ())
                    sqdone.append((ve, ve.n))
            for c in range(8):
                mm(pE[0:1, 0:512], onef[:], xtile[:, c * TILE:(c + 1) * TILE],
                   start=(c == 0), stop=(c == 7),
                   waits=(bank_wait(pE) if c == 0 else [])
                   + ([xdone[c]] if xdone else []))
            for c in range(8):
                mm(pF[0:1, 0:512], ocol[:], kpair[:, c * TILE:(c + 1) * TILE],
                   start=(c == 0), stop=(c == 7),
                   waits=([sqdone[c]] + (bank_wait(pF) if c == 0 else [])))
            p1 = penop()
            v_op(lambda v: v.tensor_scalar_mul(stat[0:1, 0:512], pE[0:1, 0:512], 1.0 / D),
                 waits=[(pe, p1)])
            v_op(lambda v: v.tensor_scalar_mul(stat[0:1, 512:1024], pF[0:1, 0:512], 1.0 / D))
            v_op(lambda v: v.tensor_mul(stat[0:1, 1024:1536], stat[0:1, 0:512], stat[0:1, 0:512]))
            vv = v_op(lambda v: v.tensor_sub(stat[0:1, 512:1024], stat[0:1, 512:1024],
                                             stat[0:1, 1024:1536]))
            aa = a_op(lambda s: s.activation(stat[0:1, 1024:1536], stat[0:1, 512:1024],
                                             AF.Sqrt, bias=epsc[0:1, 0:1]), waits=[(ve, vv)])
            v_op(lambda v: v.reciprocal(stat[0:1, 512:1024], stat[0:1, 1024:1536]),
                 waits=[(ac, aa)])
            vv = v_op(lambda v: v.tensor_mul(stat[0:1, 1024:1536], stat[0:1, 0:512],
                                             stat[0:1, 512:1024]))
            mm(pE[:, 0:512], orow_f[:], stat[0:1, 512:1024], waits=[(ve, vv)])
            mm(pF[:, 0:512], orow_f[:], stat[0:1, 1024:1536])
            p1 = penop()
            alast = 0
            aprev = 0
            rbc = pE[:, 0:512].unsqueeze(1).broadcast_to([128, 2, 512])
            cbc = pF[:, 0:512].unsqueeze(1).broadcast_to([128, 2, 512])
            for cp in range(4):
                sl = slice(cp * 1024, (cp + 1) * 1024)
                aw = ([(pe, p1)] if cp == 0 else []) + ([(ac, aprev)] if aprev else [])
                v_op(lambda v, sl=sl, xtile=xtile: v.tensor_mul(
                    kp32[:, 0:1024].rearrange("p (a q) -> p a q", a=2),
                    xtile[:, sl].rearrange("p (a q) -> p a q", a=2), rbc), waits=aw)
                vap = v_op(lambda v: v.tensor_sub(
                    kp32[:, 0:1024].rearrange("p (a q) -> p a q", a=2),
                    kp32[:, 0:1024].rearrange("p (a q) -> p a q", a=2), cbc))
                for c in (2 * cp, 2 * cp + 1):
                    sl1 = slice(c * TILE, (c + 1) * TILE)
                    ssl = slice((c % 2) * TILE, (c % 2) * TILE + TILE)
                    alast = a_op(lambda s, sl1=sl1, ssl=ssl, c=c: s.activation(
                        h_fm[:, sl1], kp32[:, ssl], AF.Identity,
                        bias=modcols[:, shoff + c:shoff + c + 1],
                        scale=capc[:, c:c + 1]), waits=[(ve, vap)])
                aprev = alast
                if fold is not None:
                    fold(cp, alast)
            bank_free[bank_key(pE)] = (ve, ve.n)
            bank_free[bank_key(pF)] = (ve, ve.n)
            P.wait("tensor", ve, ve.n)
            P.wait("tensor", ac, alast)
            return alast

        yield
        if tile == 0:
            P.wait("vector", ve, mod_ready)
            h_ready = ln_pass(capcol, 0, tern_ve, ac.n, xtile=xtile)
        else:
            h_ready = ln_pass(capcol, 0, ve.n, ac.n, xtile=xtile,
                              exw=[(pe, pe.n), (po, po.n)])
        yield

        for tc in range(4):
            for k in range(8):
                mm(pE[:, tc * 8:(tc + 1) * 8],
                   h_fm[:, k * TILE + tc * 128: k * TILE + tc * 128 + 128],
                   wbT_sb[:, k * 8:(k + 1) * 8], start=(k == 0), stop=(k == 7),
                   waits=([(ac, h_ready), dwA] + bank_wait(pE))
                   if (tc == 0 and k == 0) else ())
        p1 = penop()
        ab = a_op(lambda s: s.activation(betacol[:], pE[:, 0:32], AF.Sigmoid),
                  waits=[(pe, p1)])
        bank_free[bank_key(pE)] = (ac, ab)

        hsqdone = []
        for c in range(8):
            sl = slice(c * TILE, (c + 1) * TILE)
            if c % 2 == 0:
                a_op(lambda s, sl=sl: s.activation(kpair[:, sl], h_fm[:, sl], AF.Square))
                hsqdone.append((ac, ac.n))
            else:
                v_op(lambda v, sl=sl: v.tensor_mul(kpair[:, sl], h_fm[:, sl], h_fm[:, sl]),
                     waits=[(ac, h_ready)] if c == 1 else [])
                hsqdone.append((ve, ve.n))
        for h in range(H):
            for tc in range(4):
                mm(pF[:, tc * 8 + h: tc * 8 + h + 1],
                   kpair[:, h * TILE + tc * 128: h * TILE + tc * 128 + 128], ocol[:],
                   waits=([hsqdone[h]] + (bank_wait(pF) if h == 0 else [])) if tc == 0 else ())
        p1 = penop()
        aa = a_op(lambda s: s.activation(f1col[:], pF[:, 0:32], AF.Sqrt),
                  waits=[(pe, p1)])
        v_op(lambda v: v.tensor_scalar_max(f1col[:], f1col[:], 1e-12),
             waits=[(ac, aa)])
        f1v = fac2[:, 0:64].rearrange("p (t f h) -> p t f h", t=4, f=2)[:, :, 0, :]
        f2v = fac2[:, 0:64].rearrange("p (t f h) -> p t f h", t=4, f=2)[:, :, 1, :]
        v_op(lambda v: v.reciprocal(f1v, f1col[:].rearrange("p (t h) -> p t h", t=4)))
        v_op(lambda v: v.scalar_tensor_tensor(f2v, betacol[:].rearrange("p (t h) -> p t h", t=4),
                                              -1.0, f1v, ALU.mult, ALU.mult))
        vfac = v_op(lambda v: v.tensor_mul(
            kdwfcol[:].rearrange("p (t h) -> p t h", t=4), f1v,
            dwcol[:].unsqueeze(1).broadcast_to([128, 4, 8])))
        bank_free[bank_key(pF)] = (ve, vfac)
        yield

        for tc in range(4):
            for k in range(8):
                lh = h_fm[:, k * TILE + tc * 128: k * TILE + tc * 128 + 128]
                for half in range(2):
                    mm(pB[:, half * 512:(half + 1) * 512], lh,
                       wsh[:, k * D + half * 512: k * D + (half + 1) * 512],
                       start=(k == 0), stop=(k == 7),
                       waits=([dwWW] + bank_wait(pB))
                       if (tc == 0 and k == 0 and half == 0) else ())
            p1 = penop()
            for h in range(H):
                v_op(lambda v, tc=tc, h=h: v.tensor_single_scalar(
                    vtm[:, tc * D + h * 128: tc * D + (h + 1) * 128],
                    pB[:, h * 128:(h + 1) * 128],
                    betacol[:, tc * 8 + h: tc * 8 + h + 1], ALU.mult),
                    waits=[(ac, ab), (pe, p1)] if h == 0 else ())
            bank_free[bank_key(pB)] = (ve, ve.n)
            P.wait("tensor", ve, ve.n)
            if tc == 1:
                for fc in range(4):
                    inp = fac2[:, fc * 16:(fc + 1) * 16]
                    P.emit("tensor", lambda t, inp=inp, fc=fc: t.transpose(
                        pD[0:16, fc * 128:(fc + 1) * 128], inp, idf32[:]),
                        waits=([(ve, vfac)] + bank_wait(pD)) if fc == 0 else (), inc=pe)
                p1t = pe.n
                vM = v_op(lambda v: v.tensor_copy(Mfac[:], pD[0:16, 0:512]),
                          waits=[(pe, p1t)])
                bank_free[bank_key(pD)] = (ve, vM)
            yield
        vvtm = ve.n
        P.wait("tensor", ve, vM)
        yield
        pBa = pB[:, 0:512]
        pBb = pB[:, 512:1024]
        _bkey[id(pBa)] = bank_key(pB)
        _bkey[id(pBb)] = bank_key(pB)
        _bkey[id(psT32)] = bank_key(psT)
        kbanks = [(pE, pF), (pBa, pBb)]
        for h in range(H):
            bE, bF = kbanks[h % 2]
            mm(bE[:, 0:512], e16[:, h:h + 1].broadcast_to([16, 128]), Mfac[:],
               waits=bank_wait(bE))
            mm(bF[:, 0:512], e16[:, 8 + h:9 + h].broadcast_to([16, 128]), Mfac[:],
               waits=bank_wait(bF))
            p1 = penop()
            hsl = h_fm[:, h * TILE:(h + 1) * TILE].rearrange("p (c q) -> p c q", c=8)
            kcell = kpair[:, h * 8 * 128:(h + 1) * 8 * 128].rearrange("p (c q) -> p c q", c=8)
            v_op(lambda v, hsl=hsl, kcell=kcell, bE=bE: v.tensor_mul(
                kcell[:, :, 0:64], hsl, bE[:, 0:512].rearrange("p (c q) -> p c q", c=8)),
                waits=[(pe, p1)])
            vkb = v_op(lambda v, hsl=hsl, kcell=kcell, bF=bF: v.tensor_mul(
                kcell[:, :, 64:128], hsl, bF[:, 0:512].rearrange("p (c q) -> p c q", c=8)))
            bank_free[bank_key(bE)] = (ve, vkb)
            bank_free[bank_key(bF)] = (ve, vkb)
            if h % 2 == 1:
                yield
        P.wait("tensor", ve, vkb)
        vkp = ve.n

        for tc in range(4):
            for h in range(H):
                P.emit("tensor", lambda t, tc=tc, h=h: t.transpose(
                    psT[:, h * 128:(h + 1) * 128],
                    h_fm[:, h * TILE + tc * 128: h * TILE + tc * 128 + 128], id128[:]),
                    waits=([(ac, h_ready)] + bank_wait(psT)) if (tc == 0 and h == 0) else (),
                    inc=pe)
            p1 = pe.n
            for h in range(H):
                v_op(lambda v, tc=tc, h=h: v.tensor_single_scalar(
                    kdw_tm[:, tc * D + h * 128: tc * D + (h + 1) * 128],
                    psT[:, h * 128:(h + 1) * 128],
                    kdwfcol[:, tc * 8 + h: tc * 8 + h + 1], ALU.mult),
                    waits=[(pe, p1)] if h == 0 else ())
            bank_free[bank_key(psT)] = (ve, ve.n)
            P.wait("tensor", ve, ve.n)
            yield

        pBa = pB[:, 0:512]
        pBb = pB[:, 512:1024]
        _bkey[id(pBa)] = bank_key(pB)
        _bkey[id(pBb)] = bank_key(pB)
        _bkey[id(psT32)] = bank_key(psT)

        def a_raws(n, b0, b1, b2):
            par = n % 2
            phs = slice(par * 64, par * 64 + 64)
            for h in range(H):
                cell = kpair[:, (h * 8 + n) * 128:(h * 8 + n + 1) * 128]
                mm(b0[phs, h * 64:(h + 1) * 64], cell[:, 64:128], cell[:, 0:64],
                   tp=(0, par * 64),
                   waits=(([(ve, vkp)] if n == 0 else []) + bank_wait(b0)
                          + bank_wait(b1) + bank_wait(b2)) if h == 0 else ())
            for h in range(H):
                cell = kpair[:, (h * 8 + n) * 128:(h * 8 + n + 1) * 128]
                mm(b1[phs, h * 64:(h + 1) * 64], cell[:, 0:64], cell[:, 0:64],
                   tp=(0, par * 64))
            for h in range(H):
                cell = kpair[:, (h * 8 + n) * 128:(h * 8 + n + 1) * 128]
                mm(b2[phs, h * 64:(h + 1) * 64], cell[:, 0:64], cell[:, 64:128],
                   tp=(0, par * 64))
            return penop()

        def a_inits(n, b0, b1, b2, praw):
            par = n % 2
            phs = slice(par * 64, par * 64 + 64)
            v_op(lambda v, phs=phs, b0=b0: v.tensor_mul(Xs[phs, :], b0[phs, 0:512],
                                                        maskA[phs, :]), waits=[(pe, praw)])
            v_op(lambda v, phs=phs, n=n, b1=b1: v.tensor_mul(
                att_all[phs, (n // 2) * 512:(n // 2 + 1) * 512], b1[phs, 0:512],
                maskLTI[phs, :]))
            v_op(lambda v, phs=phs, b2=b2: v.tensor_mul(Ys[phs, :], b2[phs, 0:512],
                                                        maskAT[phs, :]))
            return v_op(lambda v, phs=phs: v.tensor_add(Ps[phs, :], Ys[phs, :], ipk[phs, :]))

        def a_xy(n, j, b0, b2, vrdy):
            par = n % 2
            phs = slice(par * 64, par * 64 + 64)
            for h in range(H):
                mm(b0[phs, h * 64:(h + 1) * 64], Ys[phs, h * 64:(h + 1) * 64],
                   Xs[phs, h * 64:(h + 1) * 64], tp=(par * 64, par * 64),
                   waits=[(ve, vrdy), (ac, ac.n)] if h == 0 else ())
            if j < 5:
                for h in range(H):
                    mm(b2[phs, h * 64:(h + 1) * 64], Xs[phs, h * 64:(h + 1) * 64],
                       Ys[phs, h * 64:(h + 1) * 64], tp=(par * 64, par * 64))
            return penop()

        def a_xyev(n, j, b0, b2, pxy):
            par = n % 2
            phs = slice(par * 64, par * 64 + 64)
            aX = a_op(lambda s, phs=phs, b0=b0: s.activation(Xs[phs, :], b0[phs, 0:512],
                                                             AF.Identity), waits=[(pe, pxy)])
            if j < 5:
                aX = a_op(lambda s, phs=phs, b2=b2: s.activation(Ys[phs, :], b2[phs, 0:512],
                                                                 AF.Identity))
            return aX

        def a_pmm(n, b1, aX):
            par = n % 2
            phs = slice(par * 64, par * 64 + 64)
            for h in range(H):
                mm(b1[phs, h * 64:(h + 1) * 64], Xs[phs, h * 64:(h + 1) * 64],
                   Ps[phs, h * 64:(h + 1) * 64], tp=(par * 64, par * 64),
                   waits=[(ac, aX)] if h == 0 else ())
            return penop()

        def a_pev(n, j, b1, ppm):
            par = n % 2
            phs = slice(par * 64, par * 64 + 64)
            if j < 5:
                return v_op(lambda v, phs=phs, b1=b1: v.tensor_add(Ps[phs, :], Ps[phs, :],
                                                                   b1[phs, 0:512]),
                            waits=[(pe, ppm)])
            return v_op(lambda v, phs=phs, n=n, b1=b1: v.tensor_add(
                at_all[phs, (n // 2) * 512:(n // 2 + 1) * 512], Ps[phs, :],
                b1[phs, 0:512]), waits=[(pe, ppm)])

        bt_even = (pD, pE, pF)
        bt_odd = (pBa, psT32, pBb)
        for pair in range(4):
            c0, c1 = 2 * pair, 2 * pair + 1
            praw0 = a_raws(c0, *bt_even)
            praw1 = a_raws(c1, *bt_odd)
            v0 = a_inits(c0, *bt_even, praw0)
            v1 = a_inits(c1, *bt_odd, praw1)
            for j in range(1, 6):
                pxy0 = a_xy(c0, j, bt_even[0], bt_even[2], v0)
                pxy1 = a_xy(c1, j, bt_odd[0], bt_odd[2], v1)
                aX0 = a_xyev(c0, j, bt_even[0], bt_even[2], pxy0)
                aX1 = a_xyev(c1, j, bt_odd[0], bt_odd[2], pxy1)
                ppm0 = a_pmm(c0, bt_even[1], aX0)
                ppm1 = a_pmm(c1, bt_odd[1], aX1)
                v0 = a_pev(c0, j, bt_even[1], ppm0)
                v1 = a_pev(c1, j, bt_odd[1], ppm1)
            for b in bt_even:
                bank_free[bank_key(b)] = (ve, v0)
            for b in bt_odd:
                bank_free[bank_key(b)] = (ve, v1)
            yield
        bank_free[bank_key(pB)] = (ve, v1)
        vchainA = ve.n

        dwR0 = g_dma(wbuf[0][:], wrT[:, 0:4096], waits=[(pe, wbuf_ms[0])], par=0)
        dwR1 = g_dma(wbuf[1][:], wrT[:, 4096:8192], waits=[(pe, wbuf_ms[1])], par=1)
        mgw = m_fm[:, 8192:12288]
        muw = m_fm[:, 12288:16384]
        if tile > 0:
            dg0 = g_dma(mgw, sc_g[0:128, :], waits=[ffn_ms["downpe"]], par=0)
            dg1 = g_dma(muw, sc_u[0:128, :], waits=[ffn_ms["downpe"]], par=1)
        def mk_kde(n, wts):
            kf_c = kpair.rearrange("p (h c q) -> p h c q", h=8, c=8)[:, :, n, 0:64]
            kb_c = kpair.rearrange("p (h c q) -> p h c q", h=8, c=8)[:, :, n, 64:128]
            dep = de_pat[:].rearrange("p (h q) -> p h q", h=8)
            pb = n % 2
            g_op(lambda g, kf_c=kf_c, dep=dep, pb=pb: g.tensor_mul(
                kde_s[:, pb * 512:(pb + 1) * 512].rearrange("p (h q) -> p h q", h=8),
                kf_c, dep), waits=wts)
            return g_op(lambda g, kb_c=kb_c, dep=dep, pb=pb: g.tensor_mul(
                kbdn_s[:, pb * 512:(pb + 1) * 512].rearrange("p (h q) -> p h q", h=8),
                kb_c, dep))

        def b_o_mms(n):
            par = n % 2
            tc = n // 2
            phs = slice(par * 64, par * 64 + 64)
            sb_v = Sb[:, (par) * 1024:(par) * 1024 + 1024]
            for h in range(H):
                o1w = bank_wait(psT) if h == 0 else ()
                mm(psT32[:, h * 64:(h + 1) * 64], sb_v[:, h * 128:(h + 1) * 128],
                   kde_s[:, par * 512 + h * 64: par * 512 + (h + 1) * 64],
                   start=True, stop=False, waits=o1w)
                mm(psT32[:, h * 64:(h + 1) * 64], vnewtm[phs, h * 128:(h + 1) * 128],
                   att_all[phs, (n // 2) * 512 + h * 64:(n // 2) * 512 + (h + 1) * 64],
                   start=False, stop=True, tp=(par * 64, 0))
            return penop()

        def b_o_evac(n, po1):
            v_op(lambda v, n=n: v.tensor_copy(
                o_fm[:].rearrange("p (h t) -> p h t", h=8)[:, :, n * 64:(n + 1) * 64],
                psT32[:, 0:512].rearrange("p (h q) -> p h q", h=8)), waits=[(pe, po1)])
            bank_free[bank_key(psT)] = (ve, ve.n)

        pkde = mk_kde(0, [(ve, vchainA)])
        prev_o = None
        for n in range(CHT):
            par = n % 2
            tc = n // 2
            phs = slice(par * 64, par * 64 + 64)
            sb_v = Sb[:, par * 1024:par * 1024 + 1024]
            for h in range(H):
                w0w = ([(po, pkde)] + bank_wait(pB)) if h == 0 else []
                if n == 0 and h == 0:
                    w0w = w0w + [(ve, vvtm)]
                mm(pB[phs, h * 128:(h + 1) * 128],
                   kbdn_s[:, par * 512 + h * 64: par * 512 + (h + 1) * 64],
                   sb_v[:, h * 128:(h + 1) * 128], start=True, stop=False, tp=(0, par * 64),
                   waits=w0w)
                mm(pB[phs, h * 128:(h + 1) * 128], idc[phs, :],
                   vtm[phs, tc * D + h * 128: tc * D + (h + 1) * 128],
                   start=False, stop=True, tp=(par * 64, par * 64))
            p1 = penop()
            aw0 = a_op(lambda s, phs=phs: s.activation(w0tm[phs, 0:512], pB[phs, 0:512],
                                                       AF.Identity), waits=[(pe, p1)])
            vw0 = v_op(lambda v, phs=phs: v.tensor_copy(w0tm[phs, 512:1024], pB[phs, 512:1024]),
                       waits=[(pe, p1)])
            for h in range(H):
                mm(pB[phs, h * 128:(h + 1) * 128],
                   at_all[phs, (n // 2) * 512 + h * 64:(n // 2) * 512 + (h + 1) * 64],
                   w0tm[phs, h * 128:(h + 1) * 128], tp=(par * 64, par * 64),
                   waits=[(ac, aw0), (ve, vw0)] if h == 0 else ())
            p1 = penop()
            avn = a_op(lambda s, phs=phs: s.activation(vnewtm[phs, 0:512], pB[phs, 0:512],
                                                       AF.Identity), waits=[(pe, p1)])
            vvn = v_op(lambda v, phs=phs: v.tensor_copy(vnewtm[phs, 512:1024], pB[phs, 512:1024]),
                       waits=[(pe, p1)])
            if prev_o is not None:
                po1 = b_o_mms(n - 1)
            if n + 1 < CHT:
                nkde = mk_kde(n + 1, [(pe, pe.n)])
            for h in range(H):
                mm(pB[:, h * 128:(h + 1) * 128],
                   kdw_tm[phs, tc * D + h * 128: tc * D + (h + 1) * 128],
                   vnewtm[phs, h * 128:(h + 1) * 128], tp=(par * 64, 0),
                   waits=[(ac, avn), (ve, vvn)] if h == 0 else ())
            pds = penop()
            if prev_o is not None:
                b_o_evac(n - 1, po1)
            P.wait("vector", pe, pds)
            for h in range(H):
                v_op(lambda v, h=h: v.scalar_tensor_tensor(
                    S32[:, h * 128:(h + 1) * 128], S32[:, h * 128:(h + 1) * 128],
                    ctdcol[:, h:h + 1], pB[:, h * 128:(h + 1) * 128],
                    ALU.mult, ALU.add))
            vS = ve.n
            bank_free[bank_key(pB)] = (ve, vS)
            v_op(lambda v, n=n: v.tensor_copy(Sb[:, ((n + 1) % 2) * 1024:((n + 1) % 2) * 1024 + 1024],
                                              S32[:]))
            P.wait("tensor", ve, ve.n)
            prev_o = n
            pkde = nkde if n + 1 < CHT else pkde
            yield
        po1 = b_o_mms(CHT - 1)
        b_o_evac(CHT - 1, po1)
        vchunks = ve.n
        achunks = ac.n

        res1done = []
        for f in range(8):
            bank = pE if f % 2 == 0 else pF
            for k in range(8):
                mm(bank[:, 0:512],
                   wbuf[k // 4][:, (k % 4) * D + f * 128: (k % 4) * D + f * 128 + 128],
                   o_fm[:, k * TILE:(k + 1) * TILE], start=(k == 0), stop=(k == 7),
                   waits=(([(ve, vchunks), (ac, achunks), dwR0, dwR1] if f == 0 else [])
                          + bank_wait(bank)) if k == 0 else ())
            p1 = penop()
            ae1 = a_op(lambda s, bank=bank, f=f: s.activation(tq3[:, 0:512], bank[:, 0:512],
                                                              AF.Identity,
                                                              scale=gacol[:, f:f + 1]),
                       waits=[(pe, p1), (ve, ve.n)])
            bank_free[bank_key(bank)] = (ac, ae1)
            v_op(lambda v, f=f, xtile=xtile: v.tensor_add(xtile[:, f * TILE:(f + 1) * TILE],
                                             xtile[:, f * TILE:(f + 1) * TILE], tq3[:, 0:512]),
                 waits=[(ac, ae1)])
            res1done.append((ve, v_op(lambda v, f=f, xtile=xtile: v.scalar_tensor_tensor(
                xtile[:, f * TILE:(f + 1) * TILE], h_fm[:, f * TILE:(f + 1) * TILE],
                gacol[:, f:f + 1], xtile[:, f * TILE:(f + 1) * TILE], ALU.mult, ALU.add))))
            if f % 2 == 1:
                yield
        vres1 = ve.n
        wbuf_ms[0] = wbuf_ms[1] = pe.n
        P.wait("scalar", po, po.n)
        P.wait("vector", po, po.n)

        h2_ready = ln_pass(cbpcol, 24, vres1, ac.n, xtile=xtile, xdone=res1done)

        yield "q1barrier"
        if tile + 1 < NT:
            s_dma(xt[(tile + 1) % 2][:], xT[:, (tile + 1) * 4096:(tile + 2) * 4096],
                  waits=[(so, so.n)], inc=dxt, selfwait=True)
        v_op(lambda v: v.tensor_tensor(tq[:], h_fm[:, 0:512], h_fm[:, 512:1024], ALU.max),
             waits=[(ac, h2_ready), ffn_ms.get("dancepe", (pe, 0))])
        v_op(lambda v: v.tensor_tensor(tq[:], tq[:], h_fm[:, 1024:1536], ALU.max))
        v_op(lambda v: v.tensor_tensor(tq[:], tq[:], h_fm[:, 1536:2048], ALU.max))
        v_op(lambda v: v.tensor_tensor(sbc[:], h_fm[:, 0:512], h_fm[:, 512:1024], ALU.min))
        v_op(lambda v: v.tensor_tensor(sbc[:], sbc[:], h_fm[:, 1024:1536], ALU.min))
        v_op(lambda v: v.tensor_tensor(sbc[:], sbc[:], h_fm[:, 1536:2048], ALU.min))
        v_op(lambda v: v.tensor_tensor(tq2[:], h_fm[:, 2048:2560], h_fm[:, 2560:3072], ALU.max))
        v_op(lambda v: v.tensor_tensor(tq2[:], tq2[:], h_fm[:, 3072:3584], ALU.max))
        v_op(lambda v: v.tensor_tensor(tq2[:], tq2[:], h_fm[:, 3584:4096], ALU.max))
        v_op(lambda v: v.tensor_tensor(sibc[:], h_fm[:, 2048:2560], h_fm[:, 2560:3072], ALU.min))
        v_op(lambda v: v.tensor_tensor(sibc[:], sibc[:], h_fm[:, 3072:3584], ALU.min))
        v_op(lambda v: v.tensor_tensor(sibc[:], sibc[:], h_fm[:, 3584:4096], ALU.min))
        v_op(lambda v: v.tensor_tensor(tq[:], tq[:], tq2[:], ALU.max))
        v_op(lambda v: v.tensor_tensor(sbc[:], sbc[:], sibc[:], ALU.min))
        vq = v_op(lambda v: v.scalar_tensor_tensor(tq[:], sbc[:], -1.0,
                                                   tq[:], ALU.mult, ALU.max))

        def quant_dance(vq, col_xform):
            for cell in range(4):
                P.emit("tensor", lambda t, cell=cell: t.transpose(
                    psT[:, cell * 128:(cell + 1) * 128], tq[:, cell * 128:(cell + 1) * 128],
                    id128[:]),
                    waits=([(ve, vq)] + bank_wait(psT)) if cell == 0 else (), inc=pe)
            p1 = pe.n
            for cell in range(4):
                v_op(lambda v, cell=cell: v.tensor_reduce(
                    amaxc[:, cell:cell + 1], psT[:, cell * 128:(cell + 1) * 128], AX.X, ALU.max),
                    waits=[(pe, p1)] if cell == 0 else ())
            bank_free[bank_key(psT)] = (ve, ve.n)
            vq2 = col_xform()
            P.emit("tensor", lambda t: t.transpose(pD[0:8, 0:128], amaxc[:, 0:8], idf32[:]),
                   waits=[(ve, vq2)] + bank_wait(pD), inc=pe)
            p1 = pe.n
            vq2 = v_op(lambda v: v.tensor_copy(Mq[:], pD[0:8, 0:128]), waits=[(pe, p1)])
            bank_free[bank_key(pD)] = (ve, vq2)
            P.wait("tensor", ve, vq2)
            for cell in range(4):
                mm(pE[:, cell * 128:(cell + 1) * 128], e8[:, 4 + cell:5 + cell].broadcast_to([8, 128]),
                   Mq[:], waits=bank_wait(pE) if cell == 0 else ())
                mm(pF[:, cell * 128:(cell + 1) * 128], e8[:, cell:cell + 1].broadcast_to([8, 128]),
                   Mq[:], waits=bank_wait(pF) if cell == 0 else ())
            p1 = penop()
            v_op(lambda v: v.tensor_copy(sbc[:], pE[:, 0:512]), waits=[(pe, p1)])
            vbc = v_op(lambda v: v.tensor_copy(sibc[:], pF[:, 0:512]))
            bank_free[bank_key(pE)] = (ve, vbc)
            bank_free[bank_key(pF)] = (ve, vbc)
            P.wait("tensor", ve, vbc)
            return vbc

        def q1cols():
            v_op(lambda v: v.tensor_scalar_max(amaxc[:, 0:4], amaxc[:, 0:4], 1e-5))
            v_op(lambda v: v.reciprocal(amaxc[:, 4:8], amaxc[:, 0:4]))
            v_op(lambda v: v.tensor_scalar_mul(amaxc[:, 4:8], amaxc[:, 4:8], 127.0))
            return v_op(lambda v: v.tensor_scalar_mul(amaxc[:, 0:4], amaxc[:, 0:4], 1.0 / 127.0))

        quant_dance(vq, q1cols)
        for cp in range(4):
            sl = slice(cp * 1024, (cp + 1) * 1024)
            bc2 = sbc[:].unsqueeze(1).broadcast_to([128, 2, 512])
            bc2i = sibc[:].unsqueeze(1).broadcast_to([128, 2, 512])
            v_op(lambda v, sl=sl, bc2=bc2: v.tensor_mul(
                kpair[:, 0:1024].rearrange("p (a q) -> p a q", a=2),
                h_fm[:, sl].rearrange("p (a q) -> p a q", a=2), bc2))
            v_op(lambda v: v.tensor_scalar(kpair[:, 0:1024], kpair[:, 0:1024],
                                           MAGIC, -MAGIC, ALU.add, ALU.add))
            v_op(lambda v, sl=sl, bc2i=bc2i: v.tensor_mul(
                xq[:, sl].rearrange("p (a q) -> p a q", a=2),
                kpair[:, 0:1024].rearrange("p (a q) -> p a q", a=2), bc2i),
                waits=[ffn_ms["qb"]] if (cp == 0 and "qb" in ffn_ms) else [])
        vxq = ve.n
        yield "ffnstart"

        if tile == 0:
            P.wait("gpsimd", dstP[0], dstP[0].n)
            P.wait("gpsimd", dstP[1], dstP[1].n)
        q1pe = pe.n
        rot = [pA, pC]
        ri = 0
        qm = ffn_ms.get("qmul", (ve, 0))
        mml = {0: qm, 1: qm, 2: qm, 3: qm}
        for og in range(8):
            for which, scr in ((0, sc_g), (1, sc_u)):
                li = og * 2 + which
                early = tile > 0 and li < 2
                if early:
                    buf = mgw if which == 0 else muw
                    dk = dg0 if which == 0 else dg1
                else:
                    buf = wbuf[li % 2][:]
                    dk = g_dma(buf, scr[og * 128:(og + 1) * 128, :],
                               waits=[(pe, wbuf_ms[li % 2])], par=li % 2)
                for osub in range(4):
                    bank = rot[ri % 2]
                    ri += 1
                    for k in range(8):
                        mm(bank[:, 0:512], buf[:, k * 512 + osub * 128: k * 512 + osub * 128 + 128],
                           xq[:, k * TILE:(k + 1) * TILE], start=(k == 0), stop=(k == 7),
                           waits=([dk, (ve, vxq)] + bank_wait(bank)) if k == 0 else ())
                    p1 = penop()
                    if osub == 3 and not early:
                        wbuf_ms[li % 2] = p1
                    if which == 0:
                        aev = a_op(lambda s, osub=osub, bank=bank: s.activation(
                            gu[:, osub * 512:(osub + 1) * 512], bank[:, 0:512], AF.Silu,
                            scale=agcol[:, 0:1]), waits=[(pe, p1), mml[osub]])
                    else:
                        aev = a_op(lambda s, osub=osub, bank=bank: s.activation(
                            gu[:, 2048 + osub * 512:2048 + (osub + 1) * 512], bank[:, 0:512],
                            AF.Identity), waits=[(pe, p1)])
                        oc = og * 4 + osub
                        mml[osub] = (ve, v_op(lambda v, osub=osub, oc=oc: v.tensor_mul(
                            m_fm[:, oc * 512:(oc + 1) * 512], gu[:, osub * 512:(osub + 1) * 512],
                            gu[:, 2048 + osub * 512:2048 + (osub + 1) * 512]),
                            waits=[(ac, aev)]))
                        if oc == 1:
                            v_op(lambda v: v.tensor_tensor(
                                tq[:], m_fm[:, 0:512], m_fm[:, 512:1024], ALU.max),
                                waits=[(pe, q1pe)])
                            v_op(lambda v: v.tensor_tensor(
                                sbc[:], m_fm[:, 0:512], m_fm[:, 512:1024], ALU.min))
                        elif oc >= 2:
                            v_op(lambda v, oc=oc: v.tensor_tensor(
                                tq[:], tq[:], m_fm[:, oc * 512:(oc + 1) * 512], ALU.max))
                            v_op(lambda v, oc=oc: v.tensor_tensor(
                                sbc[:], sbc[:], m_fm[:, oc * 512:(oc + 1) * 512], ALU.min))
                    bank_free[bank_key(bank)] = (ac, aev)
            yield
        pm = ve.n

        vq = v_op(lambda v: v.scalar_tensor_tensor(tq[:], sbc[:], -1.0,
                                                   tq[:], ALU.mult, ALU.max),
                  waits=[(ve, pm)])

        def q2cols():
            v_op(lambda v: v.tensor_single_scalar(amaxc[:, 0:4], amaxc[:, 0:4],
                                                  aucol[:, 0:1], ALU.mult))
            v_op(lambda v: v.tensor_scalar_max(amaxc[:, 0:4], amaxc[:, 0:4], 1e-5))
            v_op(lambda v: v.reciprocal(amaxc[:, 4:8], amaxc[:, 0:4]))
            v_op(lambda v: v.tensor_scalar_mul(amaxc[:, 4:8], amaxc[:, 4:8], 127.0))
            v_op(lambda v: v.tensor_single_scalar(amaxc[:, 4:8], amaxc[:, 4:8],
                                                  aucol[:, 0:1], ALU.mult))
            v_op(lambda v: v.tensor_single_scalar(amaxc[:, 0:4], amaxc[:, 0:4],
                                                  adcol[:, 0:1], ALU.mult))
            return v_op(lambda v: v.tensor_scalar_mul(amaxc[:, 0:4], amaxc[:, 0:4], 1.0 / 127.0))

        quant_dance(vq, q2cols)
        ffn_ms["dancepe"] = (pe, pe.n)
        rms = []
        amag = [ac.n, ac.n]
        pegu = pe.n
        for ip in range(16):
            sl = slice(ip * 1024, (ip + 1) * 1024)
            qa = gu[:, (ip % 2) * 2048:(ip % 2) * 2048 + 1024]
            qb = xq32[:, (ip % 2) * 1024:(ip % 2) * 1024 + 1024]
            bc2 = sbc[:].unsqueeze(1).broadcast_to([128, 2, 512])
            v_op(lambda v, sl=sl, bc2=bc2, qa=qa: v.tensor_mul(
                qa.rearrange("p (a q) -> p a q", a=2),
                m_fm[:, sl].rearrange("p (a q) -> p a q", a=2), bc2),
                waits=[(ac, amag[ip % 2])])
            a_op(lambda s, qa=qa, qb=qb: s.activation(qb, qa, AF.Identity, bias=magc[:]),
                 waits=[(ve, ve.n)] + ([(pe, pegu)] if ip < 2 else []))
            amag[ip % 2] = a_op(lambda s, sl=sl, qb=qb: s.activation(
                m_fm[:, sl], qb, AF.Identity, bias=nmagc[:]))
            if ip % 4 == 3:
                rms.append((ac, amag[ip % 2]))
        ffn_ms["qb"] = (ac, ac.n)
        ffn_ms["qmul"] = (ve, ve.n)
        vq2v = ve.n
        yield

        for fg in range(2):
            banks = [pA, pC, pD, pE]
            for kg in range(4):
                buf = wbuf[kg % 2]
                dk = g_dma(buf[:], sc_d[(kg * 2 + fg) * 128:(kg * 2 + fg + 1) * 128, :],
                           waits=[(pe, wbuf_ms[kg % 2])], par=kg % 2)
                for fsub in range(4):
                    bw = (bank_wait(banks[0]) + bank_wait(banks[1]) + bank_wait(banks[2])
                          + bank_wait(banks[3])) if (kg == 0 and fsub == 0) else []
                    for k in range(8):
                        ic = kg * 8 + k
                        mm(banks[fsub][:, 0:512],
                           buf[:, k * 512 + fsub * 128: k * 512 + fsub * 128 + 128],
                           m_fm[:, ic * 512:(ic + 1) * 512],
                           start=(kg == 0 and k == 0), stop=(kg == 3 and k == 7),
                           waits=([dk, rms[kg]] + bw)
                           if (fsub == 0 and k == 0) else
                           (bw if k == 0 else ()))
                wbuf_ms[kg % 2] = P.inc_last("tensor", pe)
            p1 = pe.n
            ffn_ms["downpe"] = (pe, pe.n)
            for fsub in range(4):
                f = fg * 4 + fsub
                v_op(lambda v, fsub=fsub, banks=banks: v.tensor_mul(
                    tq[:], banks[fsub][:, 0:512], sibc[:]),
                    waits=[(pe, p1)] if fsub == 0 else [])
                vr = v_op(lambda v, f=f, xtile=xtile: v.scalar_tensor_tensor(
                    xtile[:, f * TILE:(f + 1) * TILE], tq[:], gbcol[:, f:f + 1],
                    xtile[:, f * TILE:(f + 1) * TILE], ALU.mult, ALU.add))
                bank_free[bank_key(banks[fsub])] = (ve, vr)
            P.wait("tensor", ve, ve.n)
        vfin = ve.n
        s_dma(outT[:, tile * 4096:(tile + 1) * 4096], xtile[:],
              waits=[(ve, vfin)], inc=so, selfwait=True)

    pend, rate = tern_gen(), 2
    for tile in range(NT):
        g = tile_gen(tile)
        while True:
            try:
                mark = next(g)
            except StopIteration:
                break
            if mark == "q1barrier":
                if pend is not None:
                    for _ in pend:
                        pass
                    pend = None
                continue
            if mark == "ffnstart":
                break
            if pend is not None:
                for _ in range(rate):
                    if next(pend, "END") == "END":
                        pend = None
                        break
        pend, rate = g, 1
    if pend is not None:
        for _ in pend:
            pass

    P.wait("sync", so, so.n)

    with nc.Block() as block:
        @block.sync
        def _(e):
            P.replay("sync", e)

        @block.tensor
        def _(e):
            P.replay("tensor", e)

        @block.vector
        def _(e):
            P.replay("vector", e)

        @block.scalar
        def _(e):
            P.replay("scalar", e)

        @block.gpsimd
        def _(e):
            P.replay("gpsimd", e)

    stack.close()
    return nc


def host_consts():
    bf = ml_dtypes.bfloat16
    iota = np.tile(np.arange(1, C + 1, dtype=np.float32), H)[None, :]
    i64 = np.eye(64, dtype=np.float32)
    ipacked = np.zeros((128, 512), np.float32)
    trilS = np.zeros((128, 512), np.float32)
    triuS = np.zeros((128, 512), np.float32)
    triuI = np.zeros((128, 512), np.float32)
    tl = np.tril(np.ones((64, 64), np.float32), -1)
    ti = np.tril(np.ones((64, 64), np.float32))
    for half in range(2):
        for h in range(H):
            slr = slice(half * 64, half * 64 + 64)
            slc = slice(h * 64, (h + 1) * 64)
            ipacked[slr, slc] = i64
            trilS[slr, slc] = tl
            triuS[slr, slc] = tl.T
            triuI[slr, slc] = ti.T
    return {
        "iota_rep": iota,
        "ones_row_f": np.ones((1, 128), np.float32),
        "ones_col_b": np.ones((128, 1), bf),
        "ident128": np.eye(128, dtype=bf),
        "identcol": np.vstack([np.eye(64), np.eye(64)]).astype(bf),
        "ipacked": ipacked.astype(bf),
        "trilS": trilS.astype(bf),
        "triuS": triuS.astype(bf),
        "triuI": triuI.astype(bf),
        "e16rep": np.eye(16, dtype=bf),
        "e8rep": np.eye(8, dtype=bf),
    }


def pack_w(w):
    return np.ascontiguousarray(w.reshape(8, 128, D).transpose(1, 0, 2).reshape(128, 8 * D))


def pack_x(xt):
    return np.ascontiguousarray(
        xt.reshape(8, 128, 8, 512).transpose(1, 2, 0, 3).reshape(128, 8 * 4096))


def unpack_out(o):
    return np.ascontiguousarray(
        o.reshape(128, 8, 8, 512).transpose(2, 0, 1, 3).reshape(D, T).T)


def kernel(L0_attn_out, L0_attn_W_write, L0_attn_W_read, L0_attn_W_beta, L0_attn_decay,
           L0_ffn_w_gate, L0_ffn_w_up, L0_ffn_w_down, L0_c, L0_wmod):
    shared = host_consts()
    shared.update({
        "wwT": pack_w(np.asarray(L0_attn_W_write).T),
        "wrT": pack_w(np.asarray(L0_attn_W_read).T),
        "wbT": np.ascontiguousarray(np.asarray(L0_attn_W_beta).T),
        "dec": np.ascontiguousarray(np.asarray(L0_attn_decay).reshape(1, H)),
        "wmod": np.ascontiguousarray(np.asarray(L0_wmod)),
        "wgT": np.ascontiguousarray(np.asarray(L0_ffn_w_gate).T),
        "wuT": np.ascontiguousarray(np.asarray(L0_ffn_w_up).T),
        "wdT": np.ascontiguousarray(np.asarray(L0_ffn_w_down).T),
    })
    x = np.asarray(L0_attn_out)
    cc = np.asarray(L0_c)
    in_maps = []
    for b in range(8):
        m = dict(shared)
        m["xT"] = pack_x(np.asarray(x[b]).T)
        m["c_row"] = np.ascontiguousarray(cc[b].reshape(1, D))
        in_maps.append(m)
    nc = build_nc()
    res = run_bass_kernel_spmd(nc, in_maps, core_ids=list(range(8)))
    out = np.stack([unpack_out(np.asarray(r["outT"])) for r in res.results], axis=0)
    return out.astype(np.float32)


# version 22 (final)
# speedup vs baseline: 1.0192x; 1.0002x over previous
import numpy as np
import ml_dtypes
import concourse.bass as bass
import concourse.mybir as mybir
from concourse.bass_utils import run_bass_kernel_spmd

F32 = mybir.dt.float32
BF16 = mybir.dt.bfloat16
AF = mybir.ActivationFunctionType
ALU = mybir.AluOpType
AX = mybir.AxisListType

T = 4096
D = 1024
H = 8
C = 64
TILE = 512
NT = T // TILE
CHT = TILE // C
INNER = 4096
MAGIC = 12582912.0


class Sem:
    def __init__(self, h):
        self.h = h
        self.n = 0


class Prog:
    ENGINES = ("sync", "tensor", "vector", "scalar", "gpsimd")

    def __init__(self):
        self.ops = {e: [] for e in self.ENGINES}
        self.self_sem = {}

    def emit(self, eng, fn, waits=(), inc=None, inc_k=1):
        waits = list(waits)
        ss = self.self_sem.get(eng)
        if ss is not None and ss.n > 0 and inc is ss:
            waits.append((ss, ss.n))
        wl = [(s.h, v) for (s, v) in waits]
        tgt = None
        if inc is not None:
            inc.n += inc_k
            tgt = (inc.h, inc_k)
        self.ops[eng].append([fn, wl, tgt])

    def inc_last(self, eng, sem, k=1):
        rec = self.ops[eng][-1]
        assert isinstance(rec, list) and rec[2] is None
        sem.n += k
        rec[2] = (sem.h, k)
        return sem.n

    def wait(self, eng, sem, val=None):
        v = sem.n if val is None else val
        h = sem.h
        self.ops[eng].append(lambda e: e.wait_ge(h, v))

    def replay(self, eng, e):
        for rec in self.ops[eng]:
            if isinstance(rec, list):
                fn, wl, tgt = rec
                for h, v in wl:
                    e.wait_ge(h, v)
                r = fn(e)
                if tgt is not None:
                    r.then_inc(tgt[0], tgt[1])
            else:
                rec(e)


def build_nc():
    nc = bass.Bass("TRN2", target_bir_lowering=False)

    def din(name, shape, dt=F32):
        return nc.dram_tensor(name, shape, dt, kind="ExternalInput").ap()

    xT = din("xT", [128, 8 * 4096])
    wwT = din("wwT", [128, 8 * D])
    wrT = din("wrT", [128, 8 * D])
    wbT = din("wbT", [D, H])
    dec = din("dec", [1, H])
    c_row = din("c_row", [1, D])
    wmod = din("wmod", [D, 6 * D])
    wgT = din("wgT", [D, INNER])
    wuT = din("wuT", [D, INNER])
    wdT = din("wdT", [INNER, D])
    iota_rep = din("iota_rep", [1, 512])
    ones_row_f = din("ones_row_f", [1, 128])
    ones_col_b = din("ones_col_b", [128, 1], BF16)
    ident128 = din("ident128", [128, 128], BF16)
    identcol = din("identcol", [128, 64], BF16)
    ipacked = din("ipacked", [128, 512], BF16)
    trilS = din("trilS", [128, 512], BF16)
    triuS = din("triuS", [128, 512], BF16)
    triuI = din("triuI", [128, 512], BF16)
    e16rep = din("e16rep", [16, 16], BF16)
    e8rep = din("e8rep", [8, 8], BF16)

    outT = nc.dram_tensor("outT", [128, 8 * 4096], F32, kind="ExternalOutput").ap()
    sc_g = nc.dram_tensor("sc_g", [D, INNER], BF16, kind="Internal").ap()
    sc_u = nc.dram_tensor("sc_u", [D, INNER], BF16, kind="Internal").ap()
    sc_d = nc.dram_tensor("sc_d", [D, INNER], BF16, kind="Internal").ap()

    import contextlib
    stack = contextlib.ExitStack()

    def sb(name, shape, dt=BF16):
        return stack.enter_context(nc.sbuf_tensor(name, shape, dt))

    def ps(name, shape, dt=F32):
        return stack.enter_context(nc.psum_tensor(name, shape, dt))

    xt = [sb("xt0", [128, 8 * TILE], F32), sb("xt1", [128, 8 * TILE], F32)]
    h_fm = sb("h_fm", [128, 8 * TILE])
    kdw_tm = sb("kdw_tm", [128, 4 * D])
    vtm = sb("vtm", [128, 4 * D])
    o_fm = sb("o_fm", [128, H * TILE])
    m_fm = sb("m_fm", [128, 32 * TILE])
    kpair_t = sb("kpair", [128, H * CHT * 128])
    kpair = kpair_t[:]
    kp32 = kpair_t[:].bitcast(F32)
    xq = sb("xq", [128, 8 * TILE])
    xq32 = xq[:].bitcast(F32)
    Xs = sb("Xs", [128, 512])
    Ys = sb("Ys", [128, 512])
    Ps = sb("Ps", [128, 512])
    at_all = sb("at_all", [128, 4 * 512])
    att_all = sb("att_all", [128, 4 * 512])
    w0tm = sb("w0tm", [128, H * 128])
    vnewtm = sb("vnewtm", [128, H * 128])
    kde_s = sb("kde_s", [128, 2 * H * C])
    kbdn_s = sb("kbdn_s", [128, 2 * H * C])
    Sb = sb("Sb", [128, 2 * H * 128])
    S32 = sb("S32", [128, H * 128], F32)
    wsh = sb("wsh", [128, 8 * D])
    wbT_sb = sb("wbT_sb", [128, 8 * H])
    wbuf = [sb("wbuf0", [128, 8 * 512]), sb("wbuf1", [128, 8 * 512])]
    gu = sb("gu", [128, 8 * 512])
    maskA = sb("maskA", [128, 512])
    maskAT = sb("maskAT", [128, 512])
    maskLTI = sb("maskLTI", [128, 512])
    de_pat = sb("de_pat", [128, H * C])
    ipk = sb("ipk", [128, 512])
    idc = sb("idc", [128, 64])
    id128 = sb("id128", [128, 128])
    idf32 = sb("idf32", [128, 128], F32)
    ocol = sb("ocol", [128, 1])
    orow_f = sb("orow_f", [1, 128], F32)
    e16 = sb("e16", [16, 16])
    e8 = sb("e8", [8, 8])
    modcols = sb("modcols", [128, 64], F32)
    gacol = sb("gacol", [128, 8], F32)
    gbcol = sb("gbcol", [128, 8], F32)
    capcol = sb("capcol", [128, 8], F32)
    cbpcol = sb("cbpcol", [128, 8], F32)
    sccol_b = sb("sccol_b", [128, 8])
    rows = xq[:].bitcast(F32)[0:1, 0:1024]
    ctdcol = sb("ctdcol", [128, H], F32)
    dwcol = sb("dwcol", [128, H], F32)
    betacol = sb("betacol", [128, 32], F32)
    f1col = sb("f1col", [128, 32], F32)
    f2col = sb("f2col", [128, 32], F32)
    kdwfcol = sb("kdwfcol", [128, 32], F32)
    fac2 = sb("fac2", [128, 64], F32)
    Mfac = sb("Mfac", [16, 512])
    Mq = sb("Mq", [8, 128])
    tq = sb("tq", [128, 512])
    tq2 = sb("tq2", [128, 512])
    tq3 = sb("tq3", [128, 512])
    sbc = sb("sbc", [128, 512])
    sibc = sb("sibc", [128, 512])
    amaxc = sb("amaxc", [128, 8], F32)
    aucol = sb("aucol", [128, 1], F32)
    adcol = sb("adcol", [128, 1], F32)
    agcol = sb("agcol", [128, 1], F32)
    thrg = sb("thrg", [128, 1], F32)
    thru = sb("thru", [128, 1], F32)
    thrd = sb("thrd", [128, 1], F32)
    acc_abs = sb("acc_abs", [128, 4], F32)
    stat = sb("stat", [1, 1536], F32)
    epsc = sb("epsc", [1, 1], F32)
    onef = sb("onef", [128, 1], F32)
    magc = sb("magc", [128, 1], F32)
    nmagc = sb("nmagc", [128, 1], F32)
    negt = sb("negt", [128, 1], F32)
    cumrow = stat[0:1, 0:512]
    derow = stat[0:1, 512:1024]
    dwrow = stat[0:1, 1024:1536]

    mf32 = m_fm[:].bitcast(F32)

    pA = ps("pA", [128, 512])
    pB = ps("pB", [128, 1024])
    pC = ps("pC", [128, 512])
    pD = ps("pD", [128, 512])
    pE = ps("pE", [128, 512])
    pF = ps("pF", [128, 512])
    psT = ps("psT", [128, 1024], BF16)
    psT32 = psT[:].bitcast(F32)

    dx = Sem(stack.enter_context(nc.semaphore("dx")))
    dxt = Sem(stack.enter_context(nc.semaphore("dxt")))
    dwmP = [Sem(stack.enter_context(nc.semaphore("dwm0"))),
            Sem(stack.enter_context(nc.semaphore("dwm1")))]
    dstP = [Sem(stack.enter_context(nc.semaphore("dst0"))),
            Sem(stack.enter_context(nc.semaphore("dst1")))]
    pe = Sem(stack.enter_context(nc.semaphore("pe")))
    ve = Sem(stack.enter_context(nc.semaphore("ve")))
    ac = Sem(stack.enter_context(nc.semaphore("ac")))
    po = Sem(stack.enter_context(nc.semaphore("po")))
    so = Sem(stack.enter_context(nc.semaphore("so")))

    P = Prog()
    P.self_sem = {"vector": ve, "scalar": ac, "gpsimd": po}

    def mm(out, lhsT, rhs, start=True, stop=True, tp=None, waits=(), inc=None):
        P.emit("tensor", lambda t: t.matmul(out, lhsT, rhs, start=start, stop=stop,
                                            tile_position=tp), waits=waits, inc=inc)

    def penop(waits=()):
        assert not waits
        return P.inc_last("tensor", pe)

    def v_op(fn, waits=()):
        P.emit("vector", fn, waits=waits, inc=ve)
        return ve.n

    def a_op(fn, waits=()):
        P.emit("scalar", fn, waits=waits, inc=ac)
        return ac.n

    def g_op(fn, waits=()):
        P.emit("gpsimd", fn, waits=waits, inc=po)
        return po.n

    def s_dma(out, in_, waits=(), inc=None, selfwait=False):
        inc = inc or dx
        waits = list(waits)
        if selfwait and inc.n > 0:
            waits.append((inc, inc.n))
        P.emit("sync", lambda s: s.dma_start(out=out, in_=in_), waits=waits, inc=inc, inc_k=16)
        return (inc, inc.n)

    def g_dma(out, in_, waits=(), par=0, inc=None):
        inc = inc if inc is not None else dwmP[par]
        waits = list(waits)
        if inc.n > 0:
            waits.append((inc, inc.n))
        P.emit("gpsimd", lambda g: g.dma_start(out=out, in_=in_), waits=waits, inc=inc, inc_k=16)
        return (inc, inc.n)

    bank_free = {}
    _bkey = {}

    def bank_key(bank):
        return _bkey.get(id(bank), id(bank))

    def bank_wait(bank):
        k = bank_key(bank)
        return [bank_free[k]] if k in bank_free else []

    s_dma(ocol[:], ones_col_b[:])
    s_dma(orow_f[:], ones_row_f[:])
    s_dma(id128[:], ident128[:])
    s_dma(idc[:], identcol[:])
    s_dma(ipk[:], ipacked[:])
    s_dma(e16[:], e16rep[:])
    s_dma(e8[:], e8rep[:])
    s_dma(rows[0:1, 0:512], iota_rep[:])
    s_dma(rows[0:1, 512:520], dec[:])
    s_dma(maskA[:], trilS[:])
    s_dma(maskAT[:], triuS[:])
    s_dma(maskLTI[:], triuI[:])

    s_dma(xt[0][:], xT[:, 0:4096], inc=dxt, selfwait=True)
    dma0 = (dx, dx.n)
    dwA = g_dma(wbT_sb[:].rearrange("p (c h) -> p c h", c=8),
                wbT.rearrange("(c p) h -> p c h", p=128), par=0)

    v_op(lambda v: v.tensor_copy(idf32[:], id128[:]), waits=[dma0])
    v_op(lambda v: v.memset(epsc[:], 1e-5))
    v_op(lambda v: v.memset(onef[:], 1.0))
    v_op(lambda v: v.memset(magc[:], MAGIC))
    v_op(lambda v: v.memset(nmagc[:], -MAGIC))
    a1 = a_op(lambda s: s.activation(rows[0:1, 520:528], rows[0:1, 512:520], AF.Sigmoid),
              waits=[dma0])
    a1 = a_op(lambda s: s.activation(rows[0:1, 528:536], rows[0:1, 520:528], AF.Ln))
    v1 = v_op(lambda v: v.tensor_copy(
        cumrow.rearrange("r (h q) -> r h q", h=8),
        rows[0:1, 528:536].unsqueeze(2).broadcast_to([1, 8, 64])),
        waits=[(ac, a1)])
    v1 = v_op(lambda v: v.tensor_mul(cumrow, cumrow, rows[0:1, 0:512]))
    aexp = a_op(lambda s: s.activation(derow, cumrow, AF.Exp), waits=[(ve, v1)])
    v1 = v_op(lambda v: v.tensor_copy(
        rows[0:1, 0:512].rearrange("r (h q) -> r h q", h=8),
        cumrow[0:1, 63::64].unsqueeze(2).broadcast_to([1, 8, 64])))
    v1 = v_op(lambda v: v.tensor_sub(rows[0:1, 512:1024], rows[0:1, 0:512], cumrow))
    adw = a_op(lambda s: s.activation(dwrow, rows[0:1, 512:1024], AF.Exp), waits=[(ve, v1)])
    vneg = v_op(lambda v: v.tensor_scalar_mul(rows[0:1, 0:512], cumrow, -1.0))
    vneg = max(vneg, v1)

    def build_mask(first_rhs_is_ones, dsts):
        for half in range(2):
            for h in range(H):
                o = pE[half * 64:half * 64 + 64, h * 64:(h + 1) * 64]
                w0 = [(ve, vneg), (ac, adw)] if (half == 0 and h == 0) else ()
                if first_rhs_is_ones:
                    mm(o, cumrow[0:1, h * 64:(h + 1) * 64], orow_f[0:1, 0:64],
                       start=True, stop=False, tp=(0, half * 64), waits=w0)
                    mm(o, orow_f[0:1, 0:64], rows[0:1, h * 64:(h + 1) * 64],
                       start=False, stop=True, tp=(0, half * 64))
                else:
                    mm(o, rows[0:1, h * 64:(h + 1) * 64], orow_f[0:1, 0:64],
                       start=True, stop=False, tp=(0, half * 64), waits=w0)
                    mm(o, orow_f[0:1, 0:64], cumrow[0:1, h * 64:(h + 1) * 64],
                       start=False, stop=True, tp=(0, half * 64))
        p1 = penop()
        vm = v_op(lambda v: v.tensor_scalar_min(pE[:, 0:512], pE[:, 0:512], 0.0),
                  waits=[(pe, p1)])
        am = a_op(lambda s: s.activation(pF[:, 0:512], pE[:, 0:512], AF.Exp), waits=[(ve, vm)])
        vlast = 0
        for dstt in dsts:
            vlast = v_op(lambda v, dstt=dstt: v.tensor_mul(dstt[:], pF[:, 0:512], dstt[:]),
                         waits=[(ac, am)])
        return vlast

    vmA = build_mask(True, [maskA])
    P.wait("tensor", ve, vmA)
    vmT = build_mask(False, [maskAT, maskLTI])
    P.wait("tensor", ve, vmT)

    for h in range(H):
        mm(pD[:, h * 64:(h + 1) * 64], orow_f[:], derow[0:1, h * 64:(h + 1) * 64],
           waits=[(ac, aexp)] if h == 0 else ())
    p1 = penop()
    vdep = v_op(lambda v: v.tensor_copy(de_pat[:], pD[:, 0:512]), waits=[(pe, p1)])
    vctd = v_op(lambda v: v.tensor_copy(rows[0:1, 512:520], derow[0:1, 63::64]))
    mm(pD[:, 64:72], orow_f[:], rows[0:1, 512:520], waits=[(ve, vctd), (ve, vdep)])
    p1 = penop()
    vctd = v_op(lambda v: v.tensor_copy(ctdcol[:], pD[:, 64:72]), waits=[(pe, p1)])
    vr2 = 0
    for h in range(H):
        vr2 = v_op(lambda v, h=h: v.tensor_copy(
            rows[0:1, h * 128:(h + 1) * 128].rearrange("r (p q) -> r p q", p=2),
            dwrow[0:1, h * 64:(h + 1) * 64].unsqueeze(1).broadcast_to([1, 2, 64])),
            waits=[(ac, adw)] if h == 0 else ())
    for h in range(H):
        P.emit("tensor", lambda t, h=h: t.transpose(
            pD[:, 88 + h:89 + h], rows[0:1, h * 128:(h + 1) * 128], idf32[0:1, 0:1]),
            waits=[(ve, vr2), (ve, vctd)] if h == 0 else (), inc=pe)
    p1 = pe.n
    vdwc = v_op(lambda v: v.tensor_copy(dwcol[:], pD[:, 88:96]), waits=[(pe, p1)])

    dcr = s_dma(rows[0:1, 0:1024], c_row[:], waits=[(pe, pe.n)], selfwait=True)
    for k in range(8):
        P.emit("tensor", lambda t, k=k: t.transpose(
            pD[:, 72 + k:73 + k], rows[0:1, k * 128:(k + 1) * 128], idf32[0:1, 0:1]),
            waits=[dcr, (ve, vdwc)] if k == 0 else (), inc=pe)
    p1 = pe.n
    vcc = v_op(lambda v: v.tensor_copy(modcols[:, 40:48], pD[:, 72:80]), waits=[(pe, p1)])
    asc = a_op(lambda s: s.activation(sccol_b[:], modcols[:, 40:48], AF.Silu),
               waits=[(ve, vcc)])
    pk_prev = 0
    for k in range(8):
        piece = m_fm[:, (k % 2) * 6144:(k % 2) * 6144 + 6144]
        wts = [dwA] if k == 0 else [(pe, pk_prev)]
        dk = g_dma(piece, wmod[k * 128:(k + 1) * 128, :], waits=wts, par=k % 2)
        for mc in range(48):
            mm(pF[:, mc * 8 + k: mc * 8 + k + 1],
               piece[:, mc * 128:(mc + 1) * 128], sccol_b[:, k:k + 1],
               waits=[dk, (ac, asc)] if mc == 0 else ())
        pk_prev = P.inc_last("tensor", pe)
    p1 = pk_prev
    vmod = v_op(lambda v: v.tensor_reduce(
        modcols[:, 0:48], pF[:, 0:384].rearrange("p (m k) -> p m k", k=8), AX.X, ALU.add),
        waits=[(pe, p1)])
    v_op(lambda v: v.tensor_scalar_add(capcol[:], modcols[:, 8:16], 1.0))
    v_op(lambda v: v.tensor_scalar_add(cbpcol[:], modcols[:, 32:40], 1.0))
    for dstc, off in ((gacol, 16), (gbcol, 40)):
        v_op(lambda v, off=off: v.tensor_scalar(fac2[:, 0:8], modcols[:, off:off + 8],
                                                0.0, 2.0, ALU.is_ge, ALU.mult))
        vg = v_op(lambda v: v.tensor_scalar_sub(fac2[:, 0:8], fac2[:, 0:8], 1.0))
        ag = a_op(lambda s, off=off: s.activation(fac2[:, 8:16], modcols[:, off:off + 8], AF.Abs),
                  waits=[(ve, vg)])
        v_op(lambda v: v.tensor_scalar_max(fac2[:, 8:16], fac2[:, 8:16], 0.25),
             waits=[(ac, ag)])
        v_op(lambda v, dstc=dstc: v.tensor_mul(dstc[:], fac2[:, 0:8], fac2[:, 8:16]))
    mod_ready = ve.n
    P.wait("tensor", ve, mod_ready)


    def tern_matrix(src, scr, thr_col, acol, store):
        R, CC = src.shape
        npieces = R // 128
        vlast = ve.n
        entry_w = [(ve, ve.n), (po, po.n), (pe, pe.n)]
        for p in range(npieces):
            buf = mf32[:, (p % 2) * CC:(p % 2) * CC + CC]
            dk = g_dma(buf, src[p * 128:(p + 1) * 128, :],
                       waits=[(ve, vlast)] if p >= 2 else entry_w, par=p % 2)
            vlast = v_op(lambda v, buf=buf, p=p: v.tensor_reduce(
                acc_abs[:, 1 + p % 2:2 + p % 2], buf, AX.X, ALU.add,
                apply_absolute_value=True), waits=[dk])
            if p == 0:
                vlast = v_op(lambda v: v.tensor_copy(acc_abs[:, 0:1], acc_abs[:, 1:2]))
            else:
                vlast = v_op(lambda v, p=p: v.tensor_add(
                    acc_abs[:, 0:1], acc_abs[:, 0:1], acc_abs[:, 1 + p % 2:2 + p % 2]))
            yield
        mm(pF[0:1, 0:1], acc_abs[:, 0:1], onef[:],
           waits=[(ve, vlast)] + bank_wait(pF))
        p1 = penop()
        vthr = v_op(lambda v: v.tensor_scalar(stat[0:1, 0:1], pF[0:1, 0:1],
                                              1.0 / (R * CC), 1e-5, ALU.mult, ALU.max),
                    waits=[(pe, p1)])
        mm(pF[:, 4:5], orow_f[:], stat[0:1, 0:1], waits=[(ve, vthr)])
        p1 = penop()
        v_op(lambda v: v.tensor_copy(acol[:], pF[:, 4:5]), waits=[(pe, p1)])
        v_op(lambda v: v.tensor_scalar_mul(thr_col[:], pF[:, 4:5], 0.5))
        vthr = v_op(lambda v: v.tensor_scalar_mul(negt[:], thr_col[:], -1.0))
        bank_free[bank_key(pF)] = (ve, vthr)
        plast = po.n
        entry_pe = pe.n
        for p in range(npieces):
            buf = mf32[:, (p % 2) * CC:(p % 2) * CC + CC]
            wts = [(ve, vthr), (pe, entry_pe)] if p < 2 else [(ve, plast)]
            dk = g_dma(buf, src[p * 128:(p + 1) * 128, :], waits=wts, par=p % 2)
            v_op(lambda v, buf=buf: v.tensor_single_scalar(gu[:, 0:CC], buf,
                                                           thr_col[:], ALU.is_gt),
                 waits=[dk, (dstP[0], dstP[0].n), (dstP[1], dstP[1].n)])
            v_op(lambda v, buf=buf: v.tensor_single_scalar(xq[:, 0:CC], buf,
                                                           negt[:], ALU.is_lt))
            plast = v_op(lambda v, CC=CC: v.tensor_sub(gu[:, 0:CC], gu[:, 0:CC],
                                                       xq[:, 0:CC]))
            store(scr, p, [(ve, plast)], dstP[p % 2])
            yield
        return

    def store_gu(scr, p, waits, inc):
        out = scr.rearrange("(og q) (k o) -> q og k o", og=8, k=8)[:, :, p, :]
        in_ = gu[:, 0:INNER].rearrange("q (og o) -> q og o", og=8)
        g_dma(out, in_, waits=waits, inc=inc)

    def store_d(scr, p, waits, inc):
        kg, k = p // 8, p % 8
        out = scr.rearrange("(b q) (k o) -> q b k o", b=8, k=8)[:, kg * 2:kg * 2 + 2, k, :]
        in_ = gu[:, 0:D].rearrange("q (b o) -> q b o", b=2)
        g_dma(out, in_, waits=waits, inc=inc)

    def tern_gen():
        for a in tern_matrix(wgT, sc_g, thrg, agcol, store_gu):
            yield
        for a in tern_matrix(wuT, sc_u, thru, aucol, store_gu):
            yield
        for a in tern_matrix(wdT, sc_d, thrd, adcol, store_d):
            yield
    tern_ve = ve.n

    v_op(lambda v: v.memset(S32[:], 0.0))
    init_ve = v_op(lambda v: v.memset(Sb[:], 0.0))

    dwWW = g_dma(wsh[:], wwT[:], par=0)

    wbuf_ms = [0, 0]

    ffn_ms = {}

    def tile_gen(tile):
        xtile = xt[tile % 2]
        t0 = tile * TILE
        P.wait("vector", dxt, 16 * (tile + 1))
        P.wait("scalar", dxt, 16 * (tile + 1))
        P.wait("tensor", dxt, 16 * (tile + 1))
        if tile == 0:
            P.wait("vector", *dma0)
            P.wait("scalar", *dma0)
            P.wait("gpsimd", ve, init_ve)

        def ln_pass(capc, shoff, vpre, apre, xtile=None, exw=(), xdone=None, fold=None):
            sqdone = []
            for c in range(8):
                sl = slice(c * TILE, (c + 1) * TILE)
                if c % 2 == 0:
                    a_op(lambda s, sl=sl, xtile=xtile: s.activation(kpair[:, sl], xtile[:, sl],
                                                                    AF.Square),
                         waits=(([(ac, apre)] + list(exw)) if c == 0 else [])
                         + ([xdone[c]] if xdone else []))
                    sqdone.append((ac, ac.n))
                else:
                    v_op(lambda v, sl=sl, xtile=xtile: v.tensor_mul(kpair[:, sl], xtile[:, sl],
                                                                    xtile[:, sl]),
                         waits=([(ve, vpre)] + list(exw)) if c == 1 else ())
                    sqdone.append((ve, ve.n))
            for c in range(8):
                mm(pE[0:1, 0:512], onef[:], xtile[:, c * TILE:(c + 1) * TILE],
                   start=(c == 0), stop=(c == 7),
                   waits=(bank_wait(pE) if c == 0 else [])
                   + ([xdone[c]] if xdone else []))
            for c in range(8):
                mm(pF[0:1, 0:512], ocol[:], kpair[:, c * TILE:(c + 1) * TILE],
                   start=(c == 0), stop=(c == 7),
                   waits=([sqdone[c]] + (bank_wait(pF) if c == 0 else [])))
            p1 = penop()
            v_op(lambda v: v.tensor_scalar_mul(stat[0:1, 0:512], pE[0:1, 0:512], 1.0 / D),
                 waits=[(pe, p1)])
            v_op(lambda v: v.tensor_scalar_mul(stat[0:1, 512:1024], pF[0:1, 0:512], 1.0 / D))
            v_op(lambda v: v.tensor_mul(stat[0:1, 1024:1536], stat[0:1, 0:512], stat[0:1, 0:512]))
            vv = v_op(lambda v: v.tensor_sub(stat[0:1, 512:1024], stat[0:1, 512:1024],
                                             stat[0:1, 1024:1536]))
            aa = a_op(lambda s: s.activation(stat[0:1, 1024:1536], stat[0:1, 512:1024],
                                             AF.Sqrt, bias=epsc[0:1, 0:1]), waits=[(ve, vv)])
            v_op(lambda v: v.reciprocal(stat[0:1, 512:1024], stat[0:1, 1024:1536]),
                 waits=[(ac, aa)])
            vv = v_op(lambda v: v.tensor_mul(stat[0:1, 1024:1536], stat[0:1, 0:512],
                                             stat[0:1, 512:1024]))
            mm(pE[:, 0:512], orow_f[:], stat[0:1, 512:1024], waits=[(ve, vv)])
            mm(pF[:, 0:512], orow_f[:], stat[0:1, 1024:1536])
            p1 = penop()
            alast = 0
            aprev = 0
            rbc = pE[:, 0:512].unsqueeze(1).broadcast_to([128, 2, 512])
            cbc = pF[:, 0:512].unsqueeze(1).broadcast_to([128, 2, 512])
            for cp in range(4):
                sl = slice(cp * 1024, (cp + 1) * 1024)
                aw = ([(pe, p1)] if cp == 0 else []) + ([(ac, aprev)] if aprev else [])
                v_op(lambda v, sl=sl, xtile=xtile: v.tensor_mul(
                    kp32[:, 0:1024].rearrange("p (a q) -> p a q", a=2),
                    xtile[:, sl].rearrange("p (a q) -> p a q", a=2), rbc), waits=aw)
                vap = v_op(lambda v: v.tensor_sub(
                    kp32[:, 0:1024].rearrange("p (a q) -> p a q", a=2),
                    kp32[:, 0:1024].rearrange("p (a q) -> p a q", a=2), cbc))
                for c in (2 * cp, 2 * cp + 1):
                    sl1 = slice(c * TILE, (c + 1) * TILE)
                    ssl = slice((c % 2) * TILE, (c % 2) * TILE + TILE)
                    alast = a_op(lambda s, sl1=sl1, ssl=ssl, c=c: s.activation(
                        h_fm[:, sl1], kp32[:, ssl], AF.Identity,
                        bias=modcols[:, shoff + c:shoff + c + 1],
                        scale=capc[:, c:c + 1]), waits=[(ve, vap)])
                aprev = alast
                if fold is not None:
                    fold(cp, alast)
            bank_free[bank_key(pE)] = (ve, ve.n)
            bank_free[bank_key(pF)] = (ve, ve.n)
            P.wait("tensor", ve, ve.n)
            P.wait("tensor", ac, alast)
            return alast

        yield
        if tile == 0:
            P.wait("vector", ve, mod_ready)
            h_ready = ln_pass(capcol, 0, tern_ve, ac.n, xtile=xtile)
        else:
            h_ready = ln_pass(capcol, 0, ve.n, ac.n, xtile=xtile,
                              exw=[(pe, pe.n), (po, po.n)])
        yield

        for tc in range(4):
            for k in range(8):
                mm(pE[:, tc * 8:(tc + 1) * 8],
                   h_fm[:, k * TILE + tc * 128: k * TILE + tc * 128 + 128],
                   wbT_sb[:, k * 8:(k + 1) * 8], start=(k == 0), stop=(k == 7),
                   waits=([(ac, h_ready), dwA] + bank_wait(pE))
                   if (tc == 0 and k == 0) else ())
        p1 = penop()
        ab = a_op(lambda s: s.activation(betacol[:], pE[:, 0:32], AF.Sigmoid),
                  waits=[(pe, p1)])
        bank_free[bank_key(pE)] = (ac, ab)

        hsqdone = []
        for c in range(8):
            sl = slice(c * TILE, (c + 1) * TILE)
            if c % 2 == 0:
                a_op(lambda s, sl=sl: s.activation(kpair[:, sl], h_fm[:, sl], AF.Square))
                hsqdone.append((ac, ac.n))
            else:
                v_op(lambda v, sl=sl: v.tensor_mul(kpair[:, sl], h_fm[:, sl], h_fm[:, sl]),
                     waits=[(ac, h_ready)] if c == 1 else [])
                hsqdone.append((ve, ve.n))
        for h in range(H):
            for tc in range(4):
                mm(pF[:, tc * 8 + h: tc * 8 + h + 1],
                   kpair[:, h * TILE + tc * 128: h * TILE + tc * 128 + 128], ocol[:],
                   waits=([hsqdone[h]] + (bank_wait(pF) if h == 0 else [])) if tc == 0 else ())
        p1 = penop()
        aa = a_op(lambda s: s.activation(f1col[:], pF[:, 0:32], AF.Sqrt),
                  waits=[(pe, p1)])
        v_op(lambda v: v.tensor_scalar_max(f1col[:], f1col[:], 1e-12),
             waits=[(ac, aa)])
        f1v = fac2[:, 0:64].rearrange("p (t f h) -> p t f h", t=4, f=2)[:, :, 0, :]
        f2v = fac2[:, 0:64].rearrange("p (t f h) -> p t f h", t=4, f=2)[:, :, 1, :]
        v_op(lambda v: v.reciprocal(f1v, f1col[:].rearrange("p (t h) -> p t h", t=4)))
        v_op(lambda v: v.scalar_tensor_tensor(f2v, betacol[:].rearrange("p (t h) -> p t h", t=4),
                                              -1.0, f1v, ALU.mult, ALU.mult))
        vfac = v_op(lambda v: v.tensor_mul(
            kdwfcol[:].rearrange("p (t h) -> p t h", t=4), f1v,
            dwcol[:].unsqueeze(1).broadcast_to([128, 4, 8])))
        bank_free[bank_key(pF)] = (ve, vfac)
        yield

        for tc in range(4):
            for k in range(8):
                lh = h_fm[:, k * TILE + tc * 128: k * TILE + tc * 128 + 128]
                for half in range(2):
                    mm(pB[:, half * 512:(half + 1) * 512], lh,
                       wsh[:, k * D + half * 512: k * D + (half + 1) * 512],
                       start=(k == 0), stop=(k == 7),
                       waits=([dwWW] + bank_wait(pB))
                       if (tc == 0 and k == 0 and half == 0) else ())
            p1 = penop()
            for h in range(H):
                v_op(lambda v, tc=tc, h=h: v.tensor_single_scalar(
                    vtm[:, tc * D + h * 128: tc * D + (h + 1) * 128],
                    pB[:, h * 128:(h + 1) * 128],
                    betacol[:, tc * 8 + h: tc * 8 + h + 1], ALU.mult),
                    waits=[(ac, ab), (pe, p1)] if h == 0 else ())
            bank_free[bank_key(pB)] = (ve, ve.n)
            P.wait("tensor", ve, ve.n)
            if tc == 1:
                for fc in range(4):
                    inp = fac2[:, fc * 16:(fc + 1) * 16]
                    P.emit("tensor", lambda t, inp=inp, fc=fc: t.transpose(
                        pD[0:16, fc * 128:(fc + 1) * 128], inp, idf32[:]),
                        waits=([(ve, vfac)] + bank_wait(pD)) if fc == 0 else (), inc=pe)
                p1t = pe.n
                vM = v_op(lambda v: v.tensor_copy(Mfac[:], pD[0:16, 0:512]),
                          waits=[(pe, p1t)])
                bank_free[bank_key(pD)] = (ve, vM)
            yield
        vvtm = ve.n
        P.wait("tensor", ve, vM)
        yield
        pBa = pB[:, 0:512]
        pBb = pB[:, 512:1024]
        _bkey[id(pBa)] = bank_key(pB)
        _bkey[id(pBb)] = bank_key(pB)
        _bkey[id(psT32)] = bank_key(psT)
        kbanks = [(pE, pF), (pBa, pBb)]
        for h in range(H):
            bE, bF = kbanks[h % 2]
            mm(bE[:, 0:512], e16[:, h:h + 1].broadcast_to([16, 128]), Mfac[:],
               waits=bank_wait(bE))
            mm(bF[:, 0:512], e16[:, 8 + h:9 + h].broadcast_to([16, 128]), Mfac[:],
               waits=bank_wait(bF))
            p1 = penop()
            hsl = h_fm[:, h * TILE:(h + 1) * TILE].rearrange("p (c q) -> p c q", c=8)
            kcell = kpair[:, h * 8 * 128:(h + 1) * 8 * 128].rearrange("p (c q) -> p c q", c=8)
            v_op(lambda v, hsl=hsl, kcell=kcell, bE=bE: v.tensor_mul(
                kcell[:, :, 0:64], hsl, bE[:, 0:512].rearrange("p (c q) -> p c q", c=8)),
                waits=[(pe, p1)])
            vkb = v_op(lambda v, hsl=hsl, kcell=kcell, bF=bF: v.tensor_mul(
                kcell[:, :, 64:128], hsl, bF[:, 0:512].rearrange("p (c q) -> p c q", c=8)))
            bank_free[bank_key(bE)] = (ve, vkb)
            bank_free[bank_key(bF)] = (ve, vkb)
            if h % 2 == 1:
                yield
        P.wait("tensor", ve, vkb)
        vkp = ve.n

        for tc in range(4):
            for h in range(H):
                P.emit("tensor", lambda t, tc=tc, h=h: t.transpose(
                    psT[:, h * 128:(h + 1) * 128],
                    h_fm[:, h * TILE + tc * 128: h * TILE + tc * 128 + 128], id128[:]),
                    waits=([(ac, h_ready)] + bank_wait(psT)) if (tc == 0 and h == 0) else (),
                    inc=pe)
            p1 = pe.n
            for h in range(H):
                v_op(lambda v, tc=tc, h=h: v.tensor_single_scalar(
                    kdw_tm[:, tc * D + h * 128: tc * D + (h + 1) * 128],
                    psT[:, h * 128:(h + 1) * 128],
                    kdwfcol[:, tc * 8 + h: tc * 8 + h + 1], ALU.mult),
                    waits=[(pe, p1)] if h == 0 else ())
            bank_free[bank_key(psT)] = (ve, ve.n)
            P.wait("tensor", ve, ve.n)
            yield

        pBa = pB[:, 0:512]
        pBb = pB[:, 512:1024]
        _bkey[id(pBa)] = bank_key(pB)
        _bkey[id(pBb)] = bank_key(pB)
        _bkey[id(psT32)] = bank_key(psT)

        def a_raws(n, b0, b1, b2):
            par = n % 2
            phs = slice(par * 64, par * 64 + 64)
            for h in range(H):
                cell = kpair[:, (h * 8 + n) * 128:(h * 8 + n + 1) * 128]
                mm(b0[phs, h * 64:(h + 1) * 64], cell[:, 64:128], cell[:, 0:64],
                   tp=(0, par * 64),
                   waits=(([(ve, vkp)] if n == 0 else []) + bank_wait(b0)
                          + bank_wait(b1) + bank_wait(b2)) if h == 0 else ())
            for h in range(H):
                cell = kpair[:, (h * 8 + n) * 128:(h * 8 + n + 1) * 128]
                mm(b1[phs, h * 64:(h + 1) * 64], cell[:, 0:64], cell[:, 0:64],
                   tp=(0, par * 64))
            for h in range(H):
                cell = kpair[:, (h * 8 + n) * 128:(h * 8 + n + 1) * 128]
                mm(b2[phs, h * 64:(h + 1) * 64], cell[:, 0:64], cell[:, 64:128],
                   tp=(0, par * 64))
            return penop()

        def a_inits(n, b0, b1, b2, praw):
            par = n % 2
            phs = slice(par * 64, par * 64 + 64)
            v_op(lambda v, phs=phs, b0=b0: v.tensor_mul(Xs[phs, :], b0[phs, 0:512],
                                                        maskA[phs, :]), waits=[(pe, praw)])
            v_op(lambda v, phs=phs, n=n, b1=b1: v.tensor_mul(
                att_all[phs, (n // 2) * 512:(n // 2 + 1) * 512], b1[phs, 0:512],
                maskLTI[phs, :]))
            v_op(lambda v, phs=phs, b2=b2: v.tensor_mul(Ys[phs, :], b2[phs, 0:512],
                                                        maskAT[phs, :]))
            return v_op(lambda v, phs=phs: v.tensor_add(Ps[phs, :], Ys[phs, :], ipk[phs, :]))

        def a_xy(n, j, b0, b2, vrdy):
            par = n % 2
            phs = slice(par * 64, par * 64 + 64)
            for h in range(H):
                mm(b0[phs, h * 64:(h + 1) * 64], Ys[phs, h * 64:(h + 1) * 64],
                   Xs[phs, h * 64:(h + 1) * 64], tp=(par * 64, par * 64),
                   waits=[(ve, vrdy), (ac, ac.n)] if h == 0 else ())
            if j < 5:
                for h in range(H):
                    mm(b2[phs, h * 64:(h + 1) * 64], Xs[phs, h * 64:(h + 1) * 64],
                       Ys[phs, h * 64:(h + 1) * 64], tp=(par * 64, par * 64))
            return penop()

        def a_xyev(n, j, b0, b2, pxy):
            par = n % 2
            phs = slice(par * 64, par * 64 + 64)
            aX = a_op(lambda s, phs=phs, b0=b0: s.activation(Xs[phs, :], b0[phs, 0:512],
                                                             AF.Identity), waits=[(pe, pxy)])
            if j < 5:
                aX = a_op(lambda s, phs=phs, b2=b2: s.activation(Ys[phs, :], b2[phs, 0:512],
                                                                 AF.Identity))
            return aX

        def a_pmm(n, b1, aX):
            par = n % 2
            phs = slice(par * 64, par * 64 + 64)
            for h in range(H):
                mm(b1[phs, h * 64:(h + 1) * 64], Xs[phs, h * 64:(h + 1) * 64],
                   Ps[phs, h * 64:(h + 1) * 64], tp=(par * 64, par * 64),
                   waits=[(ac, aX)] if h == 0 else ())
            return penop()

        def a_pev(n, j, b1, ppm):
            par = n % 2
            phs = slice(par * 64, par * 64 + 64)
            if j < 5:
                return v_op(lambda v, phs=phs, b1=b1: v.tensor_add(Ps[phs, :], Ps[phs, :],
                                                                   b1[phs, 0:512]),
                            waits=[(pe, ppm)])
            return v_op(lambda v, phs=phs, n=n, b1=b1: v.tensor_add(
                at_all[phs, (n // 2) * 512:(n // 2 + 1) * 512], Ps[phs, :],
                b1[phs, 0:512]), waits=[(pe, ppm)])

        bt_even = (pD, pE, pF)
        bt_odd = (pBa, psT32, pBb)
        for pair in range(4):
            c0, c1 = 2 * pair, 2 * pair + 1
            praw0 = a_raws(c0, *bt_even)
            praw1 = a_raws(c1, *bt_odd)
            v0 = a_inits(c0, *bt_even, praw0)
            v1 = a_inits(c1, *bt_odd, praw1)
            for j in range(1, 6):
                pxy0 = a_xy(c0, j, bt_even[0], bt_even[2], v0)
                pxy1 = a_xy(c1, j, bt_odd[0], bt_odd[2], v1)
                aX0 = a_xyev(c0, j, bt_even[0], bt_even[2], pxy0)
                aX1 = a_xyev(c1, j, bt_odd[0], bt_odd[2], pxy1)
                ppm0 = a_pmm(c0, bt_even[1], aX0)
                ppm1 = a_pmm(c1, bt_odd[1], aX1)
                v0 = a_pev(c0, j, bt_even[1], ppm0)
                v1 = a_pev(c1, j, bt_odd[1], ppm1)
            for b in bt_even:
                bank_free[bank_key(b)] = (ve, v0)
            for b in bt_odd:
                bank_free[bank_key(b)] = (ve, v1)
            yield
        bank_free[bank_key(pB)] = (ve, v1)
        vchainA = ve.n

        dwR0 = g_dma(wbuf[0][:], wrT[:, 0:4096], waits=[(pe, wbuf_ms[0])], par=0)
        dwR1 = g_dma(wbuf[1][:], wrT[:, 4096:8192], waits=[(pe, wbuf_ms[1])], par=1)
        def mk_kde(n, wts):
            kf_c = kpair.rearrange("p (h c q) -> p h c q", h=8, c=8)[:, :, n, 0:64]
            kb_c = kpair.rearrange("p (h c q) -> p h c q", h=8, c=8)[:, :, n, 64:128]
            dep = de_pat[:].rearrange("p (h q) -> p h q", h=8)
            pb = n % 2
            g_op(lambda g, kf_c=kf_c, dep=dep, pb=pb: g.tensor_mul(
                kde_s[:, pb * 512:(pb + 1) * 512].rearrange("p (h q) -> p h q", h=8),
                kf_c, dep), waits=wts)
            return g_op(lambda g, kb_c=kb_c, dep=dep, pb=pb: g.tensor_mul(
                kbdn_s[:, pb * 512:(pb + 1) * 512].rearrange("p (h q) -> p h q", h=8),
                kb_c, dep))

        def b_o_mms(n):
            par = n % 2
            tc = n // 2
            phs = slice(par * 64, par * 64 + 64)
            sb_v = Sb[:, (par) * 1024:(par) * 1024 + 1024]
            for h in range(H):
                o1w = bank_wait(psT) if h == 0 else ()
                mm(psT32[:, h * 64:(h + 1) * 64], sb_v[:, h * 128:(h + 1) * 128],
                   kde_s[:, par * 512 + h * 64: par * 512 + (h + 1) * 64],
                   start=True, stop=False, waits=o1w)
                mm(psT32[:, h * 64:(h + 1) * 64], vnewtm[phs, h * 128:(h + 1) * 128],
                   att_all[phs, (n // 2) * 512 + h * 64:(n // 2) * 512 + (h + 1) * 64],
                   start=False, stop=True, tp=(par * 64, 0))
            return penop()

        def b_o_evac(n, po1):
            v_op(lambda v, n=n: v.tensor_copy(
                o_fm[:].rearrange("p (h t) -> p h t", h=8)[:, :, n * 64:(n + 1) * 64],
                psT32[:, 0:512].rearrange("p (h q) -> p h q", h=8)), waits=[(pe, po1)])
            bank_free[bank_key(psT)] = (ve, ve.n)

        pkde = mk_kde(0, [(ve, vchainA)])
        prev_o = None
        for n in range(CHT):
            par = n % 2
            tc = n // 2
            phs = slice(par * 64, par * 64 + 64)
            sb_v = Sb[:, par * 1024:par * 1024 + 1024]
            for h in range(H):
                w0w = ([(po, pkde)] + bank_wait(pB)) if h == 0 else []
                if n == 0 and h == 0:
                    w0w = w0w + [(ve, vvtm)]
                mm(pB[phs, h * 128:(h + 1) * 128],
                   kbdn_s[:, par * 512 + h * 64: par * 512 + (h + 1) * 64],
                   sb_v[:, h * 128:(h + 1) * 128], start=True, stop=False, tp=(0, par * 64),
                   waits=w0w)
                mm(pB[phs, h * 128:(h + 1) * 128], idc[phs, :],
                   vtm[phs, tc * D + h * 128: tc * D + (h + 1) * 128],
                   start=False, stop=True, tp=(par * 64, par * 64))
            p1 = penop()
            aw0 = a_op(lambda s, phs=phs: s.activation(w0tm[phs, 0:512], pB[phs, 0:512],
                                                       AF.Identity), waits=[(pe, p1)])
            vw0 = v_op(lambda v, phs=phs: v.tensor_copy(w0tm[phs, 512:1024], pB[phs, 512:1024]),
                       waits=[(pe, p1)])
            for h in range(H):
                mm(pB[phs, h * 128:(h + 1) * 128],
                   at_all[phs, (n // 2) * 512 + h * 64:(n // 2) * 512 + (h + 1) * 64],
                   w0tm[phs, h * 128:(h + 1) * 128], tp=(par * 64, par * 64),
                   waits=([(ac, aw0)] if h == 0 else [])
                   + ([(ve, vw0)] if h == 4 else []))
            p1 = penop()
            avn = a_op(lambda s, phs=phs: s.activation(vnewtm[phs, 0:512], pB[phs, 0:512],
                                                       AF.Identity), waits=[(pe, p1)])
            vvn = v_op(lambda v, phs=phs: v.tensor_copy(vnewtm[phs, 512:1024], pB[phs, 512:1024]),
                       waits=[(pe, p1)])
            if prev_o is not None:
                po1 = b_o_mms(n - 1)
            if n + 1 < CHT:
                nkde = mk_kde(n + 1, [(pe, pe.n)])
            for h in range(H):
                mm(pB[:, h * 128:(h + 1) * 128],
                   kdw_tm[phs, tc * D + h * 128: tc * D + (h + 1) * 128],
                   vnewtm[phs, h * 128:(h + 1) * 128], tp=(par * 64, 0),
                   waits=([(ac, avn)] if h == 0 else [])
                   + ([(ve, vvn)] if h == 4 else []))
            pds = penop()
            if prev_o is not None:
                b_o_evac(n - 1, po1)
            P.wait("vector", pe, pds)
            for h in range(H):
                v_op(lambda v, h=h: v.scalar_tensor_tensor(
                    S32[:, h * 128:(h + 1) * 128], S32[:, h * 128:(h + 1) * 128],
                    ctdcol[:, h:h + 1], pB[:, h * 128:(h + 1) * 128],
                    ALU.mult, ALU.add))
            vS = ve.n
            bank_free[bank_key(pB)] = (ve, vS)
            v_op(lambda v, n=n: v.tensor_copy(Sb[:, ((n + 1) % 2) * 1024:((n + 1) % 2) * 1024 + 1024],
                                              S32[:]))
            P.wait("tensor", ve, ve.n)
            prev_o = n
            pkde = nkde if n + 1 < CHT else pkde
            yield
        po1 = b_o_mms(CHT - 1)
        b_o_evac(CHT - 1, po1)
        vchunks = ve.n
        achunks = ac.n

        res1done = []
        for f in range(8):
            bank = pE if f % 2 == 0 else pF
            for k in range(8):
                mm(bank[:, 0:512],
                   wbuf[k // 4][:, (k % 4) * D + f * 128: (k % 4) * D + f * 128 + 128],
                   o_fm[:, k * TILE:(k + 1) * TILE], start=(k == 0), stop=(k == 7),
                   waits=(([(ve, vchunks), (ac, achunks), dwR0, dwR1] if f == 0 else [])
                          + bank_wait(bank)) if k == 0 else ())
            p1 = penop()
            ae1 = a_op(lambda s, bank=bank, f=f: s.activation(tq3[:, 0:512], bank[:, 0:512],
                                                              AF.Identity,
                                                              scale=gacol[:, f:f + 1]),
                       waits=[(pe, p1), (ve, ve.n)])
            bank_free[bank_key(bank)] = (ac, ae1)
            v_op(lambda v, f=f, xtile=xtile: v.tensor_add(xtile[:, f * TILE:(f + 1) * TILE],
                                             xtile[:, f * TILE:(f + 1) * TILE], tq3[:, 0:512]),
                 waits=[(ac, ae1)])
            res1done.append((ve, v_op(lambda v, f=f, xtile=xtile: v.scalar_tensor_tensor(
                xtile[:, f * TILE:(f + 1) * TILE], h_fm[:, f * TILE:(f + 1) * TILE],
                gacol[:, f:f + 1], xtile[:, f * TILE:(f + 1) * TILE], ALU.mult, ALU.add))))
            if f % 2 == 1:
                yield
        vres1 = ve.n
        wbuf_ms[0] = wbuf_ms[1] = pe.n
        P.wait("scalar", po, po.n)
        P.wait("vector", po, po.n)

        h2_ready = ln_pass(cbpcol, 24, vres1, ac.n, xtile=xtile, xdone=res1done)

        yield "q1barrier"
        if tile + 1 < NT:
            s_dma(xt[(tile + 1) % 2][:], xT[:, (tile + 1) * 4096:(tile + 2) * 4096],
                  waits=[(so, so.n)], inc=dxt, selfwait=True)
        v_op(lambda v: v.tensor_tensor(tq[:], h_fm[:, 0:512], h_fm[:, 512:1024], ALU.max),
             waits=[(ac, h2_ready), ffn_ms.get("dancepe", (pe, 0))])
        v_op(lambda v: v.tensor_tensor(tq[:], tq[:], h_fm[:, 1024:1536], ALU.max))
        v_op(lambda v: v.tensor_tensor(tq[:], tq[:], h_fm[:, 1536:2048], ALU.max))
        v_op(lambda v: v.tensor_tensor(sbc[:], h_fm[:, 0:512], h_fm[:, 512:1024], ALU.min))
        v_op(lambda v: v.tensor_tensor(sbc[:], sbc[:], h_fm[:, 1024:1536], ALU.min))
        v_op(lambda v: v.tensor_tensor(sbc[:], sbc[:], h_fm[:, 1536:2048], ALU.min))
        v_op(lambda v: v.tensor_tensor(tq2[:], h_fm[:, 2048:2560], h_fm[:, 2560:3072], ALU.max))
        v_op(lambda v: v.tensor_tensor(tq2[:], tq2[:], h_fm[:, 3072:3584], ALU.max))
        v_op(lambda v: v.tensor_tensor(tq2[:], tq2[:], h_fm[:, 3584:4096], ALU.max))
        v_op(lambda v: v.tensor_tensor(sibc[:], h_fm[:, 2048:2560], h_fm[:, 2560:3072], ALU.min))
        v_op(lambda v: v.tensor_tensor(sibc[:], sibc[:], h_fm[:, 3072:3584], ALU.min))
        v_op(lambda v: v.tensor_tensor(sibc[:], sibc[:], h_fm[:, 3584:4096], ALU.min))
        v_op(lambda v: v.tensor_tensor(tq[:], tq[:], tq2[:], ALU.max))
        v_op(lambda v: v.tensor_tensor(sbc[:], sbc[:], sibc[:], ALU.min))
        vq = v_op(lambda v: v.scalar_tensor_tensor(tq[:], sbc[:], -1.0,
                                                   tq[:], ALU.mult, ALU.max))

        def quant_dance(vq, col_xform):
            for cell in range(4):
                P.emit("tensor", lambda t, cell=cell: t.transpose(
                    psT[:, cell * 128:(cell + 1) * 128], tq[:, cell * 128:(cell + 1) * 128],
                    id128[:]),
                    waits=([(ve, vq)] + bank_wait(psT)) if cell == 0 else (), inc=pe)
            p1 = pe.n
            for cell in range(4):
                v_op(lambda v, cell=cell: v.tensor_reduce(
                    amaxc[:, cell:cell + 1], psT[:, cell * 128:(cell + 1) * 128], AX.X, ALU.max),
                    waits=[(pe, p1)] if cell == 0 else ())
            bank_free[bank_key(psT)] = (ve, ve.n)
            vq2 = col_xform()
            P.emit("tensor", lambda t: t.transpose(pD[0:8, 0:128], amaxc[:, 0:8], idf32[:]),
                   waits=[(ve, vq2)] + bank_wait(pD), inc=pe)
            p1 = pe.n
            vq2 = v_op(lambda v: v.tensor_copy(Mq[:], pD[0:8, 0:128]), waits=[(pe, p1)])
            bank_free[bank_key(pD)] = (ve, vq2)
            P.wait("tensor", ve, vq2)
            for cell in range(4):
                mm(pE[:, cell * 128:(cell + 1) * 128], e8[:, 4 + cell:5 + cell].broadcast_to([8, 128]),
                   Mq[:], waits=bank_wait(pE) if cell == 0 else ())
                mm(pF[:, cell * 128:(cell + 1) * 128], e8[:, cell:cell + 1].broadcast_to([8, 128]),
                   Mq[:], waits=bank_wait(pF) if cell == 0 else ())
            p1 = penop()
            v_op(lambda v: v.tensor_copy(sbc[:], pE[:, 0:512]), waits=[(pe, p1)])
            vbc = v_op(lambda v: v.tensor_copy(sibc[:], pF[:, 0:512]))
            bank_free[bank_key(pE)] = (ve, vbc)
            bank_free[bank_key(pF)] = (ve, vbc)
            P.wait("tensor", ve, vbc)
            return vbc

        def q1cols():
            v_op(lambda v: v.tensor_scalar_max(amaxc[:, 0:4], amaxc[:, 0:4], 1e-5))
            v_op(lambda v: v.reciprocal(amaxc[:, 4:8], amaxc[:, 0:4]))
            v_op(lambda v: v.tensor_scalar_mul(amaxc[:, 4:8], amaxc[:, 4:8], 127.0))
            return v_op(lambda v: v.tensor_scalar_mul(amaxc[:, 0:4], amaxc[:, 0:4], 1.0 / 127.0))

        quant_dance(vq, q1cols)
        for cp in range(4):
            sl = slice(cp * 1024, (cp + 1) * 1024)
            bc2 = sbc[:].unsqueeze(1).broadcast_to([128, 2, 512])
            bc2i = sibc[:].unsqueeze(1).broadcast_to([128, 2, 512])
            v_op(lambda v, sl=sl, bc2=bc2: v.tensor_mul(
                kpair[:, 0:1024].rearrange("p (a q) -> p a q", a=2),
                h_fm[:, sl].rearrange("p (a q) -> p a q", a=2), bc2))
            v_op(lambda v: v.tensor_scalar(kpair[:, 0:1024], kpair[:, 0:1024],
                                           MAGIC, -MAGIC, ALU.add, ALU.add))
            v_op(lambda v, sl=sl, bc2i=bc2i: v.tensor_mul(
                xq[:, sl].rearrange("p (a q) -> p a q", a=2),
                kpair[:, 0:1024].rearrange("p (a q) -> p a q", a=2), bc2i),
                waits=[ffn_ms["qb"]] if (cp == 0 and "qb" in ffn_ms) else [])
        vxq = ve.n
        yield "ffnstart"

        if tile == 0:
            P.wait("gpsimd", dstP[0], dstP[0].n)
            P.wait("gpsimd", dstP[1], dstP[1].n)
        q1pe = pe.n
        rot = [pA, pC]
        ri = 0
        qm = ffn_ms.get("qmul", (ve, 0))
        mml = {0: qm, 1: qm, 2: qm, 3: qm}
        for og in range(8):
            for which, scr in ((0, sc_g), (1, sc_u)):
                buf = wbuf[(og * 2 + which) % 2]
                dk = g_dma(buf[:], scr[og * 128:(og + 1) * 128, :],
                           waits=[(pe, wbuf_ms[(og * 2 + which) % 2])],
                           par=(og * 2 + which) % 2)
                for osub in range(4):
                    bank = rot[ri % 2]
                    ri += 1
                    for k in range(8):
                        mm(bank[:, 0:512], buf[:, k * 512 + osub * 128: k * 512 + osub * 128 + 128],
                           xq[:, k * TILE:(k + 1) * TILE], start=(k == 0), stop=(k == 7),
                           waits=([dk, (ve, vxq)] + bank_wait(bank)) if k == 0 else ())
                    p1 = penop()
                    if osub == 3:
                        wbuf_ms[(og * 2 + which) % 2] = p1
                    if which == 0:
                        aev = a_op(lambda s, osub=osub, bank=bank: s.activation(
                            gu[:, osub * 512:(osub + 1) * 512], bank[:, 0:512], AF.Silu,
                            scale=agcol[:, 0:1]), waits=[(pe, p1), mml[osub]])
                    else:
                        aev = a_op(lambda s, osub=osub, bank=bank: s.activation(
                            gu[:, 2048 + osub * 512:2048 + (osub + 1) * 512], bank[:, 0:512],
                            AF.Identity), waits=[(pe, p1)])
                        oc = og * 4 + osub
                        mml[osub] = (ve, v_op(lambda v, osub=osub, oc=oc: v.tensor_mul(
                            m_fm[:, oc * 512:(oc + 1) * 512], gu[:, osub * 512:(osub + 1) * 512],
                            gu[:, 2048 + osub * 512:2048 + (osub + 1) * 512]),
                            waits=[(ac, aev)]))
                        if oc == 1:
                            v_op(lambda v: v.tensor_tensor(
                                tq[:], m_fm[:, 0:512], m_fm[:, 512:1024], ALU.max),
                                waits=[(pe, q1pe)])
                            v_op(lambda v: v.tensor_tensor(
                                sbc[:], m_fm[:, 0:512], m_fm[:, 512:1024], ALU.min))
                        elif oc >= 2:
                            v_op(lambda v, oc=oc: v.tensor_tensor(
                                tq[:], tq[:], m_fm[:, oc * 512:(oc + 1) * 512], ALU.max))
                            v_op(lambda v, oc=oc: v.tensor_tensor(
                                sbc[:], sbc[:], m_fm[:, oc * 512:(oc + 1) * 512], ALU.min))
                    bank_free[bank_key(bank)] = (ac, aev)
            yield
        pm = ve.n

        vq = v_op(lambda v: v.scalar_tensor_tensor(tq[:], sbc[:], -1.0,
                                                   tq[:], ALU.mult, ALU.max),
                  waits=[(ve, pm)])

        def q2cols():
            v_op(lambda v: v.tensor_single_scalar(amaxc[:, 0:4], amaxc[:, 0:4],
                                                  aucol[:, 0:1], ALU.mult))
            v_op(lambda v: v.tensor_scalar_max(amaxc[:, 0:4], amaxc[:, 0:4], 1e-5))
            v_op(lambda v: v.reciprocal(amaxc[:, 4:8], amaxc[:, 0:4]))
            v_op(lambda v: v.tensor_scalar_mul(amaxc[:, 4:8], amaxc[:, 4:8], 127.0))
            v_op(lambda v: v.tensor_single_scalar(amaxc[:, 4:8], amaxc[:, 4:8],
                                                  aucol[:, 0:1], ALU.mult))
            v_op(lambda v: v.tensor_single_scalar(amaxc[:, 0:4], amaxc[:, 0:4],
                                                  adcol[:, 0:1], ALU.mult))
            return v_op(lambda v: v.tensor_scalar_mul(amaxc[:, 0:4], amaxc[:, 0:4], 1.0 / 127.0))

        quant_dance(vq, q2cols)
        ffn_ms["dancepe"] = (pe, pe.n)
        rms = []
        amag = [ac.n, ac.n]
        pegu = pe.n
        for ip in range(16):
            sl = slice(ip * 1024, (ip + 1) * 1024)
            qa = gu[:, (ip % 2) * 2048:(ip % 2) * 2048 + 1024]
            qb = xq32[:, (ip % 2) * 1024:(ip % 2) * 1024 + 1024]
            bc2 = sbc[:].unsqueeze(1).broadcast_to([128, 2, 512])
            v_op(lambda v, sl=sl, bc2=bc2, qa=qa: v.tensor_mul(
                qa.rearrange("p (a q) -> p a q", a=2),
                m_fm[:, sl].rearrange("p (a q) -> p a q", a=2), bc2),
                waits=[(ac, amag[ip % 2])])
            a_op(lambda s, qa=qa, qb=qb: s.activation(qb, qa, AF.Identity, bias=magc[:]),
                 waits=[(ve, ve.n)] + ([(pe, pegu)] if ip < 2 else []))
            amag[ip % 2] = a_op(lambda s, sl=sl, qb=qb: s.activation(
                m_fm[:, sl], qb, AF.Identity, bias=nmagc[:]))
            if ip % 4 == 3:
                rms.append((ac, amag[ip % 2]))
        ffn_ms["qb"] = (ac, ac.n)
        ffn_ms["qmul"] = (ve, ve.n)
        vq2v = ve.n
        yield

        for fg in range(2):
            banks = [pA, pC, pD, pE]
            for kg in range(4):
                buf = wbuf[kg % 2]
                dk = g_dma(buf[:], sc_d[(kg * 2 + fg) * 128:(kg * 2 + fg + 1) * 128, :],
                           waits=[(pe, wbuf_ms[kg % 2])], par=kg % 2)
                for fsub in range(4):
                    bw = (bank_wait(banks[0]) + bank_wait(banks[1]) + bank_wait(banks[2])
                          + bank_wait(banks[3])) if (kg == 0 and fsub == 0) else []
                    for k in range(8):
                        ic = kg * 8 + k
                        mm(banks[fsub][:, 0:512],
                           buf[:, k * 512 + fsub * 128: k * 512 + fsub * 128 + 128],
                           m_fm[:, ic * 512:(ic + 1) * 512],
                           start=(kg == 0 and k == 0), stop=(kg == 3 and k == 7),
                           waits=([dk, rms[kg]] + bw)
                           if (fsub == 0 and k == 0) else
                           (bw if k == 0 else ()))
                wbuf_ms[kg % 2] = P.inc_last("tensor", pe)
            p1 = pe.n
            for fsub in range(4):
                f = fg * 4 + fsub
                v_op(lambda v, fsub=fsub, banks=banks: v.tensor_mul(
                    tq[:], banks[fsub][:, 0:512], sibc[:]),
                    waits=[(pe, p1)] if fsub == 0 else [])
                vr = v_op(lambda v, f=f, xtile=xtile: v.scalar_tensor_tensor(
                    xtile[:, f * TILE:(f + 1) * TILE], tq[:], gbcol[:, f:f + 1],
                    xtile[:, f * TILE:(f + 1) * TILE], ALU.mult, ALU.add))
                bank_free[bank_key(banks[fsub])] = (ve, vr)
            P.wait("tensor", ve, ve.n)
        vfin = ve.n
        s_dma(outT[:, tile * 4096:(tile + 1) * 4096], xtile[:],
              waits=[(ve, vfin)], inc=so, selfwait=True)

    pend, rate = tern_gen(), 2
    for tile in range(NT):
        g = tile_gen(tile)
        while True:
            try:
                mark = next(g)
            except StopIteration:
                break
            if mark == "q1barrier":
                if pend is not None:
                    for _ in pend:
                        pass
                    pend = None
                continue
            if mark == "ffnstart":
                break
            if pend is not None:
                for _ in range(rate):
                    if next(pend, "END") == "END":
                        pend = None
                        break
        pend, rate = g, 1
    if pend is not None:
        for _ in pend:
            pass

    P.wait("sync", so, so.n)

    with nc.Block() as block:
        @block.sync
        def _(e):
            P.replay("sync", e)

        @block.tensor
        def _(e):
            P.replay("tensor", e)

        @block.vector
        def _(e):
            P.replay("vector", e)

        @block.scalar
        def _(e):
            P.replay("scalar", e)

        @block.gpsimd
        def _(e):
            P.replay("gpsimd", e)

    stack.close()
    return nc


def host_consts():
    bf = ml_dtypes.bfloat16
    iota = np.tile(np.arange(1, C + 1, dtype=np.float32), H)[None, :]
    i64 = np.eye(64, dtype=np.float32)
    ipacked = np.zeros((128, 512), np.float32)
    trilS = np.zeros((128, 512), np.float32)
    triuS = np.zeros((128, 512), np.float32)
    triuI = np.zeros((128, 512), np.float32)
    tl = np.tril(np.ones((64, 64), np.float32), -1)
    ti = np.tril(np.ones((64, 64), np.float32))
    for half in range(2):
        for h in range(H):
            slr = slice(half * 64, half * 64 + 64)
            slc = slice(h * 64, (h + 1) * 64)
            ipacked[slr, slc] = i64
            trilS[slr, slc] = tl
            triuS[slr, slc] = tl.T
            triuI[slr, slc] = ti.T
    return {
        "iota_rep": iota,
        "ones_row_f": np.ones((1, 128), np.float32),
        "ones_col_b": np.ones((128, 1), bf),
        "ident128": np.eye(128, dtype=bf),
        "identcol": np.vstack([np.eye(64), np.eye(64)]).astype(bf),
        "ipacked": ipacked.astype(bf),
        "trilS": trilS.astype(bf),
        "triuS": triuS.astype(bf),
        "triuI": triuI.astype(bf),
        "e16rep": np.eye(16, dtype=bf),
        "e8rep": np.eye(8, dtype=bf),
    }


def pack_w(w):
    return np.ascontiguousarray(w.reshape(8, 128, D).transpose(1, 0, 2).reshape(128, 8 * D))


def pack_x(xt):
    return np.ascontiguousarray(
        xt.reshape(8, 128, 8, 512).transpose(1, 2, 0, 3).reshape(128, 8 * 4096))


def unpack_out(o):
    return np.ascontiguousarray(
        o.reshape(128, 8, 8, 512).transpose(2, 0, 1, 3).reshape(D, T).T)


def kernel(L0_attn_out, L0_attn_W_write, L0_attn_W_read, L0_attn_W_beta, L0_attn_decay,
           L0_ffn_w_gate, L0_ffn_w_up, L0_ffn_w_down, L0_c, L0_wmod):
    shared = host_consts()
    shared.update({
        "wwT": pack_w(np.asarray(L0_attn_W_write).T),
        "wrT": pack_w(np.asarray(L0_attn_W_read).T),
        "wbT": np.ascontiguousarray(np.asarray(L0_attn_W_beta).T),
        "dec": np.ascontiguousarray(np.asarray(L0_attn_decay).reshape(1, H)),
        "wmod": np.ascontiguousarray(np.asarray(L0_wmod)),
        "wgT": np.ascontiguousarray(np.asarray(L0_ffn_w_gate).T),
        "wuT": np.ascontiguousarray(np.asarray(L0_ffn_w_up).T),
        "wdT": np.ascontiguousarray(np.asarray(L0_ffn_w_down).T),
    })
    x = np.asarray(L0_attn_out)
    cc = np.asarray(L0_c)
    in_maps = []
    for b in range(8):
        m = dict(shared)
        m["xT"] = pack_x(np.asarray(x[b]).T)
        m["c_row"] = np.ascontiguousarray(cc[b].reshape(1, D))
        in_maps.append(m)
    nc = build_nc()
    res = run_bass_kernel_spmd(nc, in_maps, core_ids=list(range(8)))
    out = np.stack([unpack_out(np.asarray(r["outT"])) for r in res.results], axis=0)
    return out.astype(np.float32)
